# version 06
# baseline (speedup 1.0000x reference)
import math
from contextlib import ExitStack

import numpy as np
import ml_dtypes

import concourse.bass as bass
import concourse.mybir as mybir
from concourse.bass import ds
from concourse.bass_utils import run_bass_kernel_spmd
from concourse.alu_op_type import AluOpType as ALU
from concourse.engine_type import EngineType

RG8 = [[0, 1, 2, 3, 4, 5, 6, 7]]

F32 = mybir.dt.float32
BF16 = mybir.dt.bfloat16
I32 = mybir.dt.int32
AF = mybir.ActivationFunctionType
BF = ml_dtypes.bfloat16

B, S, D = 2, 8192, 1024
H, HD = 16, 64
W = 512
G = 2
NQ, NKV = 256, 1024
NWIN = S // W
NCORE = 8
CHUNK = S // NCORE
TOK = 2 * CHUNK
NT = TOK // 128
AUGNEG = -524288.0
SHIFT = 15.0

CD_EPS, CD_INIT, CD_DECAY, CD_ITERS = 0.03, 4.0, 0.7, 20
CD_SCHED = [max(CD_INIT * CD_DECAY ** i, CD_EPS) for i in range(CD_ITERS)]
CD_FE = max(CD_INIT * CD_DECAY ** CD_ITERS, CD_EPS)
LOGK_KV = math.log(min(NKV * 9.0 / 8.0, float(S - W)))
LOGK_Q = math.log(min(NQ * 9.0 / 8.0, float(W)))


def core_chunks(c):
    return [(0, c), (1, c)]


def core_bh_jobs(c):
    return [(0, 2 * c), (0, 2 * c + 1), (1, 2 * c), (1, 2 * c + 1)]


def global_slots():
    tasks = []
    for w in range(NWIN - 1, 0, -1):
        for b in range(B):
            for g in range(G):
                tasks.append((b, w, g))
    slots = [[None] * NCORE for _ in range(8)]
    for i, t in enumerate(tasks):
        slots[i // NCORE][i % NCORE] = t
    slot_C = []
    for r in range(8):
        ws = [t[1] for t in slots[r] if t is not None]
        slot_C.append((max(ws) if ws else 1) * (W // 128))
    return slots, slot_C


SLOTS, SLOT_C = global_slots()


def deinterleave_cols(w):
    w = np.asarray(w)
    A = w.reshape(D, H, 32, 2)[:, :, :, 0].reshape(D, H * 32)
    Bh = w.reshape(D, H, 32, 2)[:, :, :, 1].reshape(D, H * 32)
    return np.ascontiguousarray(np.concatenate([A, Bh], axis=1))


def halfsplit_cols(wt, nh=H):
    wt = np.asarray(wt)
    A = wt.reshape(D, nh, 2, 32)[:, :, 0, :].reshape(D, nh * 32)
    Bh = wt.reshape(D, nh, 2, 32)[:, :, 1, :].reshape(D, nh * 32)
    return np.ascontiguousarray(np.concatenate([A, Bh], axis=1))


def rope_tables():
    inv = (1.0 / (10000.0 ** (np.arange(0, HD, 2, dtype=np.float32).astype(np.float32) / np.float32(HD)))).astype(np.float32)
    ang = (np.arange(S, dtype=np.float32)[:, None] * inv[None, :]).astype(np.float32)
    return np.cos(ang).astype(np.float32), np.sin(ang).astype(np.float32)


COS_T, SIN_T = rope_tables()


def own_positions(c):
    pos = []
    for (_, k) in core_chunks(c):
        pos.extend(range(k * CHUNK, (k + 1) * CHUNK))
    return np.array(pos, dtype=np.int64)


def stage_rope_fm(c):
    pos = own_positions(c)
    cos = COS_T[pos][:, np.arange(128) % 32].T
    sin = SIN_T[pos][:, np.arange(128) % 32].T
    return (np.ascontiguousarray(cos.astype(np.float32)),
            np.ascontiguousarray(sin.astype(np.float32)))


def causal_masks():
    p = np.arange(128)[:, None]
    f = np.arange(512)[None, :]
    m = np.stack([(f >= p + 128 * i) for i in range(4)]).astype(BF)
    mp = np.stack([(f <= p + 128 * i) for i in range(4)]).astype(BF)
    return np.ascontiguousarray(m), np.ascontiguousarray(mp)


MASK_C, MASK_P = causal_masks()

IDEN128 = np.eye(128).astype(BF)

ENGS = ("sync", "tensor", "vector", "scalar", "gpsimd")


class Sem:
    __slots__ = ("h", "n")
    def __init__(self, handle):
        self.h = handle
        self.n = 0


class Ctx:

    def __init__(self):
        self.nc = bass.Bass("TRN2", target_bir_lowering=False)
        self.stack = ExitStack()
        self.items = {e: [] for e in ENGS}
        self._si = 0
        self.ld = self.sem("ld")
        self.mm = self.sem("mm")
        self.ve = self.sem("ve")
        self.ac = self.sem("ac")
        self.st = self.sem("st")
        self.gp = self.sem("gp")
        self.cc = self.sem("cc")
        self.war = {}
        self.cvals = self.sb("cvals", [128, 32], F32)
        self.cmap = {}
        self.ctok = None

    def sem(self, name=None):
        h = self.stack.enter_context(self.nc.semaphore(name or f"s{self._si}"))
        self._si += 1
        return Sem(h)

    def sb(self, name, shape, dt):
        st = self.scopes[-1] if getattr(self, "scopes", None) else self.stack
        name = f"{name}_{self._si}" if getattr(self, "scopes", None) and len(self.scopes) > 0 else name
        self._si += 1
        return st.enter_context(self.nc.sbuf_tensor(name, shape, dt))

    def ps(self, name, shape, dt=F32):
        st = self.scopes[-1] if getattr(self, "scopes", None) else self.stack
        name = f"{name}_{self._si}" if getattr(self, "scopes", None) and len(self.scopes) > 0 else name
        self._si += 1
        return st.enter_context(self.nc.psum_tensor(name, shape, dt))

    def push_scope(self):
        if not hasattr(self, "scopes"):
            self.scopes = []
        self.scopes.append(ExitStack())

    def pop_scope(self):
        self.scopes.pop().close()

    def fence(self):
        g = self.gate(self.ld, self.mm, self.ve, self.ac, self.st, self.gp)
        if hasattr(self, "cc"):
            g += self.gate(self.cc)
        for e in ("sync", "tensor", "vector", "scalar", "gpsimd"):
            self.op(e, lambda E: E.nop(), wait=g)
        self.war = {}

    def io(self, name, shape, dt, ov=None):
        if ov and name in ov:
            return ov[name]
        return self.din(name, shape, dt)

    def oo(self, name, shape, dt, ov=None):
        if ov and name in ov:
            return ov[name]
        return self.dout(name, shape, dt)

    def din(self, name, shape, dt):
        return self.nc.dram_tensor(name, list(shape), dt, kind="ExternalInput").ap()

    def dout(self, name, shape, dt):
        return self.nc.dram_tensor(name, list(shape), dt, kind="ExternalOutput").ap()

    def dtmp(self, name, shape, dt, shared=False):
        kw = {"addr_space": "Shared"} if shared else {}
        return self.nc.dram_tensor(name, list(shape), dt, **kw).ap()

    def gate(self, *sems):
        return [(s, s.n) for s in sems]

    def allgather(self, src_ap, dst_ap, wait=()):
        return self.op("gpsimd", lambda E: E.collective_compute(
            "AllGather", ALU.bypass, replica_groups=RG8,
            ins=[src_ap], outs=[dst_ap]), wait=wait, inc=self.cc)

    def dynload(self, ap_fn, wait=(), eng="gpsimd"):
        et = {"gpsimd": EngineType.Pool, "sync": EngineType.SP,
              "scalar": EngineType.Activation}[eng]
        holder = {}
        def f(E):
            holder["v"] = self.nc.values_load(ap_fn(), engines=[et])
            return None
        tok = self.op(eng, f, wait=wait)
        return holder, tok

    def ccol(self, v):
        v = float(v)
        if v not in self.cmap:
            idx = len(self.cmap)
            assert idx < 32
            self.cmap[v] = idx
            self.ctok = self.op("vector", lambda E, idx=idx, v=v: E.memset(
                self.cvals[:, idx:idx + 1], v), inc=self.ve)
        return self.cvals[:, self.cmap[v]:self.cmap[v] + 1]

    def op(self, eng, fn, wait=(), inc=None, dma=False):
        waits = [(s.h, n) for (s, n) in wait if n > 0] if wait else []
        amt = 16 if dma else 1
        tok = None
        incs = []
        if inc is not None:
            inc.n += amt
            tok = (inc, inc.n)
            incs = [(inc.h, amt)]

        def g(E, fn=fn, waits=waits, incs=incs):
            for (h, n) in waits:
                E.wait_ge(h, n)
            r = fn(E)
            for (h, a) in incs:
                r.then_inc(h, a)

        self.items[eng].append(g)
        return tok

    def finalize(self):
        with self.nc.Block() as blk:
            for e in ENGS:
                if self.items[e]:
                    def runner(E, items=self.items[e]):
                        for f in items:
                            f(E)
                    getattr(blk, e)(runner)
        return self.nc


def later(*toks):
    out = {}
    for t in toks:
        if t is None:
            continue
        s, n = t
        out[s] = max(out.get(s, 0), n)
    return [(s, n) for s, n in out.items()]


def cdiv(a, b):
    return (a + b - 1) // b



def emit_load_w(c, wdram, w_sb, OD, rows=D, eng="gpsimd", wait0=()):
    toks = []
    nk = rows // 128
    sem = c.gp if eng == "gpsimd" else c.ld
    for k in range(nk):
        toks.append(c.op(eng, lambda E, k=k: E.dma_start(
            out=w_sb[:, k * OD:(k + 1) * OD], in_=wdram[k * 128:(k + 1) * 128, :]),
            wait=(wait0 if k == 0 else []), inc=sem, dma=True))
    return later(*toks)


def emit_mm_group(c, ps_ap, parts, wait0, war_key, last_stop=True):
    n = len(parts)
    tok = None
    warw = later(c.war.get(war_key)) if war_key in c.war else []
    for i, (lf, rf) in enumerate(parts):
        wv = (list(wait0) + warw) if i == 0 else []
        tok = c.op("tensor", lambda E, lf=lf, rf=rf, i=i: E.matmul(
            ps_ap, lf(), rf(), start=(i == 0), stop=(i == n - 1 and last_stop)),
            wait=wv, inc=(c.mm if i == n - 1 else None))
    return tok



def build_A(c=None, ov=None):
    if c is None:
        c = Ctx()
    x = c.io("x", [TOK, D], F32, ov)
    wq_de = c.io("wq_de", [D, D], F32, ov)
    wk_de = c.io("wk_de", [D, D], F32, ov)
    wv = c.io("wv", [D, D], F32, ov)
    wqkv_qk = c.io("wqkv_qk", [D, 2 * D], F32, ov)
    wqkv_v = c.io("wqkv_v", [D, D], F32, ov)
    wkvt0 = c.io("wkvt0", [D, 2 * D], F32, ov)
    wkvt1 = c.io("wkvt1", [D, 2 * D], F32, ov)
    wqgt0 = c.io("wqgt0", [D, D], F32, ov)
    wqgt1 = c.io("wqgt1", [D, D], F32, ov)
    gains = c.io("gains", [128, 40], F32, ov)
    lnb_cols = c.io("lnb_cols", [128, 8], F32, ov)
    rts = c.io("rts", [128, 4 * D], F32, ov)
    cosf = c.io("cosf", [128, TOK], F32, ov)
    sinf = c.io("sinf", [128, TOK], F32, ov)
    ident = c.io("ident", [128, 128], BF16, ov)
    o_q0 = c.oo("q0", [H * HD, TOK], BF16, ov)
    o_k0 = c.oo("k0", [H * HD, TOK], BF16, ov)
    o_v0 = c.oo("v0", [TOK, D], BF16, ov)
    o_ql = c.oo("qloc", [H * HD, TOK], BF16, ov)
    o_kl = c.oo("kloc", [H * HD, TOK], BF16, ov)
    o_vl = c.oo("vloc", [TOK, D], BF16, ov)
    o_kvk0 = c.oo("kvk0", [D, TOK], BF16, ov)
    o_kvk1 = c.oo("kvk1", [D, TOK], BF16, ov)
    o_kvv0 = c.oo("kvv0", [TOK, D], BF16, ov)
    o_kvv1 = c.oo("kvv1", [TOK, D], BF16, ov)
    o_scolg = c.oo("scolg", [128, 4 * NT + 8], F32, ov)
    o_xq0 = c.oo("xq0", [H * HD, TOK], BF16, ov)
    o_xq1 = c.oo("xq1", [H * HD, TOK], BF16, ov)

    x_t = c.sb("x_t", [128, 2 * D], F32)
    xn_t = c.sb("xn_t", [128, NT * D], F32)
    tm_bf = c.sb("tm_bf", [128, 2 * D], BF16)
    stat2 = c.sb("stat2", [128, NT * 12], F32)
    mv = c.sb("mv", [128, NT * 2], F32)
    cB = c.sb("cB", [128, 8 * NT], F32)
    scol = c.sb("scol", [128, 4 * NT + 8], F32)
    rts_sb = c.sb("rts_sb", [128, 4 * D], F32)
    trash = c.sb("trash", [128, D], F32)
    trash2 = c.sb("trash2", [128, D], F32)
    iden = c.sb("iden", [128, 128], BF16)
    xfm = c.sb("xfm", [128, 8 * TOK], BF16)
    w_sb = c.sb("w_sb", [128, 8 * 2 * D], BF16)
    gain = c.sb("gain", [128, 40], F32)
    lnbc = c.sb("lnbc", [128, 8], BF16)
    brow = c.sb("brow", [1, 2 * D], BF16)
    onesr = c.sb("onesr", [1, 512], BF16)
    ra = c.sb("ra", [128, 512], F32)
    rb = c.sb("rb", [128, 512], F32)
    rc = c.sb("rc", [128, 512], F32)
    rd = c.sb("rd", [128, 512], F32)
    pout = c.sb("pout", [128, 2 * TOK], BF16)
    cos_sb = c.sb("cos_sb", [128, TOK], F32)
    sin_sb = c.sb("sin_sb", [128, TOK], F32)

    psA = c.ps("psA", [128, 512], F32)
    psB = c.ps("psB", [128, 512], F32)
    psb_ = c.ps("psb_", [1, 512], F32)
    psT = c.ps("psT", [128, 128], BF16)

    tks = [
        c.op("sync", lambda E: E.dma_start(out=rts_sb[:, :], in_=rts[:, :]), inc=c.ld, dma=True),
        c.op("sync", lambda E: E.dma_start(out=cos_sb[:, :], in_=cosf[:, :]), inc=c.ld, dma=True),
        c.op("sync", lambda E: E.dma_start(out=sin_sb[:, :], in_=sinf[:, :]), inc=c.ld, dma=True),
        c.op("sync", lambda E: E.dma_start(out=gain[:, :], in_=gains[:, :]), inc=c.ld, dma=True),
        c.op("gpsimd", lambda E: E.dma_start(out=lnbc[:, :], in_=lnb_cols[:, :]), inc=c.gp, dma=True),
        c.op("sync", lambda E: E.dma_start(out=iden[:, :], in_=ident[:, :]), inc=c.ld, dma=True),
        c.op("vector", lambda E: E.memset(onesr[:, :], 1.0), inc=c.ve),
    ]
    t_const = later(*tks)

    xn_done = [None] * NT
    xnl_ready = [None] * NT
    xhat_ready = [None] * NT
    prev_tile = [None, None]
    for i in range(NT):
        buf = i % 2
        src = x_t[:, buf * D:(buf + 1) * D]
        tok_x = c.op("sync", lambda E, i=i, buf=buf: E.dma_start(
            out=x_t[:, buf * D:(buf + 1) * D], in_=x[i * 128:(i + 1) * 128, :]),
            wait=later(prev_tile[buf]), inc=c.ld, dma=True)
        c.op("vector", lambda E, i=i, src=src: E.bn_stats(stat2[:, i * 12:i * 12 + 6], src[:, 0:512]),
             wait=[tok_x], inc=c.ve)
        t_bs = c.op("vector", lambda E, i=i, src=src: E.bn_stats(stat2[:, i * 12 + 6:i * 12 + 12], src[:, 512:1024]), inc=c.ve)
        t_ag = c.op("vector", lambda E, i=i: E.bn_aggr(mv[:, i * 2:i * 2 + 2], stat2[:, i * 12:i * 12 + 12]), wait=[t_bs], inc=c.ve)
        eps5 = c.ccol(1e-5)
        t_sq = c.op("scalar", lambda E, i=i, eps5=eps5: E.activation(out=cB[:, 4 * NT + i:4 * NT + i + 1],
                                                          in_=mv[:, i * 2 + 1:i * 2 + 2],
                                                          func=AF.Sqrt, bias=eps5, scale=1.0),
                    wait=[t_ag] + later(c.ctok), inc=c.ac)
        t_rc = c.op("vector", lambda E, i=i: E.reciprocal(cB[:, i:i + 1], cB[:, 4 * NT + i:4 * NT + i + 1]),
                    wait=[t_sq], inc=c.ve)
        t_xn = c.op("vector", lambda E, i=i, src=src: E.tensor_scalar(
            out=xn_t[:, i * D:(i + 1) * D], in0=src, scalar1=mv[:, i * 2:i * 2 + 1],
            scalar2=cB[:, i:i + 1], op0=ALU.subtract, op1=ALU.mult), wait=[t_rc], inc=c.ve)
        xn_done[i] = t_xn
        prev_tile[buf] = t_xn

    for i in range(NT):
        tok = None
        for j in range(4):
            tok = c.op("vector", lambda E, i=i, j=j: E.scalar_tensor_tensor(
                out=trash[:, :], in0=xn_t[:, i * D:(i + 1) * D], scalar=1.0,
                in1=rts_sb[:, j * D:(j + 1) * D], op0=ALU.mult, op1=ALU.mult,
                accum_out=scol[:, j * NT + i:j * NT + i + 1]),
                wait=(later(xn_done[i]) + t_const) if j == 0 else [], inc=c.ve)

    t_n1 = c.op("vector", lambda E: E.memset(scol[:, 4 * NT:4 * NT + 4], -1e30), inc=c.ve)
    t_n2 = c.op("vector", lambda E: E.memset(scol[:, 4 * NT + 4:4 * NT + 8], 0.0), inc=c.ve)
    c.op("sync", lambda E: E.dma_start(out=o_scolg[:, :], in_=scol[:, :]),
         wait=c.gate(c.ve), inc=c.st, dma=True)

    for i in range(NT):
        c.op("vector", lambda E, i=i: E.bn_stats(stat2[:, i * 12:i * 12 + 6], xn_t[:, i * D:i * D + 512]),
             wait=later(xn_done[i]), inc=c.ve)
        t_bs = c.op("vector", lambda E, i=i: E.bn_stats(stat2[:, i * 12 + 6:i * 12 + 12], xn_t[:, i * D + 512:(i + 1) * D]), inc=c.ve)
        t_ag = c.op("vector", lambda E, i=i: E.bn_aggr(mv[:, i * 2:i * 2 + 2], stat2[:, i * 12:i * 12 + 12]), wait=[t_bs], inc=c.ve)
        eps5 = c.ccol(1e-5)
        t_sq = c.op("scalar", lambda E, i=i, eps5=eps5: E.activation(out=cB[:, 4 * NT + i:4 * NT + i + 1],
                                                          in_=mv[:, i * 2 + 1:i * 2 + 2],
                                                          func=AF.Sqrt, bias=eps5, scale=1.0),
                    wait=[t_ag] + later(c.ctok), inc=c.ac)
        t_rc = c.op("vector", lambda E, i=i: E.reciprocal(cB[:, 2 * NT + i:2 * NT + i + 1],
                                                          cB[:, 4 * NT + i:4 * NT + i + 1]),
                    wait=[t_sq], inc=c.ve)
        t_mu = c.op("vector", lambda E, i=i: E.tensor_copy(cB[:, NT + i:NT + i + 1], mv[:, i * 2:i * 2 + 1]),
                    wait=[t_ag], inc=c.ve)
        xnl_ready[i] = later(t_rc, t_mu)
        t_ss = c.op("scalar", lambda E, i=i: E.activation(out=trash2[:, :], in_=xn_t[:, i * D:(i + 1) * D],
                                                          func=AF.Square, accum_out=cB[:, 5 * NT + i:5 * NT + i + 1]),
                    wait=later(xn_done[i]), inc=c.ac)
        t_nr = c.op("scalar", lambda E, i=i: E.activation(out=cB[:, 6 * NT + i:6 * NT + i + 1],
                                                          in_=cB[:, 5 * NT + i:5 * NT + i + 1],
                                                          func=AF.Sqrt),
                    wait=[t_ss], inc=c.ac)
        t_mx = c.op("vector", lambda E, i=i: E.tensor_scalar(out=cB[:, 7 * NT + i:7 * NT + i + 1],
                                                             in0=cB[:, 6 * NT + i:6 * NT + i + 1],
                                                             scalar1=1e-12, scalar2=None,
                                                             op0=ALU.max, op1=ALU.bypass),
                    wait=[t_nr], inc=c.ve)
        t_rn = c.op("vector", lambda E, i=i: E.reciprocal(cB[:, 3 * NT + i:3 * NT + i + 1],
                                                          cB[:, 7 * NT + i:7 * NT + i + 1]),
                    wait=[t_mx], inc=c.ve)
        xhat_ready[i] = later(t_rn)

    def src_to_fm(mk_tile):
        prev = [None, None]
        last = None
        for i in range(NT):
            buf = i % 2
            t_c = mk_tile(i, buf, later(prev[buf]))
            for k in range(8):
                t_mm = c.op("tensor", lambda E, k=k, buf=buf: E.transpose(
                    psT[:, :], tm_bf[:, buf * D + k * 128:buf * D + (k + 1) * 128], iden[:, :]),
                    wait=later(t_c, last), inc=c.mm)
                last = c.op("scalar", lambda E, i=i, k=k: E.copy(
                    xfm[:, k * TOK + i * 128:k * TOK + (i + 1) * 128], psT[:, :]),
                    wait=[t_mm], inc=c.ac)
            prev[buf] = last
        return later(last)

    xch = [xfm[:, k * TOK:(k + 1) * TOK] for k in range(8)]

    def emit_bias_row(OD, wtok):
        toks = []
        for j in range(OD // 512):
            tok = emit_mm_group(
                c, psb_[:, :],
                [(lambda k=k: lnbc[:, k:k + 1],
                  lambda k=k, j=j: w_sb[:, k * OD + j * 512:k * OD + (j + 1) * 512]) for k in range(8)],
                wtok + t_const, "psb_")
            tv = c.op("vector", lambda E, j=j: E.tensor_copy(brow[:, j * 512:(j + 1) * 512], psb_[:, :]),
                      wait=[tok], inc=c.ve)
            c.war["psb_"] = tv
            toks.append(tv)
        return later(*toks)

    def fold_gain(OD, gslice, scale, wtok):
        toks = []
        for k in range(8):
            toks.append(c.op("vector", lambda E, k=k: E.tensor_scalar(
                out=w_sb[:, k * OD:(k + 1) * OD], in0=w_sb[:, k * OD:(k + 1) * OD],
                scalar1=gain[:, gslice + k:gslice + k + 1], scalar2=float(scale),
                op0=ALU.mult, op1=ALU.mult),
                wait=(wtok + t_const) if k == 0 else [], inc=c.ve))
        return later(*toks)

    def proj_rope_store(OD, chA, chB, out_dram, gate, bias):
        done = None
        for j in range(4):
            rope_toks = []
            for ti in range(4):
                t0 = ti * 512
                partsA = [(lambda k=k, oi=chA + j: w_sb[:, k * OD + oi * 128:k * OD + oi * 128 + 128],
                           lambda k=k, t0=t0: xch[k][:, t0:t0 + 512]) for k in range(8)]
                partsB = [(lambda k=k, oi=chB + j: w_sb[:, k * OD + oi * 128:k * OD + oi * 128 + 128],
                           lambda k=k, t0=t0: xch[k][:, t0:t0 + 512]) for k in range(8)]
                tA = emit_mm_group(c, psA[:, :], partsA, gate, "psA", last_stop=not bias)
                if bias:
                    tA = c.op("tensor", lambda E, j=j: E.matmul(
                        psA[:, :], brow[:, (chA + j) * 128:(chA + j + 1) * 128], onesr[:, :],
                        start=False, stop=True), inc=c.mm)
                tB = emit_mm_group(c, psB[:, :], partsB, gate, "psB", last_stop=not bias)
                if bias:
                    tB = c.op("tensor", lambda E, j=j: E.matmul(
                        psB[:, :], brow[:, (chB + j) * 128:(chB + j + 1) * 128], onesr[:, :],
                        start=False, stop=True), inc=c.mm)
                evA = c.op("vector", lambda E: E.tensor_copy(ra[:, :], psA[:, :]), wait=[tA], inc=c.ve)
                evB = c.op("vector", lambda E: E.tensor_copy(rb[:, :], psB[:, :]), wait=[tB], inc=c.ve)
                c.war["psA"] = evA
                c.war["psB"] = evB
                wpout = later(c.war.get("pout"))
                t1 = c.op("vector", lambda E, t0=t0: E.tensor_tensor(out=rc[:, :], in0=ra[:, :], in1=cos_sb[:, t0:t0 + 512], op=ALU.mult), wait=later(evA), inc=c.ve)
                t2 = c.op("vector", lambda E, t0=t0: E.tensor_tensor(out=rd[:, :], in0=rb[:, :], in1=sin_sb[:, t0:t0 + 512], op=ALU.mult), wait=later(evB), inc=c.ve)
                t3 = c.op("vector", lambda E, t0=t0: E.tensor_tensor(out=pout[:, t0:t0 + 512], in0=rc[:, :], in1=rd[:, :], op=ALU.subtract), wait=wpout, inc=c.ve)
                t4 = c.op("vector", lambda E, t0=t0: E.tensor_tensor(out=rc[:, :], in0=ra[:, :], in1=sin_sb[:, t0:t0 + 512], op=ALU.mult), inc=c.ve)
                t5 = c.op("vector", lambda E, t0=t0: E.tensor_tensor(out=rd[:, :], in0=rb[:, :], in1=cos_sb[:, t0:t0 + 512], op=ALU.mult), inc=c.ve)
                t6 = c.op("vector", lambda E, t0=t0: E.tensor_tensor(out=pout[:, TOK + t0:TOK + t0 + 512], in0=rc[:, :], in1=rd[:, :], op=ALU.add), inc=c.ve)
                rope_toks.append(t6)
            stoks = []
            for m in range(4):
                hh = 4 * j + m
                stoks.append(c.op("sync", lambda E, hh=hh, m=m: E.dma_start(
                    out=out_dram[hh * HD:hh * HD + 32, :],
                    in_=pout[32 * m:32 * m + 32, 0:TOK]),
                    wait=later(*rope_toks), inc=c.st, dma=True))
                stoks.append(c.op("sync", lambda E, hh=hh, m=m: E.dma_start(
                    out=out_dram[hh * HD + 32:hh * HD + 64, :],
                    in_=pout[32 * m:32 * m + 32, TOK:2 * TOK]),
                    wait=later(*rope_toks), inc=c.st, dma=True))
            done = later(*stoks)
            c.war["pout"] = done[0] if done else None
        return done

    def proj_fm_store(OD, out_dram, gate):
        for oi in range(OD // 128):
            evs = []
            for ti in range(4):
                t0 = ti * 512
                parts = [(lambda k=k, oi=oi: w_sb[:, k * OD + oi * 128:k * OD + oi * 128 + 128],
                          lambda k=k, t0=t0: xch[k][:, t0:t0 + 512]) for k in range(8)]
                tok = emit_mm_group(c, psA[:, :], parts, gate, "psA")
                ev = c.op("vector", lambda E, t0=t0: E.tensor_copy(pout[:, t0:t0 + 512], psA[:, :]),
                          wait=[tok] + later(c.war.get("pout")), inc=c.ve)
                c.war["psA"] = ev
                evs.append(ev)
            tok = c.op("sync", lambda E, oi=oi: E.dma_start(
                out=out_dram[oi * 128:(oi + 1) * 128, :], in_=pout[:, 0:TOK]),
                wait=later(*evs), inc=c.st, dma=True)
            c.war["pout"] = tok
        return

    def proj_tm_store(OD, out_dram, gate, bias):
        for tb in range(NT):
            evs = []
            for oj in range(OD // 512):
                ps = psA if oj % 2 == 0 else psB
                key = "psA" if oj % 2 == 0 else "psB"
                parts = [(lambda k=k, tb=tb: xch[k][:, tb * 128:(tb + 1) * 128],
                          lambda k=k, oj=oj: w_sb[:, k * OD + oj * 512:k * OD + (oj + 1) * 512]) for k in range(8)]
                tok = emit_mm_group(c, ps[:, :], parts, gate, key, last_stop=not bias)
                if bias:
                    tok = c.op("tensor", lambda E, oj=oj, ps=ps: E.matmul(
                        ps[:, :], onesr[0:1, 0:128], brow[:, oj * 512:(oj + 1) * 512],
                        start=False, stop=True), inc=c.mm)
                ev = c.op("vector", lambda E, oj=oj, ps=ps: E.tensor_copy(
                    pout[:, oj * 512:(oj + 1) * 512], ps[:, :]),
                    wait=[tok] + later(c.war.get("pout")),
                    inc=c.ve)
                c.war[key] = ev
                evs.append(ev)
            tok = c.op("sync", lambda E, tb=tb: E.dma_start(
                out=out_dram[tb * 128:(tb + 1) * 128, :], in_=pout[:, 0:OD]),
                wait=later(*evs), inc=c.st, dma=True)
            c.war["pout"] = tok
        return

    def mk_xn(i, buf, we):
        return c.op("vector", lambda E, i=i, buf=buf: E.tensor_copy(
            tm_bf[:, buf * D:(buf + 1) * D], xn_t[:, i * D:(i + 1) * D]),
            wait=later(xn_done[i]) + we, inc=c.ve)

    gfm = src_to_fm(mk_xn) + c.gate(c.ac)

    wtok = emit_load_w(c, wq_de, w_sb, D)
    proj_rope_store(D, 0, 4, o_q0, gfm + wtok, bias=False)
    g_w = c.gate(c.mm)
    wtok = emit_load_w(c, wk_de, w_sb, D, wait0=g_w)
    proj_rope_store(D, 0, 4, o_k0, gfm + wtok, bias=False)
    g_w = c.gate(c.mm)
    wtok = emit_load_w(c, wv, w_sb, D, wait0=g_w)
    proj_tm_store(D, o_v0, gfm + wtok, bias=False)

    g_x = c.gate(c.mm)

    def mk_xnl(i, buf, we):
        return c.op("vector", lambda E, i=i, buf=buf: E.tensor_scalar(
            out=tm_bf[:, buf * D:(buf + 1) * D], in0=xn_t[:, i * D:(i + 1) * D],
            scalar1=cB[:, NT + i:NT + i + 1], scalar2=cB[:, 2 * NT + i:2 * NT + i + 1],
            op0=ALU.subtract, op1=ALU.mult),
            wait=xnl_ready[i] + we + (g_x if i == 0 else []), inc=c.ve)

    gfm = src_to_fm(mk_xnl) + c.gate(c.ac)

    g_w = c.gate(c.mm)
    wtok = emit_load_w(c, wqkv_qk, w_sb, 2 * D, wait0=g_w)
    btok = emit_bias_row(2 * D, wtok)
    ftok = fold_gain(2 * D, 0, 1.0, wtok + btok)
    proj_rope_store(2 * D, 0, 4, o_ql, gfm + ftok, bias=True)
    proj_rope_store(2 * D, 8, 12, o_kl, gfm + ftok, bias=True)

    g_w = c.gate(c.mm)
    wtok = emit_load_w(c, wqkv_v, w_sb, D, wait0=g_w)
    btok = emit_bias_row(D, wtok)
    ftok = fold_gain(D, 0, 1.0, wtok + btok)
    proj_tm_store(D, o_vl, gfm + ftok, bias=True)

    g_x = c.gate(c.mm)

    def mk_xhat(i, buf, we):
        return c.op("vector", lambda E, i=i, buf=buf: E.tensor_scalar(
            out=tm_bf[:, buf * D:(buf + 1) * D], in0=xn_t[:, i * D:(i + 1) * D],
            scalar1=cB[:, 3 * NT + i:3 * NT + i + 1], scalar2=None,
            op0=ALU.mult, op1=ALU.bypass),
            wait=xhat_ready[i] + we + (g_x if i == 0 else []), inc=c.ve)

    gfm = src_to_fm(mk_xhat) + c.gate(c.ac)

    for g, (wk_, otk, otv) in enumerate([(wkvt0, o_kvk0, o_kvv0), (wkvt1, o_kvk1, o_kvv1)]):
        g_w = c.gate(c.mm)
        wtok = emit_load_w(c, wk_, w_sb, 2 * D, wait0=g_w)
        ftok = fold_gain(2 * D, 8 + 8 * g, 32.0, wtok)
        for oi in range(8):
            evs = []
            for ti in range(4):
                t0 = ti * 512
                parts = [(lambda k=k, oi=oi: w_sb[:, k * 2 * D + oi * 128:k * 2 * D + oi * 128 + 128],
                          lambda k=k, t0=t0: xch[k][:, t0:t0 + 512]) for k in range(8)]
                tok = emit_mm_group(c, psA[:, :], parts, gfm + ftok, "psA")
                ev = c.op("vector", lambda E, t0=t0: E.tensor_copy(pout[:, t0:t0 + 512], psA[:, :]),
                          wait=[tok] + later(c.war.get("pout")), inc=c.ve)
                c.war["psA"] = ev
                evs.append(ev)
            dk = c.op("sync", lambda E, oi=oi, otk=otk: E.dma_start(
                out=otk[oi * 128:(oi + 1) * 128, :], in_=pout[:, 0:TOK]),
                wait=later(*evs), inc=c.st, dma=True)
            c.war["pout"] = dk
        for tb in range(NT):
            evs = []
            for oj in range(2):
                ps = psA if oj % 2 == 0 else psB
                key = "psA" if oj % 2 == 0 else "psB"
                parts = [(lambda k=k, tb=tb: xch[k][:, tb * 128:(tb + 1) * 128],
                          lambda k=k, oj=oj: w_sb[:, k * 2 * D + D + oj * 512:k * 2 * D + D + (oj + 1) * 512]) for k in range(8)]
                tok = emit_mm_group(c, ps[:, :], parts, gfm + ftok, key)
                ev = c.op("vector", lambda E, oj=oj, ps=ps: E.tensor_copy(
                    pout[:, oj * 512:(oj + 1) * 512], ps[:, :]),
                    wait=[tok] + later(c.war.get("pout")), inc=c.ve)
                c.war[key] = ev
                evs.append(ev)
            dv = c.op("sync", lambda E, tb=tb, otv=otv: E.dma_start(
                out=otv[tb * 128:(tb + 1) * 128, :], in_=pout[:, 0:D]),
                wait=later(*evs), inc=c.st, dma=True)
            c.war["pout"] = dv
    for g, (wk_, ot) in enumerate([(wqgt0, o_xq0), (wqgt1, o_xq1)]):
        g_w = c.gate(c.mm)
        wtok = emit_load_w(c, wk_, w_sb, D, wait0=g_w)
        ftok = fold_gain(D, 24 + 8 * g, 32.0, wtok)
        proj_fm_store(D, ot, gfm + ftok)

    c.op("sync", lambda E: E.nop(), wait=c.gate(c.st))
    c.op("gpsimd", lambda E: E.nop(), wait=c.gate(c.st, c.gp))
    return c


def stage_A_inputs(inputs):
    wqkvT = np.ascontiguousarray(np.asarray(inputs["L0_ffn_w_qkv"]).T.astype(np.float32))
    qA_B = halfsplit_cols(wqkvT[:, 0:D])
    kA_B = halfsplit_cols(wqkvT[:, D:2 * D])
    com = dict(
        wq_de=deinterleave_cols(np.asarray(inputs["L0_attn_wq"]).astype(np.float32)),
        wk_de=deinterleave_cols(np.asarray(inputs["L0_attn_wk"]).astype(np.float32)),
        wv=np.ascontiguousarray(np.asarray(inputs["L0_attn_wv"]).astype(np.float32)),
        wqkv_qk=np.ascontiguousarray(np.concatenate([qA_B, kA_B], axis=1)),
        wqkv_v=np.ascontiguousarray(wqkvT[:, 2 * D:3 * D]),
        ident=IDEN128,
    )
    wkvt = np.asarray(inputs["L0_ffn_w_kv"]).transpose(0, 2, 1).astype(np.float32)
    com["wkvt0"] = np.ascontiguousarray(wkvt[0])
    com["wkvt1"] = np.ascontiguousarray(wkvt[1])
    wqgt = np.asarray(inputs["L0_ffn_w_q"]).transpose(0, 2, 1).astype(np.float32)
    com["wqgt0"] = np.ascontiguousarray(wqgt[0])
    com["wqgt1"] = np.ascontiguousarray(wqgt[1])
    gains = np.zeros((128, 40), np.float32)
    lng = np.asarray(inputs["L0_ffn_ln_g"]).astype(np.float32)
    cn = np.asarray(inputs["L0_ffn_cnorm_g"]).astype(np.float32)
    qn = np.asarray(inputs["L0_ffn_qnorm_g"]).astype(np.float32)
    gains[:, 0:8] = lng.reshape(8, 128).T
    gains[:, 8:24] = cn.reshape(2 * 8, 128).T
    gains[:, 24:40] = qn.reshape(2 * 8, 128).T
    com["gains"] = gains
    com["lnb_cols"] = np.ascontiguousarray(np.asarray(inputs["L0_ffn_ln_b"]).astype(np.float32).reshape(8, 128).T)
    rtq = np.asarray(inputs["L0_ffn_rt_q"]).astype(np.float32)
    rtkv = np.asarray(inputs["L0_ffn_rt_kv"]).astype(np.float32)
    rts4 = np.stack([rtq[0], rtq[1], rtkv[0], rtkv[1]])
    com["rts"] = np.ascontiguousarray(
        np.broadcast_to(rts4[:, None, :], (4, 128, D)).transpose(1, 0, 2).reshape(128, 4 * D).copy())
    x = np.asarray(inputs["L0_attn_x"]).astype(np.float32)

    in_maps = []
    for c in range(NCORE):
        xs = np.ascontiguousarray(np.concatenate(
            [x[b, k * CHUNK:(k + 1) * CHUNK, :] for (b, k) in core_chunks(c)], 0))
        cosf, sinf = stage_rope_fm(c)
        m = dict(com)
        m.update(x=xs, cosf=cosf, sinf=sinf)
        in_maps.append(m)
    return in_maps


_NC_CACHE = {}


def get_nc(name, builder):
    if name not in _NC_CACHE:
        _NC_CACHE[name] = builder().finalize()
    return _NC_CACHE[name]


def run_A(inputs):
    nc = get_nc("A", build_A)
    in_maps = stage_A_inputs(inputs)
    res = run_bass_kernel_spmd(nc, in_maps, core_ids=list(range(NCORE)))
    return res.results


def kernel(**inputs):
    inputs = {k: np.asarray(v) for k, v in inputs.items()}
    resA = run_A(inputs)
    resM = run_M2(inputs, resA)
    out = np.empty((B, S, D), np.float32)
    for cc in range(NCORE):
        for ci, (b, k) in enumerate(core_chunks(cc)):
            out[b, k * CHUNK:(k + 1) * CHUNK] = resM[cc]["out"][ci * CHUNK:(ci + 1) * CHUNK]
    return out



def emit_flash_job(c, ksb, qap, vtiles, n_kv, out_writer, st_bufs, KP=64,
                   masks=None, biases=None, gate=(), grp=3, null_kv=None, scale=0.125):
    scores_ps, p_sb, outT_ps = st_bufs
    masks = masks or {}
    biases = biases or {}
    bias_def = c.ccol(-SHIFT)
    groups = []
    t = 0
    while t < n_kv:
        if t in biases:
            groups.append([t]); t += 1
        else:
            gl = [t]; t += 1
            while t < n_kv and len(gl) < grp and t not in biases:
                gl.append(t); t += 1
            groups.append(gl)
    n_tot = n_kv + (1 if null_kv else 0)
    prev_exp = {}
    prev_av = {}
    av_tok = None
    n_av = 0
    first_av = True
    for gi, gl in enumerate(groups):
        par = gi % 2
        sp = scores_ps[par]
        pb = p_sb[par]
        w0 = list(gate) + later(prev_exp.get(par))
        sim_tok = None
        for j, t_ in enumerate(gl):
            sim_tok = c.op("tensor", lambda E, j=j, t_=t_, sp=sp: E.matmul(
                sp[:, j * 512:(j + 1) * 512], ksb(t_), qap,
                start=True, stop=True), wait=w0 if j == 0 else [], inc=c.mm)
        nb = len(gl)
        bias = biases.get(gl[0], bias_def) if nb == 1 else bias_def
        e_tok = c.op("scalar", lambda E, sp=sp, pb=pb, nb=nb, bias=bias: E.activation(
            out=pb[:, 0:nb * 512], in_=sp[:, 0:nb * 512], func=AF.Exp,
            bias=bias, scale=scale),
            wait=later(sim_tok, prev_av.get(par), c.ctok), inc=c.ac)
        m_tok = None
        for j, t_ in enumerate(gl):
            if t_ in masks:
                m_tok = c.op("vector", lambda E, j=j, t_=t_, pb=pb: E.tensor_tensor(
                    out=pb[:, j * 512:(j + 1) * 512], in0=pb[:, j * 512:(j + 1) * 512],
                    in1=masks[t_], op=ALU.mult), wait=later(e_tok), inc=c.ve)
        prev_exp[par] = later(e_tok, m_tok)[0]
        for j, t_ in enumerate(gl):
            wv = later(e_tok, m_tok)
            if first_av:
                wv = wv + later(c.war.get("outT"))
            n_av += 1
            av_tok = c.op("tensor", lambda E, j=j, t_=t_, pb=pb, la=(n_av == n_tot), fa=first_av: E.matmul(
                outT_ps[:, :], vtiles(t_), pb[:, j * 512:(j + 1) * 512],
                start=fa, stop=la), wait=wv, inc=c.mm)
            first_av = False
        prev_av[par] = av_tok
    if null_kv:
        kn, vn = null_kv
        par = len(groups) % 2
        sp = scores_ps[par]
        pb = p_sb[par]
        w0 = list(gate) + later(prev_exp.get(par))
        s_tok = c.op("tensor", lambda E: E.matmul(sp[0:1, 0:512], kn, qap,
                                                  start=True, stop=True), wait=w0, inc=c.mm)
        e_tok = c.op("scalar", lambda E, sp=sp, pb=pb: E.activation(
            out=pb[0:1, 0:512], in_=sp[0:1, 0:512], func=AF.Exp,
            bias=bias_def[0:1, :], scale=scale),
            wait=later(s_tok, prev_av.get(par), c.ctok), inc=c.ac)
        wv = later(e_tok)
        if first_av:
            wv = wv + later(c.war.get("outT"))
        av_tok = c.op("tensor", lambda E, pb=pb, fa=first_av: E.matmul(
            outT_ps[:, :], vn, pb[0:1, 0:512],
            start=fa, stop=True), wait=wv, inc=c.mm)
    out_writer(av_tok)


def emit_div_fm(c, outT_ps, psR, ones64, rrow, osb_dst, ev_f32, av_tok, qmask_row=None):
    t_l = c.op("vector", lambda E: E.tensor_scalar(
        out=rrow[:, :], in0=outT_ps[64:65, :], scalar1=1e-30, scalar2=None,
        op0=ALU.add, op1=ALU.bypass), wait=later(av_tok, c.war.get("rrow")), inc=c.ve)
    t_r = c.op("vector", lambda E: E.reciprocal(rrow[:, :], rrow[:, :]), wait=[t_l], inc=c.ve)
    if qmask_row is not None:
        t_r = c.op("vector", lambda E: E.tensor_tensor(out=rrow[:, :], in0=rrow[:, :],
                                                       in1=qmask_row, op=ALU.mult),
                   wait=[t_r], inc=c.ve)
    t_b = c.op("tensor", lambda E: E.matmul(psR[:, :], ones64, rrow[:, :],
                                            start=True, stop=True),
               wait=later(t_r, c.war.get("psR")), inc=c.mm)
    c.war["rrow"] = t_b
    t_e = c.op("vector", lambda E: E.tensor_copy(ev_f32[:, :], outT_ps[0:64, :]),
               wait=later(av_tok, c.war.get("ev_f32")), inc=c.ve)
    c.war["outT"] = t_e
    t_o = c.op("vector", lambda E: E.tensor_tensor(out=osb_dst, in0=ev_f32[:, :],
                                                   in1=psR[:, :], op=ALU.mult),
               wait=later(t_b, t_e), inc=c.ve)
    c.war["psR"] = t_o
    c.war["ev_f32"] = t_o
    return t_o



def build_B(c=None, ov=None):
    if c is None:
        c = Ctx()
    qj = c.io("qj", [4, HD, S], BF16, ov)
    kj = c.io("kj", [4, HD, S], BF16, ov)
    vj = c.io("vj", [4, S, HD], BF16, ov)
    maskc = c.io("maskc", [128, 4 * 512], BF16, ov)
    o_a = c.oo("aout", [4, HD, S], BF16, ov)

    ksb = c.sb("ksb", [64, S], BF16)
    qsb = c.sb("qsb", [64, S], BF16)
    vsb = c.sb("vsb", [128, 64, 65], BF16)
    msb = c.sb("msb", [128, 4 * 512], BF16)
    p_sb0 = c.sb("p_sb0", [128, 3 * 512], BF16)
    p_sb1 = c.sb("p_sb1", [128, 3 * 512], BF16)
    osb = c.sb("osb", [64, S], BF16)
    ev_f32 = c.sb("ev_f32", [64, 512], F32)
    rrow = c.sb("rrow", [1, 512], F32)
    ones64 = c.sb("ones64", [1, 64], F32)

    sc_ps0 = c.ps("sc_ps0", [128, 3 * 512], F32)
    sc_ps1 = c.ps("sc_ps1", [128, 3 * 512], F32)
    outT_ps = c.ps("outT_ps", [65, 512], F32)
    psR = c.ps("psR", [64, 512], F32)

    t_m = c.op("sync", lambda E: E.dma_start(out=msb[:, :], in_=maskc[:, :]), inc=c.ld, dma=True)
    t_o = c.op("vector", lambda E: E.memset(ones64[:, :], 1.0), inc=c.ve)

    for job in range(4):
        g_j = c.gate(c.mm, c.ve, c.ac)
        t_k = c.op("sync", lambda E, job=job: E.dma_start(out=ksb[:, :], in_=kj[job]), wait=g_j, inc=c.ld, dma=True)
        t_q = c.op("sync", lambda E, job=job: E.dma_start(out=qsb[:, :], in_=qj[job]), inc=c.ld, dma=True)
        t_v = c.op("sync", lambda E, job=job: E.dma_start(
            out=vsb[:, :, 0:64], in_=vj[job].rearrange("(t p) d -> p t d", p=128)), inc=c.ld, dma=True)
        t_1 = c.op("vector", lambda E: E.memset(vsb[:, :, 64:65], 1.0), wait=g_j + later(t_v), inc=c.ve)
        gate0 = later(t_k, t_q, t_v, t_1, t_m, t_o)
        for qt in range(16):
            Q0 = qt * 512
            n_kv = 4 * qt + 4
            masks = {4 * qt + i: msb[:, (i) * 512:(i + 1) * 512] for i in range(4)}
            emit_flash_job(
                c,
                ksb=lambda t_: ksb[:, t_ * 128:(t_ + 1) * 128],
                qap=qsb[:, Q0:Q0 + 512],
                vtiles=lambda t_: vsb[:, t_, :],
                n_kv=n_kv,
                out_writer=lambda av_tok, qt=qt, job=job: emit_div_fm(
                    c, outT_ps, psR, ones64[:, :], rrow, osb[:, qt * 512:(qt + 1) * 512],
                    ev_f32, av_tok),
                st_bufs=([sc_ps0, sc_ps1], [p_sb0, p_sb1], outT_ps),
                masks=masks, gate=gate0)
        g_all = c.gate(c.ve)
        c.op("sync", lambda E, job=job: E.dma_start(out=o_a[job], in_=osb[:, :]),
             wait=g_all, inc=c.st, dma=True)
    c.op("sync", lambda E: E.nop(), wait=c.gate(c.st))
    c.op("gpsimd", lambda E: E.nop(), wait=c.gate(c.st))
    return c


def assemble_full_fm(res, key):
    dims = res[0][key].shape[0]
    out = np.empty((B, dims, S), res[0][key].dtype)
    for cc in range(NCORE):
        for ci, (b, k) in enumerate(core_chunks(cc)):
            out[b, :, k * CHUNK:(k + 1) * CHUNK] = res[cc][key][:, ci * CHUNK:(ci + 1) * CHUNK]
    return out


def assemble_full_tm(res, key):
    dims = res[0][key].shape[1]
    out = np.empty((B, S, dims), res[0][key].dtype)
    for cc in range(NCORE):
        for ci, (b, k) in enumerate(core_chunks(cc)):
            out[b, k * CHUNK:(k + 1) * CHUNK, :] = res[cc][key][ci * CHUNK:(ci + 1) * CHUNK, :]
    return out


def stage_B_inputs(resA):
    qf = assemble_full_fm(resA, "q0")
    kf = assemble_full_fm(resA, "k0")
    vf = assemble_full_tm(resA, "v0")
    in_maps = []
    for cc in range(NCORE):
        qjv = np.stack([qf[b, h * HD:(h + 1) * HD, :] for (b, h) in core_bh_jobs(cc)])
        kjv = np.stack([kf[b, h * HD:(h + 1) * HD, :] for (b, h) in core_bh_jobs(cc)])
        vjv = np.stack([vf[b, :, h * HD:(h + 1) * HD] for (b, h) in core_bh_jobs(cc)])
        in_maps.append(dict(qj=np.ascontiguousarray(qjv), kj=np.ascontiguousarray(kjv),
                            vj=np.ascontiguousarray(vjv),
                            maskc=np.ascontiguousarray(MASK_C.transpose(1, 0, 2).reshape(128, 4 * 512))))
    return in_maps


def run_B(resA):
    nc = get_nc("B", build_B)
    res = run_bass_kernel_spmd(nc, stage_B_inputs(resA), core_ids=list(range(NCORE)))
    return res.results



def build_C(c=None, ov=None):
    if c is None:
        c = Ctx()
    qj = c.io("qj", [4, HD, S], BF16, ov)
    kj = c.io("kj", [4, HD, S], BF16, ov)
    vj = c.io("vj", [4, S, HD], BF16, ov)
    maskc = c.io("maskc", [128, 4 * 512], BF16, ov)
    maskp = c.io("maskp", [128, 4 * 512], BF16, ov)
    o_a = c.oo("aout", [4, HD, S], BF16, ov)

    ksb = c.sb("ksb", [64, S], BF16)
    qsb = c.sb("qsb", [64, S], BF16)
    vsb = c.sb("vsb", [128, 64, 65], BF16)
    msb = c.sb("msb", [128, 8 * 512], BF16)
    p_sb0 = c.sb("p_sb0", [128, 3 * 512], BF16)
    p_sb1 = c.sb("p_sb1", [128, 3 * 512], BF16)
    osb = c.sb("osb", [64, S], BF16)
    ev_f32 = c.sb("ev_f32", [64, 512], F32)
    rrow = c.sb("rrow", [1, 512], F32)
    ones64 = c.sb("ones64", [1, 64], F32)
    sc_ps0 = c.ps("sc_ps0", [128, 3 * 512], F32)
    sc_ps1 = c.ps("sc_ps1", [128, 3 * 512], F32)
    outT_ps = c.ps("outT_ps", [65, 512], F32)
    psR = c.ps("psR", [64, 512], F32)

    t_m = c.op("sync", lambda E: E.dma_start(out=msb[:, 0:4 * 512], in_=maskp[:, :]), inc=c.ld, dma=True)
    t_m2 = c.op("sync", lambda E: E.dma_start(out=msb[:, 4 * 512:8 * 512], in_=maskc[:, :]), inc=c.ld, dma=True)
    t_o = c.op("vector", lambda E: E.memset(ones64[:, :], 1.0), inc=c.ve)

    for job in range(4):
        g_j = c.gate(c.mm, c.ve, c.ac)
        t_k = c.op("sync", lambda E, job=job: E.dma_start(out=ksb[:, :], in_=kj[job]), wait=g_j, inc=c.ld, dma=True)
        t_q = c.op("sync", lambda E, job=job: E.dma_start(out=qsb[:, :], in_=qj[job]), inc=c.ld, dma=True)
        t_v = c.op("sync", lambda E, job=job: E.dma_start(
            out=vsb[:, :, 0:64], in_=vj[job].rearrange("(t p) d -> p t d", p=128)), inc=c.ld, dma=True)
        t_1 = c.op("vector", lambda E: E.memset(vsb[:, :, 64:65], 1.0), wait=g_j + later(t_v), inc=c.ve)
        gate0 = later(t_k, t_q, t_v, t_1, t_m, t_m2, t_o)
        for w in range(NWIN):
            if w == 0:
                tiles = [4 * w + i for i in range(4)]
                masks = {j: msb[:, (4 + i) * 512:(4 + i + 1) * 512] for j, i in enumerate(range(4))}
            else:
                tiles = [4 * (w - 1) + i for i in range(4)] + [4 * w + i for i in range(4)]
                masks = {j: msb[:, j * 512:(j + 1) * 512] for j in range(8)}
            emit_flash_job(
                c,
                ksb=lambda t_, tiles=tiles: ksb[:, tiles[t_] * 128:(tiles[t_] + 1) * 128],
                qap=qsb[:, w * 512:(w + 1) * 512],
                vtiles=lambda t_, tiles=tiles: vsb[:, tiles[t_], :],
                n_kv=len(tiles),
                out_writer=lambda av_tok, w=w: emit_div_fm(
                    c, outT_ps, psR, ones64[:, :], rrow, osb[:, w * 512:(w + 1) * 512],
                    ev_f32, av_tok),
                st_bufs=([sc_ps0, sc_ps1], [p_sb0, p_sb1], outT_ps),
                masks=masks, gate=gate0)
        g_all = c.gate(c.ve)
        c.op("sync", lambda E, job=job: E.dma_start(out=o_a[job], in_=osb[:, :]),
             wait=g_all, inc=c.st, dma=True)
    c.op("sync", lambda E: E.nop(), wait=c.gate(c.st))
    c.op("gpsimd", lambda E: E.nop(), wait=c.gate(c.st))
    return c


def stage_C_inputs(resA):
    qf = assemble_full_fm(resA, "qloc")
    kf = assemble_full_fm(resA, "kloc")
    vf = assemble_full_tm(resA, "vloc")
    mkc = np.ascontiguousarray(MASK_C.transpose(1, 0, 2).reshape(128, 4 * 512))
    mkp = np.ascontiguousarray(MASK_P.transpose(1, 0, 2).reshape(128, 4 * 512))
    in_maps = []
    for cc in range(NCORE):
        qjv = np.stack([qf[b, h * HD:(h + 1) * HD, :] for (b, h) in core_bh_jobs(cc)])
        kjv = np.stack([kf[b, h * HD:(h + 1) * HD, :] for (b, h) in core_bh_jobs(cc)])
        vjv = np.stack([vf[b, :, h * HD:(h + 1) * HD] for (b, h) in core_bh_jobs(cc)])
        in_maps.append(dict(qj=np.ascontiguousarray(qjv), kj=np.ascontiguousarray(kjv),
                            vj=np.ascontiguousarray(vjv), maskc=mkc, maskp=mkp))
    return in_maps


def run_C(resA):
    nc = get_nc("C", build_C)
    res = run_bass_kernel_spmd(nc, stage_C_inputs(resA), core_ids=list(range(NCORE)))
    return res.results



SC_KV = sum(SLOT_C)
KV_OFF = [sum(SLOT_C[:r]) for r in range(8)]
SC_Q = 8 * 4


def emit_coor(c, s_all, mask_all, SC, logk, n_iters, bufs, gate):
    bneg, tb1, tb2, ebuf = bufs["bneg"], bufs["tb1"], bufs["tb2"], bufs["ebuf"]
    rmax8, rsum8, g8, g8r, arow = bufs["rmax8"], bufs["rsum8"], bufs["g8"], bufs["g8r"], bufs["arow"]
    psT2, psB, psTot, psRow, psAE = bufs["psT2"], bufs["psB"], bufs["psTot"], bufs["psRow"], bufs["psAE"]
    ind8, ones128r, ones128c, idenf = bufs["ind8"], bufs["ones128r"], bufs["ones128c"], bufs["idenf"]
    slot_cols = bufs["slot_cols"]

    t_b = c.op("vector", lambda E: E.tensor_scalar(
        out=bneg[:, 0:SC], in0=s_all, scalar1=-1.0, scalar2=None,
        op0=ALU.mult, op1=ALU.bypass), wait=list(gate), inc=c.ve)
    a_tok = None
    for it in range(n_iters):
        ce = CD_SCHED[it]
        t1 = c.op("vector", lambda E: E.tensor_tensor(out=tb1[:, 0:SC], in0=s_all,
                                                      in1=bneg[:, 0:SC], op=ALU.add),
                  wait=later(t_b), inc=c.ve)
        if mask_all is not None:
            t2 = c.op("vector", lambda E: E.tensor_tensor(out=tb2[:, 0:SC], in0=tb1[:, 0:SC],
                                                          in1=mask_all, op=ALU.add),
                      wait=[t1], inc=c.ve)
        else:
            t2 = c.op("vector", lambda E: E.tensor_copy(tb2[:, 0:SC], tb1[:, 0:SC]),
                      wait=[t1], inc=c.ve)
        tmx = None
        for r, (off, C) in enumerate(slot_cols):
            tmx = c.op("vector", lambda E, off=off, C=C, r=r: E.tensor_scalar(
                out=tb1[:, off:off + C], in0=tb2[:, off:off + C], scalar1=0.0,
                scalar2=-3.0e38, op0=ALU.add, op1=ALU.max,
                accum_out=rmax8[:, r:r + 1]), wait=[t2] if r == 0 else [], inc=c.ve)
        ttr = c.op("tensor", lambda E: E.transpose(psT2[:, :], rmax8[:, 0:8], idenf[:, :]),
                   wait=later(tmx, c.war.get("psT2")), inc=c.mm)
        tgm = c.op("vector", lambda E: E.reduce_max(g8[0:8, 0:1], psT2[0:8, :], axis=mybir.AxisListType.X),
                   wait=[ttr], inc=c.ve)
        c.war["psT2"] = tgm
        tns = c.op("vector", lambda E, ce=ce: E.tensor_scalar(
            out=g8[:, 1:2], in0=g8[:, 0:1], scalar1=float(-1.0 / ce), scalar2=None,
            op0=ALU.mult, op1=ALU.bypass), wait=[tgm], inc=c.ve)
        ttr2 = c.op("tensor", lambda E: E.transpose(psT2[0:1, 0:8], g8[0:8, 1:2], idenf[0:8, 0:8]),
                    wait=later(tns, c.war.get("psT2")), inc=c.mm)
        tcp = c.op("vector", lambda E: E.tensor_copy(g8r[:, 0:8], psT2[0:1, 0:8]),
                   wait=[ttr2], inc=c.ve)
        c.war["psT2"] = tcp
        tb_ = c.op("tensor", lambda E: E.matmul(psB[:, 0:8], ones128r[:, :], g8r[:, 0:8],
                                                start=True, stop=True),
                   wait=later(tcp, c.war.get("psB")), inc=c.mm)
        tbg = c.op("vector", lambda E: E.tensor_copy(bufs["biasg"][:, 0:8], psB[:, 0:8]),
                   wait=later(tb_, c.war.get("biasg")), inc=c.ve)
        c.war["psB"] = tbg
        texp = None
        for r, (off, C) in enumerate(slot_cols):
            texp = c.op("scalar", lambda E, r=r, off=off, C=C, ce=ce: E.activation(
                out=ebuf[:, off:off + C], in_=tb2[:, off:off + C], func=AF.Exp,
                bias=bufs["biasg"][:, r:r + 1], scale=float(1.0 / ce),
                accum_out=rsum8[:, r:r + 1]), wait=[tbg] if r == 0 else [], inc=c.ac)
        c.war["biasg"] = texp
        ttot = c.op("tensor", lambda E: E.matmul(psTot[:, 0:1], rsum8[:, 0:8], ones128c[:, :],
                                                 start=True, stop=True),
                    wait=later(texp, c.war.get("psTot")), inc=c.mm)
        tln = c.op("scalar", lambda E: E.activation(out=g8[0:8, 2:3], in_=psTot[0:8, 0:1],
                                                    func=AF.Ln), wait=[ttot], inc=c.ac)
        c.war["psTot"] = tln
        ta1 = c.op("vector", lambda E, ce=ce: E.tensor_scalar(
            out=g8[:, 3:4], in0=g8[:, 2:3], scalar1=float(-ce),
            scalar2=float(ce * logk), op0=ALU.mult, op1=ALU.add),
            wait=[tln], inc=c.ve)
        ta2 = c.op("vector", lambda E: E.tensor_tensor(out=g8[:, 4:5], in0=g8[:, 3:4],
                                                       in1=g8[:, 0:1], op=ALU.subtract),
                   wait=[ta1], inc=c.ve)
        trow = c.op("tensor", lambda E, SC=SC: E.matmul(psRow[0:1, 0:SC], g8[0:8, 4:5], ind8[0:8, 0:SC],
                                                        start=True, stop=True),
                    wait=later(ta2, c.war.get("psRow")), inc=c.mm)
        tcw = c.op("vector", lambda E, SC=SC: E.tensor_copy(arow[:, 0:SC], psRow[0:1, 0:SC]),
                   wait=[trow], inc=c.ve)
        c.war["psRow"] = tcw
        tae = c.op("tensor", lambda E, SC=SC: E.matmul(psAE[:, 0:SC], ones128r[:, :], arow[:, 0:SC],
                                                       start=True, stop=True),
                   wait=later(tcw, c.war.get("psAE")), inc=c.mm)
        a_tok = tae
        if it < n_iters - 1:
            tr1 = c.op("vector", lambda E, SC=SC: E.scalar_tensor_tensor(
                out=tb1[:, 0:SC], in0=s_all, scalar=1.0, in1=psAE[:, 0:SC],
                op0=ALU.mult, op1=ALU.add), wait=[tae], inc=c.ve)
            t_b = c.op("vector", lambda E, SC=SC: E.tensor_scalar(
                out=bneg[:, 0:SC], in0=tb1[:, 0:SC], scalar1=0.0, scalar2=-1.0,
                op0=ALU.max, op1=ALU.mult), wait=[tr1], inc=c.ve)
            c.war["psAE"] = t_b
    return a_tok


def emit_select(c, s_all, SC, cap, bufs, a_tok, out_part, mkbias=None):
    tb1, ebuf = bufs["tb1"], bufs["ebuf"]
    psAE, psRow = bufs["psAE"], bufs["psRow"]
    psP1, psCS = bufs["psP1"], bufs["psCS"]
    satb, prefs = bufs["satb"], bufs["prefs"]
    ustrict, ones128b, resetrow, arow = bufs["ustrict"], bufs["ones128b"], bufs["resetrow"], bufs["arow"]
    crow, crow2 = bufs["crow"], bufs["crow2"]

    tf = c.op("vector", lambda E: E.scalar_tensor_tensor(
        out=tb1[:, 0:SC], in0=s_all, scalar=1.0, in1=psAE[:, 0:SC],
        op0=ALU.mult, op1=ALU.add), wait=later(a_tok), inc=c.ve)
    c.war["psAE"] = tf
    ts = c.op("vector", lambda E: E.tensor_scalar(
        out=satb[:, 0:SC], in0=tb1[:, 0:SC], scalar1=0.0, scalar2=None,
        op0=ALU.is_ge, op1=ALU.bypass), wait=[tf], inc=c.ve)
    tp1 = c.op("tensor", lambda E: E.matmul(psP1[:, 0:SC], ustrict[:, :], satb[:, 0:SC],
                                            start=True, stop=True),
               wait=later(ts, c.war.get("psP1")), inc=c.mm)
    tcs = c.op("tensor", lambda E: E.matmul(psCS[0:1, 0:SC], ones128b[:, :], satb[:, 0:SC],
                                            start=True, stop=True),
               wait=later(ts, c.war.get("psCS")), inc=c.mm)
    tcc = c.op("vector", lambda E: E.tensor_copy(crow[:, 0:SC], psCS[0:1, 0:SC]),
               wait=[tcs], inc=c.ve)
    c.war["psCS"] = tcc
    tsc = c.op("vector", lambda E: E.tensor_tensor_scan(
        crow2[:, 0:SC], resetrow[:, 0:SC], crow[:, 0:SC], 0.0, ALU.mult, ALU.add),
        wait=[tcc], inc=c.ve)
    tex = c.op("vector", lambda E: E.tensor_tensor(out=crow2[:, 0:SC], in0=crow2[:, 0:SC],
                                                   in1=crow[:, 0:SC], op=ALU.subtract),
               wait=[tsc], inc=c.ve)
    tbc = c.op("tensor", lambda E: E.matmul(psRow[0:1, 0:SC], bufs["ones1b"], crow2[:, 0:SC],
                                            start=True, stop=True),
               wait=later(tex, c.war.get("psRow")), inc=c.mm) if False else None
    tbc = c.op("tensor", lambda E: E.matmul(bufs["psAE"][:, 0:SC], bufs["ones128r"][:, :], crow2[:, 0:SC],
                                            start=True, stop=True),
               wait=later(tex, c.war.get("psAE")), inc=c.mm)
    te = c.op("vector", lambda E: E.tensor_copy(prefs[:, 0:SC], psP1[:, 0:SC]),
              wait=[tp1], inc=c.ve)
    c.war["psP1"] = te
    tpr = c.op("vector", lambda E: E.tensor_tensor(out=prefs[:, 0:SC], in0=prefs[:, 0:SC],
                                                   in1=bufs["psAE"][:, 0:SC], op=ALU.add),
               wait=later(te, tbc), inc=c.ve)
    c.war["psAE"] = tpr
    tpl = c.op("vector", lambda E, cap=cap: E.tensor_scalar(
        out=tb1[:, 0:SC], in0=prefs[:, 0:SC], scalar1=float(cap), scalar2=None,
        op0=ALU.is_lt, op1=ALU.bypass), wait=[tpr], inc=c.ve)
    tpart = c.op("vector", lambda E: E.tensor_tensor(out=out_part[:, 0:SC], in0=tb1[:, 0:SC],
                                                     in1=satb[:, 0:SC], op=ALU.mult),
                 wait=[tpl], inc=c.ve)
    tokb = None
    if mkbias is not None:
        tokb = c.op("vector", lambda E: E.tensor_scalar(
            out=mkbias[:, 0:SC], in0=out_part[:, 0:SC], scalar1=59985.0, scalar2=-60000.0,
            op0=ALU.mult, op1=ALU.add), wait=[tpart], inc=c.ve)
    return later(tpart, tokb)


def build_E(c=None, ov=None):
    if c is None:
        c = Ctx()
    skv = c.io("skv", [128, SC_KV], F32, ov)
    mkv = c.io("mkv", [128, SC_KV], F32, ov)
    sq = c.io("sq", [128, SC_Q], F32, ov)
    qslot = c.io("qslot", [8, 128, 8 * 512], BF16, ov)
    nullk = c.io("nullk", [8, 64, 16], BF16, ov)
    nullv = c.io("nullv", [8, 1, 16 * 65], BF16, ov)
    ustrict = c.io("ustrict", [128, 128], BF16, ov)
    idenf = c.io("idenf", [128, 128], F32, ov)
    ind8 = c.io("ind8", [8, SC_KV], F32, ov)
    ind8q = c.io("ind8q", [8, SC_Q], F32, ov)
    resetkv = c.io("resetkv", [1, SC_KV], F32, ov)
    resetq = c.io("resetq", [1, SC_Q], F32, ov)
    bigmjkv = c.io("bigmjkv", [128, SC_KV], F32, ov)
    bigmjq = c.io("bigmjq", [128, SC_Q], F32, ov)
    kregs = [c.io(f"kreg{r}", [1024, SLOT_C[r] * 128], BF16) for r in range(8)]
    vregs = [c.io(f"vreg{r}", [SLOT_C[r] * 128, 1024], BF16) for r in range(8)]
    o_a = c.oo("aout", [8, 128, 8 * 512], BF16, ov)
    o_pq = c.oo("pq", [8, 128, 4], F32, ov)
    o_part = c.oo("dbg_part", [128, SC_KV], F32, ov)

    skv_sb = c.sb("skv_sb", [128, SC_KV], F32)
    mkv_sb = c.sb("mkv_sb", [128, SC_KV], F32)
    sq_sb = c.sb("sq_sb", [128, SC_Q], F32)
    bneg = c.sb("bneg", [128, SC_KV], F32)
    tb1 = c.sb("tb1", [128, SC_KV], F32)
    tb2 = c.sb("tb2", [128, SC_KV], F32)
    ebuf = c.sb("ebuf", [128, SC_KV], F32)
    rmax8 = c.sb("rmax8", [128, 8], F32)
    rsum8 = c.sb("rsum8", [128, 8], F32)
    g8 = c.sb("g8", [128, 16], F32)
    g8r = c.sb("g8r", [1, 8], F32)
    biasg = c.sb("biasg", [128, 8], F32)
    arow = c.sb("arow", [1, SC_KV], F32)
    crow = c.sb("crow", [1, SC_KV], F32)
    crow2 = c.sb("crow2", [1, SC_KV], F32)
    satb = c.sb("satb", [128, SC_KV], BF16)
    prefs = c.sb("prefs", [128, SC_KV], F32)
    part_kv = c.sb("part_kv", [128, SC_KV], F32)
    biasb = c.sb("biasb", [128, SC_KV], F32)
    part_q = c.sb("part_q", [128, SC_Q], F32)
    ones128r = c.sb("ones128r", [1, 128], F32)
    ones128c = c.sb("ones128c", [128, 1], F32)
    ones128b = c.sb("ones128b", [128, 1], BF16)
    ust_sb = c.sb("ust_sb", [128, 128], BF16)
    idf_sb = c.sb("idf_sb", [128, 128], F32)
    ind8_sb = c.sb("ind8_sb", [8, SC_KV], F32)
    ind8q_sb = c.sb("ind8q_sb", [8, SC_Q], F32)
    rkv_sb = c.sb("rkv_sb", [1, SC_KV], F32)
    rq_sb = c.sb("rq_sb", [1, SC_Q], F32)
    bmjkv_sb = c.sb("bmjkv_sb", [128, SC_KV], F32)
    bmj_sb = c.sb("bmj_sb", [128, SC_Q], F32)
    keyb = c.sb("keyb", [128, SC_KV], F32)
    msk32 = c.sb("msk32", [128, SC_KV], I32)
    g8i = c.sb("g8i", [128, 2], I32)
    ksb = c.sb("ksb", [64, 7680], BF16)
    q2sb = c.sb("q2sb", [64, 512], BF16)
    vsb = c.sb("vsb", [128, 60, 65], BF16)
    nk_sb = c.sb("nk_sb", [64, 16], BF16)
    nv_sb = c.sb("nv_sb", [1, 16 * 65], BF16)
    p_sb0 = c.sb("p_sb0", [128, 3 * 512], BF16)
    p_sb1 = c.sb("p_sb1", [128, 3 * 512], BF16)
    osb = c.sb("osb", [128, 8 * 512], BF16)
    ev_f32 = c.sb("ev_f32", [64, 512], F32)
    rrow = c.sb("rrow", [1, 512], F32)
    ones64 = c.sb("ones64", [1, 64], F32)

    tks = [
        c.op("sync", lambda E: E.dma_start(out=skv_sb[:, :], in_=skv[:, :]), inc=c.ld, dma=True),
        c.op("sync", lambda E: E.dma_start(out=mkv_sb[:, :], in_=mkv[:, :]), inc=c.ld, dma=True),
        c.op("sync", lambda E: E.dma_start(out=sq_sb[:, :], in_=sq[:, :]), inc=c.ld, dma=True),
        c.op("sync", lambda E: E.dma_start(out=ust_sb[:, :], in_=ustrict[:, :]), inc=c.ld, dma=True),
        c.op("sync", lambda E: E.dma_start(out=idf_sb[:, :], in_=idenf[:, :]), inc=c.ld, dma=True),
        c.op("sync", lambda E: E.dma_start(out=ind8_sb[:, :], in_=ind8[:, :]), inc=c.ld, dma=True),
        c.op("sync", lambda E: E.dma_start(out=ind8q_sb[:, :], in_=ind8q[:, :]), inc=c.ld, dma=True),
        c.op("sync", lambda E: E.dma_start(out=rkv_sb[:, :], in_=resetkv[:, :]), inc=c.ld, dma=True),
        c.op("sync", lambda E: E.dma_start(out=rq_sb[:, :], in_=resetq[:, :]), inc=c.ld, dma=True),
        c.op("sync", lambda E: E.dma_start(out=bmjkv_sb[:, :], in_=bigmjkv[:, :]), inc=c.ld, dma=True),
        c.op("sync", lambda E: E.dma_start(out=bmj_sb[:, :], in_=bigmjq[:, :]), inc=c.ld, dma=True),
        c.op("vector", lambda E: E.memset(ones128r[:, :], 1.0), inc=c.ve),
        c.op("vector", lambda E: E.memset(ones128c[:, :], 1.0), inc=c.ve),
        c.op("vector", lambda E: E.memset(ones128b[:, :], 1.0), inc=c.ve),
        c.op("vector", lambda E: E.memset(ones64[:, :], 1.0), inc=c.ve),
    ]
    gate_in = later(*tks)

    with (c.nc.psum_tensor("psT2", [8, 128], F32) as psT2,
          c.nc.psum_tensor("psB", [128, 8], F32) as psB,
          c.nc.psum_tensor("psTot", [8, 1], F32) as psTot,
          c.nc.psum_tensor("psRow", [1, SC_KV], F32) as psRow,
          c.nc.psum_tensor("psAE", [128, SC_KV], F32) as psAE,
          c.nc.psum_tensor("psP1", [128, SC_KV], F32) as psP1,
          c.nc.psum_tensor("psCS", [1, SC_KV], F32) as psCS):
        bufs = dict(bneg=bneg, tb1=tb1, tb2=tb2, ebuf=ebuf, rmax8=rmax8, rsum8=rsum8,
                    biasg=biasg,
                    g8=g8, g8r=g8r, arow=arow, crow=crow, crow2=crow2, satb=satb,
                    prefs=prefs, psT2=psT2, psB=psB, psTot=psTot, psRow=psRow, psAE=psAE,
                    psP1=psP1, psCS=psCS, ind8=ind8_sb, ones128r=ones128r,
                    ones128c=ones128c, ones128b=ones128b, idenf=idf_sb,
                    ustrict=ust_sb, resetrow=rkv_sb,
                    slot_cols=[(KV_OFF[r], SLOT_C[r]) for r in range(8)])
        bufs["keyb"] = keyb
        bufs["msk32"] = msk32
        bufs["g8i"] = g8i
        a_kv = emit_coor(c, skv_sb[:, :], mkv_sb[:, :], SC_KV, LOGK_KV, CD_ITERS, bufs, gate_in)
        sel_kv = emit_topk_bisect(c, skv_sb[:, :], SC_KV, NKV, bufs, a_kv, part_kv, bmjkv_sb[:, :])
        tbi = c.op("vector", lambda E: E.tensor_scalar(
            out=biasb[:, :], in0=part_kv[:, :], scalar1=59985.0, scalar2=-60000.0,
            op0=ALU.mult, op1=ALU.add), wait=sel_kv, inc=c.ve)
        sel_kv = later(tbi)
        c.op("sync", lambda E: E.dma_start(out=o_part[:, :], in_=part_kv[:, :]),
             wait=sel_kv, inc=c.st, dma=True)
        bufs["ind8"] = ind8q_sb
        bufs["resetrow"] = rq_sb
        bufs["slot_cols"] = [(4 * r, 4) for r in range(8)]
        a_q = emit_coor(c, sq_sb[:, :], None, SC_Q, LOGK_Q, CD_ITERS, bufs, sel_kv)
        sel_q = emit_topk_bisect(c, sq_sb[:, :], SC_Q, NQ, bufs, a_q, part_q, bmj_sb[:, :])
        for r in range(8):
            c.op("sync", lambda E, r=r: E.dma_start(out=o_pq[r], in_=part_q[:, 4 * r:4 * (r + 1)]),
                 wait=sel_q, inc=c.st, dma=True)
    g_coor = c.gate(c.ve, c.ac, c.mm)

    sc_ps0 = c.ps("sc_ps0", [128, 3 * 512], F32)
    sc_ps1 = c.ps("sc_ps1", [128, 3 * 512], F32)
    outT_ps = c.ps("outT_ps", [65, 512], F32)
    psR = c.ps("psR", [64, 512], F32)

    for r in range(8):
        C = SLOT_C[r]
        off = KV_OFF[r]
        g_slot = c.gate(c.mm, c.ve, c.ac)
        t_nk = c.op("sync", lambda E, r=r: E.dma_start(out=nk_sb[:, :], in_=nullk[r]),
                    wait=g_slot, inc=c.ld, dma=True)
        t_nv = c.op("sync", lambda E, r=r: E.dma_start(out=nv_sb[:, :], in_=nullv[r]), inc=c.ld, dma=True)
        for h in range(H):
            g_h = c.gate(c.mm)
            t_k = c.op("sync", lambda E, r=r, h=h, C=C: E.dma_start(
                out=ksb[:, 0:C * 128], in_=kregs[r][h * 64:(h + 1) * 64, :]),
                wait=g_h, inc=c.ld, dma=True)
            t_v = c.op("sync", lambda E, r=r, h=h, C=C: E.dma_start(
                out=vsb[:, 0:C, 0:64],
                in_=vregs[r][:, h * 64:(h + 1) * 64].rearrange("(t p) d -> p t d", p=128)),
                inc=c.ld, dma=True)
            t_1 = c.op("vector", lambda E, C=C: E.memset(vsb[:, 0:C, 64:65], 1.0),
                       wait=g_h + later(t_v), inc=c.ve)
            ch, po = h // 2, (h % 2) * 64
            t_q = c.op("sync", lambda E, r=r, po=po, ch=ch: E.dma_start(
                out=q2sb[:, :], in_=qslot[r][po:po + 64, ch * 512:(ch + 1) * 512]),
                wait=g_h, inc=c.ld, dma=True)

            gate0 = later(t_q, t_nk, t_nv, t_k, t_v, t_1) + g_coor + gate_in
            biases = {t: biasb[:, off + t:off + t + 1] for t in range(C)}
            emit_flash_job(
                c,
                ksb=lambda t_: ksb[:, t_ * 128:(t_ + 1) * 128][0:64, :] if False else ksb[0:64, t_ * 128:(t_ + 1) * 128],
                qap=q2sb[:, :],
                vtiles=lambda t_: vsb[:, t_, :],
                n_kv=C,
                out_writer=lambda av_tok, ch=ch, po=po: emit_div_fm(
                    c, outT_ps, psR, ones64[:, :], rrow,
                    osb[po:po + 64, ch * 512:(ch + 1) * 512], ev_f32, av_tok),
                st_bufs=([sc_ps0, sc_ps1], [p_sb0, p_sb1], outT_ps),
                biases=biases, gate=gate0,
                null_kv=(nk_sb[:, h:h + 1], nv_sb[0:1, h * 65:(h + 1) * 65]))
        g_all = c.gate(c.ve)
        c.op("sync", lambda E, r=r: E.dma_start(out=o_a[r], in_=osb[:, :]),
             wait=g_all, inc=c.st, dma=True)
    c.op("sync", lambda E: E.nop(), wait=c.gate(c.st))
    c.op("gpsimd", lambda E: E.nop(), wait=c.gate(c.st))
    return c


def stage_E_consts():
    ust = np.triu(np.ones((128, 128), np.float32), 1).astype(BF)
    idenf = np.eye(128, dtype=np.float32)
    ind8 = np.zeros((8, SC_KV), np.float32)
    resetkv = np.ones((1, SC_KV), np.float32)
    for r in range(8):
        ind8[r, KV_OFF[r]:KV_OFF[r] + SLOT_C[r]] = 1.0
        resetkv[0, KV_OFF[r]] = 0.0
    ind8q = np.zeros((8, SC_Q), np.float32)
    resetq = np.ones((1, SC_Q), np.float32)
    for r in range(8):
        ind8q[r, 4 * r:4 * r + 4] = 1.0
        resetq[0, 4 * r] = 0.0
    bigmj = np.zeros((128, SC_Q), np.float32)
    for r in range(8):
        for col in range(4):
            bigmj[:, 4 * r + col] = 1.0e6 - (col * 128 + np.arange(128))
    bigmjkv = np.zeros((128, SC_KV), np.float32)
    for r in range(8):
        for col in range(SLOT_C[r]):
            bigmjkv[:, KV_OFF[r] + col] = 1.0e6 - (col * 128 + np.arange(128))
    return dict(ustrict=np.ascontiguousarray(ust), idenf=idenf, ind8=ind8,
                ind8q=ind8q, resetkv=resetkv, resetq=resetq, bigmjq=bigmj,
                bigmjkv=bigmjkv)


def scol_to_sfull(resA):
    s_full = np.zeros((B, S, 4), np.float32)
    for cc in range(NCORE):
        sc = resA[cc]["scolg"].astype(np.float32)
        for ci, (b, k) in enumerate(core_chunks(cc)):
            for j in range(4):
                blk = sc[:, j * NT + ci * 8: j * NT + (ci + 1) * 8]
                s_full[b, k * CHUNK:(k + 1) * CHUNK, j] = blk.T.reshape(CHUNK)
    return s_full


def stage_E_inputs(resA, inputs):
    s_full = scol_to_sfull(resA)
    xq_full = [assemble_full_fm(resA, f"xq{g}") for g in range(G)]
    kvk_full = [assemble_full_fm(resA, f"kvk{g}") for g in range(G)]
    kvv_full = [assemble_full_tm(resA, f"kvv{g}") for g in range(G)]
    nkv = np.asarray(inputs["L0_ffn_null_kv"], np.float32)
    consts = stage_E_consts()
    in_maps = []
    for cc in range(NCORE):
        skv = np.zeros((128, SC_KV), np.float32)
        mkv = np.zeros((128, SC_KV), np.float32)
        sq = np.zeros((128, SC_Q), np.float32)
        qslot = np.zeros((8, 128, 8 * 512), BF)
        nullk = np.zeros((8, 64, 16), BF)
        nullv = np.zeros((8, 1, 16 * 65), BF)
        nullv[:, 0, 64::65] = 1.0
        m = dict(consts)
        for r in range(8):
            task = SLOTS[r][cc]
            Cr = SLOT_C[r]
            if task is None:
                m[f"kreg{r}"] = np.zeros((1024, Cr * 128), BF)
                m[f"vreg{r}"] = np.zeros((Cr * 128, 1024), BF)
                continue
            b, w, g = task
            R = w * W
            Ct = R // 128
            sv = s_full[b, :R, 2 + g]
            skv[:, KV_OFF[r]:KV_OFF[r] + Ct] = sv.reshape(Ct, 128).T
            skv[:, KV_OFF[r] + Ct:KV_OFF[r] + Cr] = -1e30
            mkv[:, KV_OFF[r] + Ct:KV_OFF[r] + Cr] = -1e30
            sq[:, 4 * r:4 * r + 4] = s_full[b, w * W:(w + 1) * W, g].reshape(4, 128).T
            qs = xq_full[g][b][:, w * W:(w + 1) * W]
            qslot[r] = qs.reshape(8, 128, 512).transpose(1, 0, 2).reshape(128, 8 * 512)
            kr = np.zeros((1024, Cr * 128), BF)
            kr[:, 0:R] = kvk_full[g][b][:, :R]
            m[f"kreg{r}"] = np.ascontiguousarray(kr)
            vr = np.zeros((Cr * 128, 1024), BF)
            vr[0:R, :] = kvv_full[g][b][:R, :]
            m[f"vreg{r}"] = np.ascontiguousarray(vr)
            nullk[r] = nkv[0, g, :, 0, :].T.astype(BF)
            nv = np.zeros((16, 65), BF)
            nv[:, 0:64] = nkv[1, g, :, 0, :].astype(BF)
            nv[:, 64] = 1.0
            nullv[r] = nv.reshape(1, 16 * 65)
        m.update(skv=skv, mkv=mkv, sq=sq, qslot=qslot, nullk=nullk, nullv=nullv)
        in_maps.append(m)
    return in_maps


def run_E(resA, inputs):
    nc = get_nc("E", build_E)
    res = run_bass_kernel_spmd(nc, stage_E_inputs(resA, inputs), core_ids=list(range(NCORE)))
    return res.results


def emit_topk_bisect(c, s_all, SC, cap, bufs, a_tok, out_part, bigmj, n_iter=48):
    tb1, tb2, ebuf = bufs["tb1"], bufs["tb2"], bufs["ebuf"]
    psAE, psT2, psB, psTot = bufs["psAE"], bufs["psT2"], bufs["psB"], bufs["psTot"]
    g8, g8r = bufs["g8"], bufs["g8r"]
    rc8 = bufs["rsum8"]
    idenf, ones128r, ones128c = bufs["idenf"], bufs["ones128r"], bufs["ones128c"]
    slot_cols = bufs["slot_cols"]
    keyb = bufs["keyb"]
    midb = bufs["biasg"]
    trashq = bufs["rmax8"]

    tf = c.op("vector", lambda E: E.scalar_tensor_tensor(
        out=tb1[:, 0:SC], in0=s_all, scalar=1.0, in1=psAE[:, 0:SC],
        op0=ALU.mult, op1=ALU.add), wait=later(a_tok), inc=c.ve)
    c.war["psAE"] = tf
    tv = c.op("vector", lambda E: E.tensor_scalar(
        out=tb2[:, 0:SC], in0=tb1[:, 0:SC], scalar1=0.0, scalar2=None,
        op0=ALU.min, op1=ALU.bypass), wait=[tf], inc=c.ve)
    msk32 = bufs["msk32"]
    g8i = bufs["g8i"]
    tsat = c.op("vector", lambda E: E.tensor_scalar(
        out=msk32[:, 0:SC], in0=tb1[:, 0:SC], scalar1=0.0, scalar2=None,
        op0=ALU.is_ge, op1=ALU.bypass), wait=[tf], inc=c.ve)
    tkey = c.op("vector", lambda E: E.select(out=keyb[:, 0:SC], mask=msk32[:, 0:SC],
                                             on_true=bigmj, on_false=tb2[:, 0:SC]),
                wait=later(tv, tsat), inc=c.ve)
    tlo = c.op("vector", lambda E: E.memset(g8[:, 8:9], -65536.0), inc=c.ve)
    thi = c.op("vector", lambda E: E.memset(g8[:, 9:10], 2.0e6), inc=c.ve)
    tprev = later(tkey, tlo, thi)
    for it in range(n_iter):
        tm1 = c.op("vector", lambda E: E.tensor_tensor(out=g8[:, 10:11], in0=g8[:, 8:9],
                                                       in1=g8[:, 9:10], op=ALU.add),
                   wait=tprev, inc=c.ve)
        tm2 = c.op("vector", lambda E: E.tensor_scalar(out=g8[:, 10:11], in0=g8[:, 10:11],
                                                       scalar1=0.5, scalar2=None,
                                                       op0=ALU.mult, op1=ALU.bypass),
                   wait=[tm1], inc=c.ve)
        ttr = c.op("tensor", lambda E: E.transpose(psT2[0:1, 0:8], g8[0:8, 10:11], idenf[0:8, 0:8]),
                   wait=later(tm2, c.war.get("psT2")), inc=c.mm)
        tcp = c.op("vector", lambda E: E.tensor_copy(g8r[:, 0:8], psT2[0:1, 0:8]),
                   wait=[ttr], inc=c.ve)
        c.war["psT2"] = tcp
        tb_ = c.op("tensor", lambda E: E.matmul(psB[:, 0:8], ones128r[:, :], g8r[:, 0:8],
                                                start=True, stop=True),
                   wait=later(tcp, c.war.get("psB")), inc=c.mm)
        tmb = c.op("vector", lambda E: E.tensor_copy(midb[:, 0:8], psB[:, 0:8]),
                   wait=later(tb_, c.war.get("biasg")), inc=c.ve)
        c.war["psB"] = tmb
        tcn = None
        for r, (off, C) in enumerate(slot_cols):
            tcn = c.op("vector", lambda E, r=r, off=off, C=C: E.tensor_scalar(
                out=ebuf[:, off:off + C], in0=keyb[:, off:off + C],
                scalar1=midb[:, r:r + 1], scalar2=0.0, op0=ALU.is_ge, op1=ALU.add,
                accum_out=rc8[:, r:r + 1]), wait=[tmb] if r == 0 else [], inc=c.ve)
        ttot = c.op("tensor", lambda E: E.matmul(psTot[:, 0:1], rc8[:, 0:8], ones128c[:, :],
                                                 start=True, stop=True),
                    wait=later(tcn, c.war.get("psTot")), inc=c.mm)
        tge = c.op("vector", lambda E, cap=cap: E.tensor_scalar(
            out=g8i[0:8, 0:1], in0=psTot[0:8, 0:1], scalar1=float(cap), scalar2=None,
            op0=ALU.is_ge, op1=ALU.bypass), wait=[ttot], inc=c.ve)
        c.war["psTot"] = tge
        ts1 = c.op("vector", lambda E: E.select(out=g8[0:8, 8:9], mask=g8i[0:8, 0:1],
                                                on_true=g8[0:8, 10:11], on_false=g8[0:8, 8:9]),
                   wait=[tge], inc=c.ve)
        ts2 = c.op("vector", lambda E: E.select(out=g8[0:8, 9:10], mask=g8i[0:8, 0:1],
                                                on_true=g8[0:8, 9:10], on_false=g8[0:8, 10:11]),
                   wait=[ts1], inc=c.ve)
        tprev = [ts2[0] if False else ts2][0] and later(ts2)
    ttr = c.op("tensor", lambda E: E.transpose(psT2[0:1, 0:8], g8[0:8, 8:9], idenf[0:8, 0:8]),
               wait=later(later(ts2)[0], c.war.get("psT2")), inc=c.mm)
    tcp = c.op("vector", lambda E: E.tensor_copy(g8r[:, 0:8], psT2[0:1, 0:8]),
               wait=[ttr], inc=c.ve)
    c.war["psT2"] = tcp
    tb_ = c.op("tensor", lambda E: E.matmul(psB[:, 0:8], ones128r[:, :], g8r[:, 0:8],
                                            start=True, stop=True),
               wait=later(tcp, c.war.get("psB")), inc=c.mm)
    tmb = c.op("vector", lambda E: E.tensor_copy(midb[:, 0:8], psB[:, 0:8]),
               wait=later(tb_, c.war.get("biasg")), inc=c.ve)
    c.war["psB"] = tmb
    tp = None
    for r, (off, C) in enumerate(slot_cols):
        tp = c.op("vector", lambda E, r=r, off=off, C=C: E.tensor_scalar(
            out=out_part[:, off:off + C], in0=keyb[:, off:off + C],
            scalar1=midb[:, r:r + 1], scalar2=None, op0=ALU.is_ge, op1=ALU.bypass),
            wait=[tmb] if r == 0 else [], inc=c.ve)
    return later(tp)



def build_F(c=None, ov=None):
    if c is None:
        c = Ctx()
    x = c.io("x", [TOK, D], F32, ov)
    attn0 = c.io("attn0", [H * HD, TOK], BF16, ov)
    attnl = c.io("attnl", [H * HD, TOK], BF16, ov)
    attng0 = c.io("attng0", [H * HD, TOK], BF16, ov)
    attng1 = c.io("attng1", [H * HD, TOK], BF16, ov)
    pq = c.io("pq", [128, NT * 2], F32, ov)
    wo = c.io("wo", [D, D], F32, ov)
    wob = c.io("wob", [1, D], F32, ov)
    wlo = c.io("wlo", [D, D], F32, ov)
    wog0 = c.io("wog0", [D, D], F32, ov)
    wog1 = c.io("wog1", [D, D], F32, ov)
    wq1 = c.io("wq1", [D, D], F32, ov)
    wk1 = c.io("wk1", [D, D], F32, ov)
    wv1 = c.io("wv1", [D, D], F32, ov)
    bqkv = c.io("bqkv", [1, 3 * D], F32, ov)
    rwb = c.io("rwb", [128, 8 * D], F32, ov)
    ebias = c.io("ebias", [128, 8], F32, ov)
    sW1 = c.io("sW1", [D, 512], F32, ov)
    sW2 = c.io("sW2", [512, D], F32, ov)
    rW1 = [c.io(f"rW1_{e}", [D, 512], F32, ov) for e in range(8)]
    rW2 = [c.io(f"rW2_{e}", [512, D], F32, ov) for e in range(8)]
    hbias = c.io("hbias", [1, 9 * 512], F32, ov)
    rb2 = c.io("rb2", [8, D], F32, ov)
    sb2 = c.io("sb2", [1, D], F32, ov)
    ident = c.io("ident", [128, 128], BF16, ov)
    sel8d = c.io("sel8d", [8, 8 * 128], BF16, ov)

    o_x1 = c.oo("x1", [TOK, D], F32, ov)
    o_xn1 = c.oo("xn1", [TOK, D], F32, ov)
    o_g1 = c.oo("g1", [TOK, D], F32, ov)
    o_q1 = c.oo("q1", [H * HD, TOK], BF16, ov)
    o_k1 = c.oo("k1", [H * HD, TOK], BF16, ov)
    o_v1 = c.oo("v1", [TOK, D], BF16, ov)
    o_aug = c.oo("k1aug", [1, TOK], BF16, ov)
    o_qm = c.oo("qmask", [1, TOK], BF16, ov)
    o_dbg = c.oo("dbg", [128, 64], F32, ov)
    s0scr = c.dtmp("s0scr", [TOK, D], F32)
    hscr = c.dtmp("hscr", [9, 512, TOK], BF16)

    BUF_A = c.sb("BUF_A", [128, NT * D], F32)
    asrc = c.sb("asrc", [128, 8 * TOK], BF16)
    w_sb = c.sb("w_sb", [128, 8 * D], BF16)
    x_t = c.sb("x_t", [128, 2 * D], F32)
    sc1 = c.sb("sc1", [128, 2 * D], F32)
    pout = c.sb("pout", [128, TOK], BF16)
    stat2 = c.sb("stat2", [128, 12], F32)
    mv = c.sb("mv", [128, 2 * NT], F32)
    cols = c.sb("cols", [128, 32], F32)
    pq_sb = c.sb("pq_sb", [128, NT * 2], F32)
    brow = c.sb("brow", [1, 3 * D], BF16)
    onesr = c.sb("onesr", [1, 512], BF16)
    iden = c.sb("iden", [128, 128], BF16)
    tm_bf = c.sb("tm_bf", [128, 2 * D], BF16)
    lgc = c.sb("lgc", [128, NT * 8], F32)
    actc = c.sb("actc", [128, NT * 8], F32)
    combc = c.sb("combc", [128, NT * 8], F32)
    m8 = c.sb("m8", [128, 8], F32)
    ct8 = c.sb("ct8", [8, TOK], BF16)
    sel8 = c.sb("sel8", [8, 8 * 128], BF16)
    hrow = c.sb("hrow", [1, 9 * 512], BF16)
    rb2sb = c.sb("rb2sb", [8, D], BF16)
    sb2row = c.sb("sb2row", [1, D], BF16)
    hsb = c.sb("hsb", [128, 4 * TOK], BF16)
    hld = c.sb("hld", [128, 4 * 128], BF16)
    rsdbg = c.sb("rsdbg", [128, NT], F32)
    augc = c.sb("augc", [128, NT], BF16)
    qmc = c.sb("qmc", [128, NT], F32)
    augr = c.sb("augr", [1, TOK], BF16)
    qmr = c.sb("qmr", [1, TOK], BF16)
    rwf = c.sb("rwf", [128, D], F32)
    ebias_sb = c.sb("ebias_sb", [128, 8], F32)
    trash = c.sb("trash", [128, D], F32)
    wobrow = c.sb("wobrow", [1, D], BF16)

    psA = c.ps("psA", [128, 512], F32)
    psB = c.ps("psB", [128, 512], F32)
    psT = c.ps("psT", [128, 128], BF16)
    psC = c.ps("psC", [128, 512], F32)

    tks = [
        c.op("sync", lambda E: E.dma_start(out=iden[:, :], in_=ident[:, :]), inc=c.ld, dma=True),
        c.op("sync", lambda E: E.dma_start(out=pq_sb[:, :], in_=pq[:, :]), inc=c.ld, dma=True),
        c.op("sync", lambda E: E.dma_start(out=ebias_sb[:, :], in_=ebias[:, :]), inc=c.ld, dma=True),
        c.op("gpsimd", lambda E: E.dma_start(out=brow[:, :], in_=bqkv[:, :]), inc=c.gp, dma=True),
        c.op("gpsimd", lambda E: E.dma_start(out=hrow[:, :], in_=hbias[:, :]), inc=c.gp, dma=True),
        c.op("gpsimd", lambda E: E.dma_start(out=rb2sb[:, :], in_=rb2[:, :]), inc=c.gp, dma=True),
        c.op("gpsimd", lambda E: E.dma_start(out=sb2row[:, :], in_=sb2[:, :]), inc=c.gp, dma=True),
        c.op("gpsimd", lambda E: E.dma_start(out=wobrow[:, :], in_=wob[:, :]), inc=c.gp, dma=True),
        c.op("sync", lambda E: E.dma_start(out=sel8[:, :], in_=sel8d[:, :]), inc=c.ld, dma=True),
        c.op("vector", lambda E: E.memset(onesr[:, :], 1.0), inc=c.ve),
    ]
    t_const = later(*tks)

    def load_fm(src_ap, gate):
        toks = []
        for kk in range(8):
            toks.append(c.op("sync", lambda E, kk=kk, src_ap=src_ap: E.dma_start(
                out=asrc[:, kk * TOK:(kk + 1) * TOK], in_=src_ap[kk * 128:(kk + 1) * 128, :]),
                wait=(gate if kk == 0 else []), inc=c.ld, dma=True))
        return later(*toks)

    def tm_proj_pass(wdram, evac_fn, gate, bias_row=None):
        wtok = emit_load_w(c, wdram, w_sb, D, wait0=c.gate(c.mm))
        for tb in range(NT):
            for oj in range(2):
                ps = psA if oj % 2 == 0 else psB
                key = "psA" if oj % 2 == 0 else "psB"
                parts = [(lambda k=k, tb=tb: asrc[:, k * TOK + tb * 128:k * TOK + (tb + 1) * 128],
                          lambda k=k, oj=oj: w_sb[:, k * D + oj * 512:k * D + (oj + 1) * 512]) for k in range(8)]
                tok = emit_mm_group(c, ps[:, :], parts, gate + wtok, key,
                                    last_stop=(bias_row is None))
                if bias_row is not None:
                    tok = c.op("tensor", lambda E, oj=oj, ps=ps, bias_row=bias_row: E.matmul(
                        ps[:, :], onesr[0:1, 0:128], bias_row[0:1, oj * 512:(oj + 1) * 512],
                        start=False, stop=True), inc=c.mm)
                evac_fn(tb, oj, ps, "psA" if oj % 2 == 0 else "psB", tok)

    g0 = load_fm(attnl, [])
    def ev_local(tb, oj, ps, key, tok):
        ev = c.op("vector", lambda E, tb=tb, oj=oj, ps=ps: E.tensor_copy(
            BUF_A[:, tb * D + oj * 512:tb * D + (oj + 1) * 512], ps[:, :]), wait=[tok], inc=c.ve)
        c.war[key] = ev
    tm_proj_pass(wlo, ev_local, g0)

    def ev_glob(gg):
        def f(tb, oj, ps, key, tok):
            sl = slice(tb * D + oj * 512, tb * D + (oj + 1) * 512)
            ev = c.op("vector", lambda E, tb=tb, sl=sl, ps=ps, gg=gg: E.scalar_tensor_tensor(
                out=BUF_A[:, sl], in0=ps[:, :], scalar=pq_sb[:, gg * NT + tb:gg * NT + tb + 1],
                in1=BUF_A[:, sl], op0=ALU.mult, op1=ALU.add), wait=[tok] + t_const, inc=c.ve)
            c.war[key] = ev
        return f
    g0 = load_fm(attng0, c.gate(c.mm))
    tm_proj_pass(wog0, ev_glob(0), g0)
    g0 = load_fm(attng1, c.gate(c.mm))
    tm_proj_pass(wog1, ev_glob(1), g0)
    g0 = load_fm(attn0, c.gate(c.mm))
    def ev_s0(tb, oj, ps, key, tok):
        wv = [tok] + (later(c.war.get("sc1dma")) if oj == 0 else [])
        ev = c.op("vector", lambda E, oj=oj, ps=ps: E.tensor_copy(
            sc1[:, oj * 512:(oj + 1) * 512], ps[:, :]), wait=wv, inc=c.ve)
        c.war[key] = ev
        if oj == 1:
            d = c.op("sync", lambda E, tb=tb: E.dma_start(out=s0scr[tb * 128:(tb + 1) * 128, :],
                                                          in_=sc1[:, 0:D]), wait=[ev], inc=c.st, dma=True)
            c.war["sc1dma"] = d
    tm_proj_pass(wo, ev_s0, g0 + c.gate(c.gp), bias_row=wobrow)

    g_acc = c.gate(c.ve, c.mm, c.st)
    xn1_done = [None] * NT
    prev = [None, None]
    for i in range(NT):
        buf = i % 2
        xs = x_t[:, buf * D:(buf + 1) * D]
        s0s = sc1[:, buf * D:(buf + 1) * D]
        t_x = c.op("sync", lambda E, i=i, buf=buf: E.dma_start(
            out=x_t[:, buf * D:(buf + 1) * D], in_=x[i * 128:(i + 1) * 128, :]),
            wait=(g_acc if i == 0 else []) + later(prev[buf]), inc=c.ld, dma=True)
        t_s = c.op("sync", lambda E, i=i, buf=buf: E.dma_start(
            out=sc1[:, buf * D:(buf + 1) * D], in_=s0scr[i * 128:(i + 1) * 128, :]),
            wait=(g_acc if i == 0 else []), inc=c.ld, dma=True)
        t_c = c.op("vector", lambda E, i=i: E.scalar_tensor_tensor(
            out=cols[:, 0:1], in0=pq_sb[:, i:i + 1], scalar=1.0,
            in1=pq_sb[:, NT + i:NT + i + 1], op0=ALU.mult, op1=ALU.add),
            wait=(g_acc if i == 0 else []), inc=c.ve)
        t_c2 = c.op("vector", lambda E: E.tensor_scalar(
            out=cols[:, 1:2], in0=cols[:, 0:1], scalar1=1.0, scalar2=1e-5,
            op0=ALU.add, op1=ALU.max), wait=[t_c], inc=c.ve)
        t_rc = c.op("vector", lambda E: E.reciprocal(cols[:, 2:3], cols[:, 1:2]), wait=[t_c2], inc=c.ve)
        t_g = c.op("vector", lambda E, i=i: E.tensor_scalar(
            out=trash[:, 0:D], in0=BUF_A[:, i * D:(i + 1) * D], scalar1=cols[:, 2:3],
            scalar2=None, op0=ALU.mult, op1=ALU.bypass), wait=[t_rc], inc=c.ve)
        t_si = c.op("scalar", lambda E: E.activation(out=trash[:, 0:D], in_=trash[:, 0:D],
                                                     func=AF.Silu), wait=[t_g], inc=c.ac)
        t_m = c.op("vector", lambda E, s0s=s0s: E.tensor_tensor(
            out=trash[:, 0:D], in0=trash[:, 0:D], in1=s0s, op=ALU.mult),
            wait=[t_si] + later(t_s), inc=c.ve)
        t_x1 = c.op("vector", lambda E, xs=xs: E.tensor_tensor(
            out=xs, in0=xs, in1=trash[:, 0:D], op=ALU.add), wait=[t_m] + later(t_x), inc=c.ve)
        t_st = c.op("sync", lambda E, i=i, buf=buf: E.dma_start(
            out=o_x1[i * 128:(i + 1) * 128, :], in_=x_t[:, buf * D:(buf + 1) * D]),
            wait=[t_x1], inc=c.st, dma=True)
        t_b1 = c.op("vector", lambda E, xs=xs: E.bn_stats(stat2[:, 0:6], xs[:, 0:512]), wait=[t_x1], inc=c.ve)
        t_b2 = c.op("vector", lambda E, xs=xs: E.bn_stats(stat2[:, 6:12], xs[:, 512:1024]), inc=c.ve)
        t_ag = c.op("vector", lambda E, i=i: E.bn_aggr(mv[:, 2 * i:2 * i + 2], stat2[:, 0:12]), wait=[t_b1, t_b2], inc=c.ve)
        t_ep = c.op("vector", lambda E, i=i: E.tensor_scalar(
            out=cols[:, 24 + (i % 2):25 + (i % 2)], in0=mv[:, 2 * i + 1:2 * i + 2],
            scalar1=1e-5, scalar2=None, op0=ALU.add, op1=ALU.bypass), wait=[t_ag], inc=c.ve)
        t_sq = c.op("scalar", lambda E, i=i: E.activation(out=cols[:, 8 + i:9 + i],
                                                          in_=cols[:, 24 + (i % 2):25 + (i % 2)],
                                                          func=AF.Sqrt),
                    wait=[t_ep], inc=c.ac)
        t_rc2 = c.op("vector", lambda E, i=i: E.reciprocal(cols[:, 4:5], cols[:, 8 + i:9 + i]), wait=[t_sq], inc=c.ve)
        t_xn = c.op("vector", lambda E, i=i, xs=xs: E.tensor_scalar(
            out=BUF_A[:, i * D:(i + 1) * D], in0=xs, scalar1=mv[:, 2 * i:2 * i + 1],
            scalar2=cols[:, 4:5], op0=ALU.subtract, op1=ALU.mult), wait=[t_rc2], inc=c.ve)
        c.op("sync", lambda E, i=i: E.dma_start(
            out=o_xn1[i * 128:(i + 1) * 128, :], in_=BUF_A[:, i * D:(i + 1) * D]),
            wait=[t_xn], inc=c.st, dma=True)
        xn1_done[i] = t_xn
        prev[buf] = later(t_x1, t_st)[0]
        t_rs = c.op("vector", lambda E, i=i: E.tensor_scalar(
            out=trash[:, 0:D], in0=BUF_A[:, i * D:(i + 1) * D], scalar1=0.0, scalar2=0.0,
            op0=ALU.add, op1=ALU.add, accum_out=cols[:, 5:6]), wait=[t_xn], inc=c.ve)
        t_qm = c.op("vector", lambda E, i=i: E.tensor_scalar(
            out=qmc[:, i:i + 1], in0=cols[:, 5:6], scalar1=0.0, scalar2=None,
            op0=ALU.not_equal, op1=ALU.bypass), wait=[t_rs], inc=c.ve)
        c.op("vector", lambda E, i=i: E.tensor_copy(rsdbg[:, i:i + 1], cols[:, 5:6]), inc=c.ve)
        c.op("vector", lambda E, i=i: E.tensor_scalar(
            out=augc[:, i:i + 1], in0=qmc[:, i:i + 1], scalar1=float(-AUGNEG),
            scalar2=float(AUGNEG), op0=ALU.mult, op1=ALU.add), wait=[t_qm], inc=c.ve)

    g_xn = later(*[t for t in xn1_done]) + t_const
    for e in range(8):
        t_rw = c.op("sync", lambda E, e=e: E.dma_start(out=rwf[:, :], in_=rwb[:, e * D:(e + 1) * D]),
                    wait=c.gate(c.ve), inc=c.ld, dma=True)
        for i in range(NT):
            c.op("vector", lambda E, i=i, e=e: E.scalar_tensor_tensor(
                out=trash[:, 0:D], in0=BUF_A[:, i * D:(i + 1) * D], scalar=1.0,
                in1=rwf[:, :], op0=ALU.mult, op1=ALU.mult,
                accum_out=lgc[:, i * 8 + e:i * 8 + e + 1]),
                wait=(g_xn + later(t_rw)) if i == 0 else [], inc=c.ve)
    for i in range(NT):
        t_lb = c.op("vector", lambda E, i=i: E.tensor_tensor(
            out=lgc[:, i * 8:(i + 1) * 8], in0=lgc[:, i * 8:(i + 1) * 8],
            in1=ebias_sb[:, 0:8], op=ALU.add), wait=c.gate(c.ve) if i == 0 else [], inc=c.ve)
        t_r = c.op("vector", lambda E, i=i: E.tensor_scalar(
            out=actc[:, i * 8:(i + 1) * 8], in0=lgc[:, i * 8:(i + 1) * 8],
            scalar1=0.0, scalar2=None, op0=ALU.max, op1=ALU.bypass), wait=[t_lb], inc=c.ve)
        t_a0 = c.op("vector", lambda E, i=i: E.tensor_scalar(
            out=combc[:, i * 8:(i + 1) * 8], in0=lgc[:, i * 8:(i + 1) * 8],
            scalar1=-1.0, scalar2=None, op0=ALU.mult, op1=ALU.bypass), wait=[t_lb], inc=c.ve)
        t_a = c.op("vector", lambda E, i=i: E.tensor_tensor(
            out=combc[:, i * 8:(i + 1) * 8], in0=combc[:, i * 8:(i + 1) * 8],
            in1=lgc[:, i * 8:(i + 1) * 8], op=ALU.max), wait=[t_a0], inc=c.ve)
        t_e = c.op("scalar", lambda E, i=i: E.activation(
            out=combc[:, i * 8:(i + 1) * 8], in_=combc[:, i * 8:(i + 1) * 8],
            func=AF.Exp, scale=-1.0), wait=[t_a], inc=c.ac)
        t_l0 = c.op("vector", lambda E, i=i: E.tensor_scalar(
            out=combc[:, i * 8:(i + 1) * 8], in0=combc[:, i * 8:(i + 1) * 8],
            scalar1=1.0, scalar2=None, op0=ALU.add, op1=ALU.bypass), wait=[t_e], inc=c.ve)
        t_l = c.op("scalar", lambda E, i=i: E.activation(
            out=combc[:, i * 8:(i + 1) * 8], in_=combc[:, i * 8:(i + 1) * 8],
            func=AF.Ln), wait=[t_l0], inc=c.ac)
        t_sp = c.op("vector", lambda E, i=i: E.tensor_tensor(
            out=actc[:, i * 8:(i + 1) * 8], in0=actc[:, i * 8:(i + 1) * 8],
            in1=combc[:, i * 8:(i + 1) * 8], op=ALU.add), wait=[t_l], inc=c.ve)
        t_sa = c.op("scalar", lambda E, i=i: E.activation(
            out=actc[:, i * 8:(i + 1) * 8], in_=actc[:, i * 8:(i + 1) * 8],
            func=AF.Sqrt), wait=[t_sp], inc=c.ac)
        t_m8 = c.op("vector", lambda E, i=i: E.max(out=m8[:, 0:8], in_=actc[:, i * 8:(i + 1) * 8]),
                    wait=[t_sa], inc=c.ve)
        t_ge = c.op("vector", lambda E, i=i: E.tensor_scalar(
            out=combc[:, i * 8:(i + 1) * 8], in0=actc[:, i * 8:(i + 1) * 8],
            scalar1=m8[:, 1:2], scalar2=None, op0=ALU.is_ge, op1=ALU.bypass),
            wait=[t_m8], inc=c.ve)
        c.op("vector", lambda E, i=i: E.tensor_tensor(
            out=combc[:, i * 8:(i + 1) * 8], in0=combc[:, i * 8:(i + 1) * 8],
            in1=actc[:, i * 8:(i + 1) * 8], op=ALU.mult), wait=[t_ge], inc=c.ve)

    g_cols = c.gate(c.ve)
    last = None
    for i in range(NT):
        t1 = c.op("tensor", lambda E, i=i: E.transpose(psT[0:1, 0:128], augc[:, i:i + 1], iden[:, :]),
                  wait=g_cols + later(last) + t_const, inc=c.mm)
        last = c.op("vector", lambda E, i=i: E.tensor_copy(augr[:, i * 128:(i + 1) * 128], psT[0:1, 0:128]),
                    wait=[t1], inc=c.ve)
    combcb = c.sb("combcb", [128, 8], BF16)
    for i in range(NT):
        t0b = c.op("vector", lambda E, i=i: E.tensor_copy(combcb[:, :], combc[:, i * 8:(i + 1) * 8]),
                   wait=later(last), inc=c.ve)
        t1 = c.op("tensor", lambda E, i=i: E.transpose(psT[0:8, 0:128], combcb[:, :], iden[:, :]),
                  wait=later(t0b), inc=c.mm)
        last = c.op("vector", lambda E, i=i: E.tensor_copy(ct8[:, i * 128:(i + 1) * 128], psT[0:8, 0:128]),
                    wait=[t1], inc=c.ve)
    qmcb = c.sb("qmcb", [128, NT], BF16)
    t_qc = c.op("vector", lambda E: E.tensor_copy(qmcb[:, :], qmc[:, :]), wait=later(last), inc=c.ve)
    for i in range(NT):
        t1 = c.op("tensor", lambda E, i=i: E.transpose(psT[0:1, 0:128], qmcb[:, i:i + 1], iden[:, :]),
                  wait=later(last, t_qc), inc=c.mm)
        last = c.op("vector", lambda E, i=i: E.tensor_copy(qmr[:, i * 128:(i + 1) * 128], psT[0:1, 0:128]),
                    wait=[t1], inc=c.ve)
    c.op("sync", lambda E: E.dma_start(out=o_aug[:, :], in_=augr[:, :]), wait=later(last), inc=c.st, dma=True)
    c.op("sync", lambda E: E.dma_start(out=o_qm[:, :], in_=qmr[:, :]), wait=later(last), inc=c.st, dma=True)
    t_cr = c.gate(c.ve)[0]

    g_asrc = c.gate(c.mm)
    prevt = [None, None]
    lastT = None
    for i in range(NT):
        buf = i % 2
        t_cst = c.op("vector", lambda E, i=i, buf=buf: E.tensor_copy(
            tm_bf[:, buf * D:(buf + 1) * D], BUF_A[:, i * D:(i + 1) * D]),
            wait=later(xn1_done[i], prevt[buf]) + (g_asrc if i == 0 else []), inc=c.ve)
        for kk in range(8):
            t_mm = c.op("tensor", lambda E, kk=kk, buf=buf: E.transpose(
                psT[:, :], tm_bf[:, buf * D + kk * 128:buf * D + (kk + 1) * 128], iden[:, :]),
                wait=later(t_cst, lastT), inc=c.mm)
            lastT = c.op("scalar", lambda E, i=i, kk=kk: E.copy(
                asrc[:, kk * TOK + i * 128:kk * TOK + (i + 1) * 128], psT[:, :]),
                wait=[t_mm], inc=c.ac)
        prevt[buf] = lastT
    g_xfm = c.gate(c.ac)

    def fm_proj_store(wdram, out_dram, brow_off):
        wtok = emit_load_w(c, wdram, w_sb, D, wait0=c.gate(c.mm))
        for oi in range(8):
            evs = []
            for ti in range(4):
                t0 = ti * 512
                parts = [(lambda k=k, oi=oi: w_sb[:, k * D + oi * 128:k * D + oi * 128 + 128],
                          lambda k=k, t0=t0: asrc[:, k * TOK + t0:k * TOK + t0 + 512]) for k in range(8)]
                tok = emit_mm_group(c, psA[:, :], parts, g_xfm + wtok + t_const, "psA", last_stop=False)
                tok = c.op("tensor", lambda E, oi=oi, brow_off=brow_off: E.matmul(
                    psA[:, :], brow[0:1, brow_off + oi * 128:brow_off + (oi + 1) * 128], onesr[:, :],
                    start=False, stop=True), inc=c.mm)
                ev = c.op("scalar", lambda E, t0=t0: E.activation(
                    out=pout[:, t0:t0 + 512], in_=psA[:, :], func=AF.Relu),
                    wait=[tok] + later(c.war.get("pout")), inc=c.ac)
                c.war["psA"] = ev
                evs.append(ev)
            d = c.op("sync", lambda E, oi=oi, out_dram=out_dram: E.dma_start(
                out=out_dram[oi * 128:(oi + 1) * 128, :], in_=pout[:, :]),
                wait=later(*evs), inc=c.st, dma=True)
            c.war["pout"] = d
    fm_proj_store(wq1, o_q1, 0)
    fm_proj_store(wk1, o_k1, D)
    wtok = emit_load_w(c, wv1, w_sb, D, wait0=c.gate(c.mm))
    for tb in range(NT):
        evs = []
        for oj in range(2):
            ps = psA if oj % 2 == 0 else psB
            key = "psA" if oj % 2 == 0 else "psB"
            parts = [(lambda k=k, tb=tb: asrc[:, k * TOK + tb * 128:k * TOK + (tb + 1) * 128],
                      lambda k=k, oj=oj: w_sb[:, k * D + oj * 512:k * D + (oj + 1) * 512]) for k in range(8)]
            tok = emit_mm_group(c, ps[:, :], parts, g_xfm + wtok, key, last_stop=False)
            tok = c.op("tensor", lambda E, oj=oj, ps=ps: E.matmul(
                ps[:, :], onesr[0:1, 0:128], brow[0:1, 2 * D + oj * 512:2 * D + (oj + 1) * 512],
                start=False, stop=True), inc=c.mm)
            ev = c.op("scalar", lambda E, oj=oj, ps=ps: E.activation(
                out=pout[:, oj * 512:(oj + 1) * 512], in_=ps[:, :], func=AF.Relu),
                wait=[tok] + later(c.war.get("pout")), inc=c.ac)
            c.war[key] = ev
            evs.append(ev)
        d = c.op("sync", lambda E, tb=tb: E.dma_start(
            out=o_v1[tb * 128:(tb + 1) * 128, :], in_=pout[:, 0:D]),
            wait=later(*evs), inc=c.st, dma=True)
        c.war["pout"] = d

    for br in range(9):
        wd = sW1 if br == 0 else rW1[br - 1]
        wtok = emit_load_w(c, wd, w_sb, 512, wait0=c.gate(c.mm))
        dtoks = []
        for oi in range(4):
            evs = []
            for ti in range(4):
                t0 = ti * 512
                parts = [(lambda k=k, oi=oi: w_sb[:, k * 512 + oi * 128:k * 512 + oi * 128 + 128],
                          lambda k=k, t0=t0: asrc[:, k * TOK + t0:k * TOK + t0 + 512]) for k in range(8)]
                tok = emit_mm_group(c, psA[:, :], parts, g_xfm + wtok + t_const + (later(c.war.get("hsbdma")) if (oi == 0 and ti == 0) else []), "psA", last_stop=False)
                tok = c.op("tensor", lambda E, oi=oi, br=br: E.matmul(
                    psA[:, :], hrow[0:1, br * 512 + oi * 128:br * 512 + (oi + 1) * 128], onesr[:, :],
                    start=False, stop=True), inc=c.mm)
                ev = c.op("scalar", lambda E, oi=oi, t0=t0: E.activation(
                    out=hsb[:, oi * TOK + t0:oi * TOK + t0 + 512], in_=psA[:, :], func=AF.Silu),
                    wait=[tok], inc=c.ac)
                c.war["psA"] = ev
                if br > 0:
                    tb_ = c.op("tensor", lambda E, br=br, t0=t0: E.matmul(
                        psC[:, :], sel8[:, (br - 1) * 128:br * 128], ct8[:, t0:t0 + 512],
                        start=True, stop=True), wait=later(ev, c.war.get("psC"), t_cr), inc=c.mm)
                    ev = c.op("vector", lambda E, oi=oi, t0=t0: E.tensor_tensor(
                        out=hsb[:, oi * TOK + t0:oi * TOK + t0 + 512],
                        in0=hsb[:, oi * TOK + t0:oi * TOK + t0 + 512], in1=psC[:, :], op=ALU.mult),
                        wait=[tb_], inc=c.ve)
                    c.war["psC"] = ev
                evs.append(ev)
            dtoks.append(c.op("sync", lambda E, br=br, oi=oi: E.dma_start(
                out=hscr[br, oi * 128:(oi + 1) * 128, :], in_=hsb[:, oi * TOK:(oi + 1) * TOK]),
                wait=later(*evs), inc=c.st, dma=True))
        c.war["hsbdma"] = later(*dtoks)[0]

    g_h = c.gate(c.st, c.mm)
    for br in range(10):
        if br < 9:
            wd = sW2 if br == 0 else rW2[br - 1]
            wtok = emit_load_w(c, wd, w_sb, D, rows=512, wait0=c.gate(c.mm))
        for tb in range(NT):
            if br < 9:
                t_hl = c.op("sync", lambda E, br=br, tb=tb: E.dma_start(
                    out=hld[:, :].rearrange("p (k f) -> p k f", k=4),
                    in_=hscr[br, :, tb * 128:(tb + 1) * 128].rearrange("(k p) t -> p k t", p=128)),
                    wait=(g_h if (br == 0 and tb == 0) else []) + later(c.war.get("hld")), inc=c.ld, dma=True)
            for oj in range(2):
                ps = psA if oj % 2 == 0 else psB
                key = "psA" if oj % 2 == 0 else "psB"
                if br < 9:
                    parts = [(lambda k=k: hld[:, k * 128:(k + 1) * 128],
                              lambda k=k, oj=oj: w_sb[:, k * D + oj * 512:k * D + (oj + 1) * 512]) for k in range(4)]
                    tok = emit_mm_group(c, ps[:, :], parts, later(t_hl) + wtok, key)
                else:
                    tok = emit_mm_group(c, ps[:, :],
                                        [(lambda tb=tb: ct8[:, tb * 128:(tb + 1) * 128],
                                          lambda oj=oj: rb2sb[:, oj * 512:(oj + 1) * 512])],
                                        c.gate(c.ve) if tb == 0 and oj == 0 else [], key, last_stop=False)
                    tok = c.op("tensor", lambda E, oj=oj, ps=ps: E.matmul(
                        ps[:, :], onesr[0:1, 0:128], sb2row[0:1, oj * 512:(oj + 1) * 512],
                        start=False, stop=True), inc=c.mm)
                sl = slice(tb * D + oj * 512, tb * D + (oj + 1) * 512)
                if br == 0:
                    ev = c.op("vector", lambda E, sl=sl, ps=ps: E.tensor_copy(BUF_A[:, sl], ps[:, :]),
                              wait=[tok], inc=c.ve)
                else:
                    ev = c.op("vector", lambda E, sl=sl, ps=ps: E.scalar_tensor_tensor(
                        out=BUF_A[:, sl], in0=ps[:, :], scalar=1.0, in1=BUF_A[:, sl],
                        op0=ALU.mult, op1=ALU.add), wait=[tok], inc=c.ve)
                c.war[key] = ev
            if br < 9:
                c.war["hld"] = ev
    c.op("sync", lambda E: E.dma_start(out=o_dbg[:, 0:32], in_=mv[:, :]), wait=c.gate(c.ve, c.ac), inc=c.st, dma=True)
    c.op("sync", lambda E: E.dma_start(out=o_dbg[:, 32:48], in_=rsdbg[:, :]), inc=c.st, dma=True)
    c.op("sync", lambda E: E.dma_start(out=o_dbg[:, 48:64], in_=cols[:, 0:16]), inc=c.st, dma=True)
    g_y = c.gate(c.ve)
    for i in range(NT):
        t_si = c.op("scalar", lambda E, i=i: E.activation(
            out=BUF_A[:, i * D:(i + 1) * D], in_=BUF_A[:, i * D:(i + 1) * D], func=AF.Silu),
            wait=(g_y if i == 0 else []), inc=c.ac)
        c.op("sync", lambda E, i=i: E.dma_start(
            out=o_g1[i * 128:(i + 1) * 128, :], in_=BUF_A[:, i * D:(i + 1) * D]),
            wait=[t_si], inc=c.st, dma=True)

    c.op("sync", lambda E: E.nop(), wait=c.gate(c.st))
    c.op("gpsimd", lambda E: E.nop(), wait=c.gate(c.st))
    return c


def stage_F_inputs(inputs, resA, resB, resC, resE):
    a0 = assemble_full_fm(resB, "aout2") if False else None
    def asm_bh(res, key):
        out = np.empty((B, H * HD, S), BF)
        for cc in range(NCORE):
            for j, (b, h) in enumerate(core_bh_jobs(cc)):
                out[b, h * HD:(h + 1) * HD, :] = res[cc][key][j]
        return out
    attn0f = asm_bh(resB, "aout")
    attnlf = asm_bh(resC, "aout")
    attngf = np.zeros((G, B, H * HD, S), BF)
    pqf = np.zeros((B, S, G), np.float32)
    for cc in range(NCORE):
        for r in range(8):
            task = SLOTS[r][cc]
            if task is None:
                continue
            b, w, g = task
            blk = resE[cc]["aout"][r].reshape(128, 8, 512).transpose(1, 0, 2).reshape(H * HD, 512)
            attngf[g, b, :, w * W:(w + 1) * W] = blk
            pqf[b, w * W:(w + 1) * W, g] = resE[cc]["pq"][r].T.reshape(W)
    com = stage_F_com(inputs)
    x = np.asarray(inputs["L0_attn_x"], np.float32)
    in_maps = []
    for cc in range(NCORE):
        m = dict(com)
        xs, a0s, als, ag0, ag1 = [], [], [], [], []
        pqs = []
        for (b, k) in core_chunks(cc):
            sl = slice(k * CHUNK, (k + 1) * CHUNK)
            xs.append(x[b, sl, :])
            a0s.append(attn0f[b, :, sl])
            als.append(attnlf[b, :, sl])
            ag0.append(attngf[0, b, :, sl])
            ag1.append(attngf[1, b, :, sl])
            pqs.append(pqf[b, sl, :])
        m["x"] = np.ascontiguousarray(np.concatenate(xs, 0))
        m["attn0"] = np.ascontiguousarray(np.concatenate(a0s, 1))
        m["attnl"] = np.ascontiguousarray(np.concatenate(als, 1))
        m["attng0"] = np.ascontiguousarray(np.concatenate(ag0, 1))
        m["attng1"] = np.ascontiguousarray(np.concatenate(ag1, 1))
        pqo = np.concatenate(pqs, 0)
        m["pq"] = np.ascontiguousarray(pqo.reshape(NT, 128, 2).transpose(1, 2, 0).reshape(128, 2 * NT))
        in_maps.append(m)
    return in_maps


def stage_F_com(inputs):
    wo = np.ascontiguousarray(np.asarray(inputs["L0_attn_wo_w"], np.float32))
    com = dict(
        wo=wo, wob=np.asarray(inputs["L0_attn_wo_b"], np.float32).reshape(1, D),
        wlo=np.ascontiguousarray(np.asarray(inputs["L0_ffn_w_lo"], np.float32).T),
        wog0=np.ascontiguousarray(np.asarray(inputs["L0_ffn_w_out"], np.float32)[0].T),
        wog1=np.ascontiguousarray(np.asarray(inputs["L0_ffn_w_out"], np.float32)[1].T),
        wq1=np.ascontiguousarray(np.asarray(inputs["L1_attn_Wq"], np.float32)),
        wk1=np.ascontiguousarray(np.asarray(inputs["L1_attn_Wk"], np.float32)),
        wv1=np.ascontiguousarray(np.asarray(inputs["L1_attn_Wv"], np.float32)),
        bqkv=np.concatenate([np.asarray(inputs["L1_attn_bq"], np.float32),
                             np.asarray(inputs["L1_attn_bk"], np.float32),
                             np.asarray(inputs["L1_attn_bv"], np.float32)]).reshape(1, 3 * D),
        rwb=np.ascontiguousarray(np.broadcast_to(
            np.asarray(inputs["L1_ffn_routing_W"], np.float32).reshape(1, 8 * D), (128, 8 * D))),
        ebias=np.ascontiguousarray(np.broadcast_to(
            np.asarray(inputs["L1_ffn_expert_bias"], np.float32).reshape(1, 8), (128, 8))),
        sW1=np.ascontiguousarray(np.asarray(inputs["L1_ffn_sW1"], np.float32)),
        sW2=np.ascontiguousarray(np.asarray(inputs["L1_ffn_sW2"], np.float32)),
        hbias=np.concatenate([np.asarray(inputs["L1_ffn_sb1"], np.float32),
                              np.asarray(inputs["L1_ffn_rb1"], np.float32).reshape(8 * 512)]).reshape(1, 9 * 512),
        rb2=np.ascontiguousarray(np.asarray(inputs["L1_ffn_rb2"], np.float32)),
        sb2=np.asarray(inputs["L1_ffn_sb2"], np.float32).reshape(1, D),
        ident=IDEN128,
    )
    for e in range(8):
        com[f"rW1_{e}"] = np.ascontiguousarray(np.asarray(inputs["L1_ffn_rW1"], np.float32)[e])
        com[f"rW2_{e}"] = np.ascontiguousarray(np.asarray(inputs["L1_ffn_rW2"], np.float32)[e])
    sel8 = np.zeros((8, 8 * 128), BF)
    for e in range(8):
        sel8[e, e * 128:(e + 1) * 128] = 1.0
    com["sel8d"] = sel8
    return com


def run_F(inputs, resA, resB, resC, resE):
    nc = get_nc("F", build_F)
    res = run_bass_kernel_spmd(nc, stage_F_inputs(inputs, resA, resB, resC, resE),
                               core_ids=list(range(NCORE)))
    return res.results



def build_G1(c=None, ov=None, vload=None):
    if c is None:
        c = Ctx()
    qj = c.io("qj", [4, HD, S], BF16, ov)
    kj = c.io("kj", [4, HD, S], BF16, ov)
    vj = c.io("vj", [4, S, HD], BF16, ov)
    augj = c.io("augj", [4, 1, S], BF16, ov)
    qmj = c.io("qmj", [4, 1, S], BF16, ov)
    maskc = c.io("maskc", [128, 4 * 512], BF16, ov)
    o_a = c.oo("aout", [4, HD, S], BF16, ov)

    ksb = c.sb("ksb", [65, S], BF16)
    qsb = c.sb("qsb", [65, S], BF16)
    qm_sb = c.sb("qm_sb", [1, S], BF16)
    vsb = c.sb("vsb", [128, 64, 65], BF16)
    msb = c.sb("msb", [128, 4 * 512], BF16)
    p_sb0 = c.sb("p_sb0", [128, 3 * 512], BF16)
    p_sb1 = c.sb("p_sb1", [128, 3 * 512], BF16)
    osb = c.sb("osb", [64, S], BF16)
    ev_f32 = c.sb("ev_f32", [64, 512], F32)
    rrow = c.sb("rrow", [1, 512], F32)
    ones64 = c.sb("ones64", [1, 64], F32)
    sc_ps0 = c.ps("sc_ps0", [128, 3 * 512], F32)
    sc_ps1 = c.ps("sc_ps1", [128, 3 * 512], F32)
    outT_ps = c.ps("outT_ps", [65, 512], F32)
    psR = c.ps("psR", [64, 512], F32)

    t_m = c.op("sync", lambda E: E.dma_start(out=msb[:, :], in_=maskc[:, :]), inc=c.ld, dma=True)
    t_o = c.op("vector", lambda E: E.memset(ones64[:, :], 1.0), inc=c.ve)
    t_q1 = c.op("vector", lambda E: E.memset(qsb[64:65, :], 1.0), inc=c.ve)

    for job in range(4):
        g_j = c.gate(c.mm, c.ve, c.ac)
        t_k = c.op("sync", lambda E, job=job: E.dma_start(out=ksb[0:64, :], in_=kj[job]), wait=g_j, inc=c.ld, dma=True)
        t_ka = c.op("sync", lambda E, job=job: E.dma_start(out=ksb[64:65, :], in_=augj[job]), inc=c.ld, dma=True)
        t_q = c.op("sync", lambda E, job=job: E.dma_start(out=qsb[0:64, :], in_=qj[job]), inc=c.ld, dma=True)
        t_qm = c.op("sync", lambda E, job=job: E.dma_start(out=qm_sb[:, :], in_=qmj[job]), inc=c.ld, dma=True)
        if vload is None:
            t_v = c.op("sync", lambda E, job=job: E.dma_start(
                out=vsb[:, :, 0:64], in_=vj[job].rearrange("(t p) d -> p t d", p=128)), inc=c.ld, dma=True)
        else:
            t_v = c.op("gpsimd", lambda E, job=job: E.dma_start(
                out=vsb[:, :, 0:64], in_=vload(job)), wait=g_j, inc=c.gp, dma=True)
        t_1 = c.op("vector", lambda E: E.memset(vsb[:, :, 64:65], 1.0), wait=g_j + later(t_v), inc=c.ve)
        gate0 = later(t_k, t_ka, t_q, t_qm, t_v, t_1, t_m, t_o, t_q1)
        for qt in range(16):
            Q0 = qt * 512
            n_kv = 4 * qt + 4
            masks = {4 * qt + i: msb[:, i * 512:(i + 1) * 512] for i in range(4)}
            emit_flash_job(
                c,
                ksb=lambda t_: ksb[:, t_ * 128:(t_ + 1) * 128],
                qap=qsb[:, Q0:Q0 + 512],
                vtiles=lambda t_: vsb[:, t_, :],
                n_kv=n_kv,
                out_writer=lambda av_tok, qt=qt: emit_div_fm(
                    c, outT_ps, psR, ones64[:, :], rrow, osb[:, qt * 512:(qt + 1) * 512],
                    ev_f32, av_tok, qmask_row=qm_sb[:, qt * 512:(qt + 1) * 512]),
                st_bufs=([sc_ps0, sc_ps1], [p_sb0, p_sb1], outT_ps),
                masks=masks, gate=gate0)
        g_all = c.gate(c.ve)
        c.op("sync", lambda E, job=job: E.dma_start(out=o_a[job], in_=osb[:, :]),
             wait=g_all, inc=c.st, dma=True)
    c.op("sync", lambda E: E.nop(), wait=c.gate(c.st))
    c.op("gpsimd", lambda E: E.nop(), wait=c.gate(c.st))
    return c


def build_G2(c=None, ov=None):
    if c is None:
        c = Ctx()
    x1 = c.io("x1", [TOK, D], F32, ov)
    xn1 = c.io("xn1", [TOK, D], F32, ov)
    g1 = c.io("g1", [TOK, D], F32, ov)
    attnf = c.io("attnf", [D, TOK], BF16, ov)
    gmb = c.io("gmb", [128, D], F32, ov)
    btb = c.io("btb", [128, D], F32, ov)
    identg = c.io("identg", [128, 128], BF16, ov)
    o_out = c.oo("out", [TOK, D], F32, ov)

    afm = c.sb("afm", [128, 8 * 128], BF16)
    ideng = c.sb("ideng", [128, 128], BF16)
    psTg = c.ps("psTg", [128, 128], BF16)
    a_t = c.sb("a_t", [128, 2 * D], F32)
    b_t = c.sb("b_t", [128, 2 * D], F32)
    c_t = c.sb("c_t", [128, 2 * D], F32)
    gm = c.sb("gm", [128, D], F32)
    bt = c.sb("bt", [128, D], F32)
    stat2 = c.sb("stat2", [128, 12], F32)
    mv = c.sb("mv", [128, 2 * NT], F32)
    cols = c.sb("cols", [128, 32], F32)

    t_g = c.op("sync", lambda E: E.dma_start(out=gm[:, :], in_=gmb[:, :]), inc=c.ld, dma=True)
    t_b = c.op("sync", lambda E: E.dma_start(out=bt[:, :], in_=btb[:, :]), inc=c.ld, dma=True)
    t_id = c.op("sync", lambda E: E.dma_start(out=ideng[:, :], in_=identg[:, :]), inc=c.ld, dma=True)
    prev = [None, None]
    for i in range(NT):
        buf = i % 2
        asl = a_t[:, buf * D:(buf + 1) * D]
        bsl = b_t[:, buf * D:(buf + 1) * D]
        csl = c_t[:, buf * D:(buf + 1) * D]
        tl = []
        for kk in range(8):
            tl.append(c.op("sync", lambda E, i=i, kk=kk: E.dma_start(
                out=afm[:, kk * 128:(kk + 1) * 128],
                in_=attnf[kk * 128:(kk + 1) * 128, i * 128:(i + 1) * 128]),
                wait=later(prev[buf], t_g, t_b, t_id) + later(c.war.get("afm")) if kk == 0 else [],
                inc=c.ld, dma=True))
        t_a = None
        for kk in range(8):
            tmm = c.op("tensor", lambda E, kk=kk: E.transpose(
                psTg[:, :], afm[:, kk * 128:(kk + 1) * 128], ideng[:, :]),
                wait=later(*(tl + [t_a, c.war.get("psTg")])), inc=c.mm)
            t_a = c.op("scalar", lambda E, kk=kk, buf=buf: E.copy(
                a_t[:, buf * D + kk * 128:buf * D + (kk + 1) * 128], psTg[:, :]),
                wait=[tmm], inc=c.ac)
            c.war["psTg"] = t_a
        c.war["afm"] = t_a
        t_x = c.op("sync", lambda E, i=i, buf=buf: E.dma_start(
            out=b_t[:, buf * D:(buf + 1) * D], in_=xn1[i * 128:(i + 1) * 128, :]),
            inc=c.ld, dma=True)
        t_o = c.op("vector", lambda E, asl=asl, bsl=bsl: E.tensor_tensor(
            out=asl, in0=asl, in1=bsl, op=ALU.add), wait=later(t_a, t_x), inc=c.ve)
        c.war["afm"] = t_o
        t_b1 = c.op("vector", lambda E, asl=asl: E.bn_stats(stat2[:, 0:6], asl[:, 0:512]), wait=[t_o], inc=c.ve)
        t_b2 = c.op("vector", lambda E, asl=asl: E.bn_stats(stat2[:, 6:12], asl[:, 512:1024]), inc=c.ve)
        t_ag = c.op("vector", lambda E, i=i: E.bn_aggr(mv[:, 2 * i:2 * i + 2], stat2[:, 0:12]),
                    wait=[t_b1, t_b2], inc=c.ve)
        t_ep = c.op("vector", lambda E, i=i: E.tensor_scalar(
            out=cols[:, (i % 2):(i % 2) + 1], in0=mv[:, 2 * i + 1:2 * i + 2],
            scalar1=1e-8, scalar2=None, op0=ALU.add, op1=ALU.bypass), wait=[t_ag], inc=c.ve)
        t_sq = c.op("scalar", lambda E, i=i: E.activation(
            out=cols[:, 8 + i:9 + i], in_=cols[:, (i % 2):(i % 2) + 1], func=AF.Sqrt),
            wait=[t_ep], inc=c.ac)
        t_rc = c.op("vector", lambda E, i=i: E.reciprocal(cols[:, 2 + (i % 2):3 + (i % 2)], cols[:, 8 + i:9 + i]),
                    wait=[t_sq], inc=c.ve)
        t_n = c.op("vector", lambda E, i=i, asl=asl: E.tensor_scalar(
            out=asl, in0=asl, scalar1=mv[:, 2 * i:2 * i + 1],
            scalar2=cols[:, 2 + (i % 2):3 + (i % 2)], op0=ALU.subtract, op1=ALU.mult),
            wait=[t_rc], inc=c.ve)
        t_s1 = c.op("vector", lambda E, asl=asl: E.tensor_tensor(
            out=asl, in0=asl, in1=gm[:, :], op=ALU.mult), wait=[t_n], inc=c.ve)
        t_s2 = c.op("vector", lambda E, asl=asl: E.tensor_tensor(
            out=asl, in0=asl, in1=bt[:, :], op=ALU.add), wait=[t_s1], inc=c.ve)
        t_gl = c.op("sync", lambda E, i=i, buf=buf: E.dma_start(
            out=c_t[:, buf * D:(buf + 1) * D], in_=g1[i * 128:(i + 1) * 128, :]),
            wait=later(prev[buf]), inc=c.ld, dma=True)
        t_m = c.op("vector", lambda E, asl=asl, csl=csl: E.tensor_tensor(
            out=asl, in0=asl, in1=csl, op=ALU.mult), wait=[t_s2] + later(t_gl), inc=c.ve)
        t_xl = c.op("sync", lambda E, i=i, buf=buf: E.dma_start(
            out=b_t[:, buf * D:(buf + 1) * D], in_=x1[i * 128:(i + 1) * 128, :]),
            inc=c.ld, dma=True)
        t_f = c.op("vector", lambda E, asl=asl, bsl=bsl: E.tensor_tensor(
            out=asl, in0=asl, in1=bsl, op=ALU.add), wait=[t_m] + later(t_xl), inc=c.ve)
        t_st = c.op("sync", lambda E, i=i, buf=buf: E.dma_start(
            out=o_out[i * 128:(i + 1) * 128, :], in_=a_t[:, buf * D:(buf + 1) * D]),
            wait=[t_f], inc=c.st, dma=True)
        prev[buf] = t_st
    c.op("sync", lambda E: E.nop(), wait=c.gate(c.st))
    c.op("gpsimd", lambda E: E.nop(), wait=c.gate(c.st))
    return c


def stage_G1_inputs(resF):
    def asm_fm(key):
        out = np.empty((B, resF[0][key].shape[0], S), resF[0][key].dtype)
        for cc in range(NCORE):
            for ci, (b, k) in enumerate(core_chunks(cc)):
                out[b, :, k * CHUNK:(k + 1) * CHUNK] = resF[cc][key][:, ci * CHUNK:(ci + 1) * CHUNK]
        return out
    def asm_tm(key):
        out = np.empty((B, S, resF[0][key].shape[-1]), resF[0][key].dtype)
        for cc in range(NCORE):
            for ci, (b, k) in enumerate(core_chunks(cc)):
                out[b, k * CHUNK:(k + 1) * CHUNK] = resF[cc][key][ci * CHUNK:(ci + 1) * CHUNK]
        return out
    qf = asm_fm("q1"); kf = asm_fm("k1"); vf = asm_tm("v1")
    augf = asm_fm("k1aug")
    qmf = asm_fm("qmask")
    mkc = np.ascontiguousarray(MASK_C.transpose(1, 0, 2).reshape(128, 4 * 512))
    in_maps = []
    for cc in range(NCORE):
        jobs = core_bh_jobs(cc)
        in_maps.append(dict(
            qj=np.ascontiguousarray(np.stack([qf[b, h * HD:(h + 1) * HD, :] for (b, h) in jobs])),
            kj=np.ascontiguousarray(np.stack([kf[b, h * HD:(h + 1) * HD, :] for (b, h) in jobs])),
            vj=np.ascontiguousarray(np.stack([vf[b, :, h * HD:(h + 1) * HD] for (b, h) in jobs])),
            augj=np.ascontiguousarray(np.stack([augf[b] for (b, h) in jobs])),
            qmj=np.ascontiguousarray(np.stack([qmf[b] for (b, h) in jobs])),
            maskc=mkc))
    return in_maps


def run_G1(resF):
    nc = get_nc("G1", build_G1)
    return run_bass_kernel_spmd(nc, stage_G1_inputs(resF), core_ids=list(range(NCORE))).results



def build_BCE(c=None):
    c = Ctx()
    ovB = {n: c.din("B" + n, sh, dt) for n, sh, dt in [
        ("qj", [4, HD, S], BF16), ("kj", [4, HD, S], BF16),
        ("vj", [4, S, HD], BF16), ("maskc", [128, 4 * 512], BF16)]}
    ovB["aout"] = c.dout("Baout", [4, HD, S], BF16)
    ovC = {n: c.din("C" + n, sh, dt) for n, sh, dt in [
        ("qj", [4, HD, S], BF16), ("kj", [4, HD, S], BF16),
        ("vj", [4, S, HD], BF16), ("maskc", [128, 4 * 512], BF16),
        ("maskp", [128, 4 * 512], BF16)]}
    ovC["aout"] = c.dout("Caout", [4, HD, S], BF16)
    c.push_scope()
    build_B(c, ovB)
    c.fence()
    c.pop_scope()
    c.push_scope()
    build_C(c, ovC)
    c.fence()
    c.pop_scope()
    c.push_scope()
    build_E(c, None)
    c.pop_scope()
    return c


def run_BCE(resA, inputs):
    nc = get_nc("BCE", build_BCE)
    mB = stage_B_inputs(resA)
    mC = stage_C_inputs(resA)
    mE = stage_E_inputs(resA, inputs)
    in_maps = []
    for cc in range(NCORE):
        m = {}
        m.update({"B" + k: v for k, v in mB[cc].items()})
        m.update({"C" + k: v for k, v in mC[cc].items()})
        m.update(mE[cc])
        in_maps.append(m)
    return run_bass_kernel_spmd(nc, in_maps, core_ids=list(range(NCORE))).results



def build_FG(c=None, ovF_extra=None):
    if c is None:
        c = Ctx()
    q1s = c.dtmp("q1s", [H * HD, TOK], BF16)
    k1s = c.dtmp("k1s", [H * HD, TOK], BF16)
    v1s = c.dtmp("v1s", [TOK, D], BF16)
    aqs = c.dtmp("aqs", [2, TOK], BF16)
    x1s = c.dtmp("x1s", [TOK, D], F32)
    xn1s = c.dtmp("xn1s", [TOK, D], F32)
    g1s = c.dtmp("g1s", [TOK, D], F32)
    q1ag = c.dtmp("q1ag", [8, H * HD, TOK], BF16)
    k1ag = c.dtmp("k1ag", [8, H * HD, TOK], BF16)
    v1ag = c.dtmp("v1ag", [8, TOK, D], BF16)
    aqag = c.dtmp("aqag", [8, 2, TOK], BF16)
    qj_i = c.dtmp("qj_i", [4, HD, S], BF16)
    kj_i = c.dtmp("kj_i", [4, HD, S], BF16)
    vj_i = c.dtmp("vj_i", [4, S, HD], BF16)
    aj_i = c.dtmp("aj_i", [4, 1, S], BF16)
    qmj_i = c.dtmp("qmj_i", [4, 1, S], BF16)
    packg = c.dtmp("packg", [4, HD, S], BF16)
    pgag = c.dtmp("pgag", [8, 4, HD, S], BF16)
    attnf_i = c.dtmp("attnf_i", [D, TOK], BF16)
    offs = c.din("offs", [1, 4], I32)
    offs_sb = c.sb("offs_sb", [1, 4], I32)
    t_off = c.op("sync", lambda E: E.dma_start(out=offs_sb[:, :], in_=offs[:, :]),
                 inc=c.ld, dma=True)
    s0, _ = c.dynload(lambda: offs_sb[0:1, 0:1], wait=[t_off], eng="sync")
    s1, _ = c.dynload(lambda: offs_sb[0:1, 1:2], eng="sync")
    h0, _ = c.dynload(lambda: offs_sb[0:1, 0:1], wait=[t_off])
    h1, _ = c.dynload(lambda: offs_sb[0:1, 1:2])
    h2, _ = c.dynload(lambda: offs_sb[0:1, 2:3])

    ovF = dict(x1=x1s, xn1=xn1s, g1=g1s, q1=q1s, k1=k1s, v1=v1s,
               k1aug=aqs[0:1, :], qmask=aqs[1:2, :])
    if ovF_extra:
        ovF.update(ovF_extra)
    c.push_scope()
    build_F(c, ovF)
    c.fence()
    c.pop_scope()

    c.allgather(q1s[:, :], q1ag[:, :, :])
    c.allgather(k1s[:, :], k1ag[:, :, :])
    c.allgather(v1s[:, :], v1ag[:, :, :])
    tcc = c.allgather(aqs[:, :], aqag[:, :, :])
    for b in range(2):
        c.op("sync", lambda E, b=b: E.dma_start(
            out=qj_i[2 * b:2 * b + 2].rearrange("j p (k f) -> j p k f", k=8),
            in_=q1ag[:, ds(s0["v"], 2 * HD), b * CHUNK:(b + 1) * CHUNK].rearrange("k (j p) f -> j p k f", j=2)),
            wait=[tcc], inc=c.ld, dma=True)
        c.op("sync", lambda E, b=b: E.dma_start(
            out=kj_i[2 * b:2 * b + 2].rearrange("j p (k f) -> j p k f", k=8),
            in_=k1ag[:, ds(s0["v"], 2 * HD), b * CHUNK:(b + 1) * CHUNK].rearrange("k (j p) f -> j p k f", j=2)),
            inc=c.ld, dma=True)
        for j in (2 * b, 2 * b + 1):
            if b == 0:
                sv = s0 if j % 2 == 0 else s1
                for k in range(8):
                    c.op("sync", lambda E, j=j, b=b, k=k, sv=sv: E.dma_start(
                        out=vj_i[j, k * CHUNK:(k + 1) * CHUNK, :],
                        in_=v1ag[k, b * CHUNK:(b + 1) * CHUNK, ds(sv["v"], HD)]),
                        wait=[tcc] if k == 0 else [], inc=c.ld, dma=True)
            else:
                hv = h0 if j % 2 == 0 else h1
                for k in range(8):
                    c.op("gpsimd", lambda E, j=j, b=b, k=k, hv=hv: E.dma_start(
                        out=vj_i[j, k * CHUNK:(k + 1) * CHUNK, :],
                        in_=v1ag[k, b * CHUNK:(b + 1) * CHUNK, ds(hv["v"], HD)]),
                        wait=[tcc], inc=c.gp, dma=True)
            c.op("gpsimd", lambda E, j=j, b=b: E.dma_start(
                out=aj_i[j].rearrange("o (k f) -> o k f", k=8),
                in_=aqag[:, 0:1, b * CHUNK:(b + 1) * CHUNK].rearrange("k o f -> o k f")),
                wait=[tcc], inc=c.gp, dma=True)
            c.op("gpsimd", lambda E, j=j, b=b: E.dma_start(
                out=qmj_i[j].rearrange("o (k f) -> o k f", k=8),
                in_=aqag[:, 1:2, b * CHUNK:(b + 1) * CHUNK].rearrange("k o f -> o k f")),
                wait=[tcc], inc=c.gp, dma=True)
    c.fence()

    ovG1 = dict(qj=qj_i, kj=kj_i, vj=vj_i, augj=aj_i, qmj=qmj_i, aout=packg)
    c.push_scope()
    build_G1(c, ovG1)
    c.fence()
    c.pop_scope()

    tg = c.allgather(packg[:, :, :], pgag[:, :, :, :])
    for b in range(B):
        c.op("gpsimd", lambda E, b=b: E.dma_start(
            out=attnf_i[:, b * CHUNK:(b + 1) * CHUNK].rearrange("(a e p) f -> a e p f", a=8, e=2),
            in_=pgag[:, 2 * b:2 * b + 2, :, ds(h2["v"], CHUNK)]),
            wait=[tg], inc=c.gp, dma=True)
    c.fence()

    ovG2 = dict(x1=x1s, xn1=xn1s, g1=g1s, attnf=attnf_i)
    c.push_scope()
    build_G2(c, ovG2)
    c.pop_scope()
    return c



def task_slot(b, w, g):
    i = (NWIN - 1 - w) * 4 + b * 2 + g
    return i // 8, i % 8


def build_M2(c=None):
    c = Ctx()
    ovB = {n: c.din("B" + n, sh, dt) for n, sh, dt in [
        ("qj", [4, HD, S], BF16), ("kj", [4, HD, S], BF16),
        ("vj", [4, S, HD], BF16), ("maskc", [128, 4 * 512], BF16)]}
    ovB["aout"] = c.dtmp("b_aout", [4, HD, S], BF16)
    ovC = {n: c.din("C" + n, sh, dt) for n, sh, dt in [
        ("qj", [4, HD, S], BF16), ("kj", [4, HD, S], BF16),
        ("vj", [4, S, HD], BF16), ("maskc", [128, 4 * 512], BF16),
        ("maskp", [128, 4 * 512], BF16)]}
    ovC["aout"] = c.dtmp("c_aout", [4, HD, S], BF16)
    ovE = dict(aout=c.dtmp("e_aout", [8, 128, 8 * 512], BF16),
               pq=c.dtmp("e_pq", [8, 128, 4], F32),
               dbg_part=c.dtmp("e_dbg", [128, SC_KV], F32))
    bag = c.dtmp("bag", [8, 4, HD, S], BF16)
    cag = c.dtmp("cag", [8, 4, HD, S], BF16)
    eag = c.dtmp("eag", [8 * 8 * 128 + 128, 8 * 512], BF16)
    pqag = c.dtmp("pqag", [8 * 8 * 128 + 128, 4], F32)
    attn0_i = c.dtmp("attn0_i", [H * HD, TOK], BF16)
    attnl_i = c.dtmp("attnl_i", [H * HD, TOK], BF16)
    attng0_i = c.dtmp("attng0_i", [H * HD, TOK], BF16)
    attng1_i = c.dtmp("attng1_i", [H * HD, TOK], BF16)
    pq_i = c.dtmp("f_pq", [128, NT * 2], F32)
    offs2 = c.din("offs2", [1, 12], I32)
    offs2_sb = c.sb("offs2_sb", [1, 12], I32)
    zb = c.sb("zb", [128, 8 * 512], BF16)
    zf = c.sb("zf", [128, 4], F32)

    t_o2 = c.op("sync", lambda E: E.dma_start(out=offs2_sb[:, :], in_=offs2[:, :]),
                inc=c.ld, dma=True)
    a2, _ = c.dynload(lambda: offs2_sb[0:1, 8:9], wait=[t_o2], eng="scalar")
    tvs = [c.dynload(lambda i=i: offs2_sb[0:1, i:i + 1], eng="scalar")[0] for i in range(8)]
    t_z1 = c.op("vector", lambda E: E.memset(zb[:, :], 0.0), inc=c.ve)
    t_z2 = c.op("vector", lambda E: E.memset(zf[:, :], 0.0), inc=c.ve)
    t_zd1 = c.op("sync", lambda E: E.dma_start(out=eag[8192:8320, :], in_=zb[:, :]),
                 wait=[t_z1], inc=c.ld, dma=True)
    t_zd2 = c.op("sync", lambda E: E.dma_start(out=pqag[8192:8320, :], in_=zf[:, :]),
                 wait=[t_z2], inc=c.ld, dma=True)

    c.push_scope(); build_B(c, ovB); c.fence(); c.pop_scope()
    c.push_scope(); build_C(c, ovC); c.fence(); c.pop_scope()
    c.push_scope(); build_E(c, ovE); c.fence(); c.pop_scope()

    c.allgather(ovB["aout"][:, :, :], bag[:, :, :, :])
    c.allgather(ovC["aout"][:, :, :], cag[:, :, :, :])
    c.allgather(ovE["aout"][:, :, :], eag[0:8192, :])
    tcc2 = c.allgather(ovE["pq"][:, :, :], pqag[0:8192, :])
    wz = [tcc2, t_zd1, t_zd2]
    for b in range(B):
        c.op("scalar", lambda E, b=b: E.dma_start(
            out=attn0_i[:, b * CHUNK:(b + 1) * CHUNK].rearrange("(a e p) f -> a e p f", a=8, e=2),
            in_=bag[:, 2 * b:2 * b + 2, :, ds(a2["v"], CHUNK)]), wait=wz, inc=c.ac, dma=True)
        c.op("scalar", lambda E, b=b: E.dma_start(
            out=attnl_i[:, b * CHUNK:(b + 1) * CHUNK].rearrange("(a e p) f -> a e p f", a=8, e=2),
            in_=cag[:, 2 * b:2 * b + 2, :, ds(a2["v"], CHUNK)]), wait=wz, inc=c.ac, dma=True)
    for g in range(G):
        dstg = attng0_i if g == 0 else attng1_i
        for b in range(B):
            for wl in range(2):
                tv = tvs[g * 4 + b * 2 + wl]
                c.op("scalar", lambda E, dstg=dstg, b=b, wl=wl, tv=tv: E.dma_start(
                    out=dstg[:, b * CHUNK + wl * 512:b * CHUNK + (wl + 1) * 512
                             ].rearrange("(ch p) f -> ch p f", ch=8),
                    in_=eag[ds(tv["v"], 128), :].rearrange("p (ch f) -> ch p f", ch=8)),
                    wait=wz, inc=c.ac, dma=True)
                c.op("scalar", lambda E, g=g, b=b, wl=wl, tv=tv: E.dma_start(
                    out=pq_i[:, g * NT + b * 8 + wl * 4:g * NT + b * 8 + wl * 4 + 4],
                    in_=pqag[ds(tv["v"], 128), :]),
                    wait=wz, inc=c.ac, dma=True)
    c.fence()

    build_FG(c, ovF_extra=dict(attn0=attn0_i, attnl=attnl_i, attng0=attng0_i,
                               attng1=attng1_i, pq=pq_i))
    return c


def stage_M2_inputs(inputs, resA):
    mB = stage_B_inputs(resA)
    mC = stage_C_inputs(resA)
    mE = stage_E_inputs(resA, inputs)
    com = stage_F_com(inputs)
    x = np.asarray(inputs["L0_attn_x"], np.float32)
    mkc = np.ascontiguousarray(MASK_C.transpose(1, 0, 2).reshape(128, 4 * 512))
    gmb = np.ascontiguousarray(np.broadcast_to(np.asarray(inputs["L1_attn_gamma"], np.float32)[None, :], (128, D)))
    btb = np.ascontiguousarray(np.broadcast_to(np.asarray(inputs["L1_attn_beta"], np.float32)[None, :], (128, D)))
    in_maps = []
    for cc in range(NCORE):
        m = {}
        m.update({"B" + k: v for k, v in mB[cc].items()})
        m.update({"C" + k: v for k, v in mC[cc].items()})
        m.update(mE[cc])
        m.update(com)
        xs = [x[b, k * CHUNK:(k + 1) * CHUNK, :] for (b, k) in core_chunks(cc)]
        m["x"] = np.ascontiguousarray(np.concatenate(xs, 0))
        m["maskc"] = mkc
        m["gmb"] = gmb
        m["btb"] = btb
        m["identg"] = IDEN128
        m["offs"] = np.array([[128 * cc, 128 * cc + 64, 1024 * cc, 0]], np.int32)
        o2 = np.zeros((1, 12), np.int32)
        for g in range(G):
            for b in range(B):
                for wl in range(2):
                    w = 2 * cc + wl
                    if w == 0:
                        v = 8192
                    else:
                        r, ce = task_slot(b, w, g)
                        v = ce * 1024 + r * 128
                    o2[0, g * 4 + b * 2 + wl] = v
        o2[0, 8] = 1024 * cc
        m["offs2"] = o2
        in_maps.append(m)
    return in_maps


def run_M2(inputs, resA):
    nc = get_nc("M2", build_M2)
    return run_bass_kernel_spmd(nc, stage_M2_inputs(inputs, resA),
                                core_ids=list(range(NCORE))).results


def run_FG(inputs, resA, resB, resC, resE):
    nc = get_nc("FG", build_FG)
    mF = stage_F_inputs(inputs, resA, resB, resC, resE)
    mkc = np.ascontiguousarray(MASK_C.transpose(1, 0, 2).reshape(128, 4 * 512))
    gmb = np.ascontiguousarray(np.broadcast_to(np.asarray(inputs["L1_attn_gamma"], np.float32)[None, :], (128, D)))
    btb = np.ascontiguousarray(np.broadcast_to(np.asarray(inputs["L1_attn_beta"], np.float32)[None, :], (128, D)))
    in_maps = []
    for cc in range(NCORE):
        m = dict(mF[cc])
        m["maskc"] = mkc
        m["gmb"] = gmb
        m["btb"] = btb
        m["identg"] = IDEN128
        m["offs"] = np.array([[128 * cc, 128 * cc + 64, 1024 * cc, 0]], np.int32)
        in_maps.append(m)
    return run_bass_kernel_spmd(nc, in_maps, core_ids=list(range(NCORE))).results


def run_G2(inputs, resF, resG1):
    attnf = np.empty((B, H * HD, S), BF)
    for cc in range(NCORE):
        for j, (b, h) in enumerate(core_bh_jobs(cc)):
            attnf[b, h * HD:(h + 1) * HD, :] = resG1[cc]["aout"][j]
    gmb = np.ascontiguousarray(np.broadcast_to(np.asarray(inputs["L1_attn_gamma"], np.float32)[None, :], (128, D)))
    btb = np.ascontiguousarray(np.broadcast_to(np.asarray(inputs["L1_attn_beta"], np.float32)[None, :], (128, D)))
    in_maps = []
    for cc in range(NCORE):
        at = []
        for (b, k) in core_chunks(cc):
            at.append(attnf[b, :, k * CHUNK:(k + 1) * CHUNK])
        in_maps.append(dict(
            x1=resF[cc]["x1"], xn1=resF[cc]["xn1"], g1=resF[cc]["g1"],
            attnf=np.ascontiguousarray(np.concatenate(at, 1)), gmb=gmb, btb=btb,
            identg=IDEN128))
    nc = get_nc("G2", build_G2)
    return run_bass_kernel_spmd(nc, in_maps, core_ids=list(range(NCORE))).results


# version 07 (final)
# speedup vs baseline: 1.1678x; 1.1678x over previous
import math
from contextlib import ExitStack

import numpy as np
import ml_dtypes

import concourse.bass as bass
import concourse.mybir as mybir
from concourse.bass import ds
from concourse.bass_utils import run_bass_kernel_spmd
from concourse.alu_op_type import AluOpType as ALU
from concourse.engine_type import EngineType

RG8 = [[0, 1, 2, 3, 4, 5, 6, 7]]

F32 = mybir.dt.float32
BF16 = mybir.dt.bfloat16
I32 = mybir.dt.int32
AF = mybir.ActivationFunctionType
BF = ml_dtypes.bfloat16

B, S, D = 2, 8192, 1024
H, HD = 16, 64
W = 512
G = 2
NQ, NKV = 256, 1024
NWIN = S // W
NCORE = 8
CHUNK = S // NCORE
TOK = 2 * CHUNK
NT = TOK // 128
AUGNEG = -524288.0
SHIFT = 15.0

CD_EPS, CD_INIT, CD_DECAY, CD_ITERS = 0.03, 4.0, 0.7, 20
CD_SCHED = [max(CD_INIT * CD_DECAY ** i, CD_EPS) for i in range(CD_ITERS)]
CD_FE = max(CD_INIT * CD_DECAY ** CD_ITERS, CD_EPS)
LOGK_KV = math.log(min(NKV * 9.0 / 8.0, float(S - W)))
LOGK_Q = math.log(min(NQ * 9.0 / 8.0, float(W)))


def core_chunks(c):
    return [(0, c), (1, c)]


def core_bh_jobs(c):
    return [(0, 2 * c), (0, 2 * c + 1), (1, 2 * c), (1, 2 * c + 1)]


def global_slots():
    tasks = []
    for w in range(NWIN - 1, 0, -1):
        for b in range(B):
            for g in range(G):
                tasks.append((b, w, g))
    slots = [[None] * NCORE for _ in range(8)]
    for i, t in enumerate(tasks):
        slots[i // NCORE][i % NCORE] = t
    slot_C = []
    for r in range(8):
        ws = [t[1] for t in slots[r] if t is not None]
        slot_C.append((max(ws) if ws else 1) * (W // 128))
    return slots, slot_C


SLOTS, SLOT_C = global_slots()


def deinterleave_cols(w):
    w = np.asarray(w)
    A = w.reshape(D, H, 32, 2)[:, :, :, 0].reshape(D, H * 32)
    Bh = w.reshape(D, H, 32, 2)[:, :, :, 1].reshape(D, H * 32)
    return np.ascontiguousarray(np.concatenate([A, Bh], axis=1))


def halfsplit_cols(wt, nh=H):
    wt = np.asarray(wt)
    A = wt.reshape(D, nh, 2, 32)[:, :, 0, :].reshape(D, nh * 32)
    Bh = wt.reshape(D, nh, 2, 32)[:, :, 1, :].reshape(D, nh * 32)
    return np.ascontiguousarray(np.concatenate([A, Bh], axis=1))


def rope_tables():
    inv = (1.0 / (10000.0 ** (np.arange(0, HD, 2, dtype=np.float32).astype(np.float32) / np.float32(HD)))).astype(np.float32)
    ang = (np.arange(S, dtype=np.float32)[:, None] * inv[None, :]).astype(np.float32)
    return np.cos(ang).astype(np.float32), np.sin(ang).astype(np.float32)


COS_T, SIN_T = rope_tables()


def own_positions(c):
    pos = []
    for (_, k) in core_chunks(c):
        pos.extend(range(k * CHUNK, (k + 1) * CHUNK))
    return np.array(pos, dtype=np.int64)


def stage_rope_fm(c):
    pos = own_positions(c)
    cos = COS_T[pos][:, np.arange(128) % 32].T
    sin = SIN_T[pos][:, np.arange(128) % 32].T
    return (np.ascontiguousarray(cos.astype(np.float32)),
            np.ascontiguousarray(sin.astype(np.float32)))


def causal_masks():
    p = np.arange(128)[:, None]
    f = np.arange(512)[None, :]
    m = np.stack([(f >= p + 128 * i) for i in range(4)]).astype(BF)
    mp = np.stack([(f <= p + 128 * i) for i in range(4)]).astype(BF)
    return np.ascontiguousarray(m), np.ascontiguousarray(mp)


MASK_C, MASK_P = causal_masks()

IDEN128 = np.eye(128).astype(BF)

ENGS = ("sync", "tensor", "vector", "scalar", "gpsimd")


class Sem:
    __slots__ = ("h", "n")
    def __init__(self, handle):
        self.h = handle
        self.n = 0


class Ctx:

    def __init__(self):
        self.nc = bass.Bass("TRN2", target_bir_lowering=False)
        self.stack = ExitStack()
        self.items = {e: [] for e in ENGS}
        self._si = 0
        self.ld = self.sem("ld")
        self.mm = self.sem("mm")
        self.ve = self.sem("ve")
        self.ac = self.sem("ac")
        self.st = self.sem("st")
        self.gp = self.sem("gp")
        self.cc = self.sem("cc")
        self.war = {}
        self.cvals = self.sb("cvals", [128, 32], F32)
        self.cmap = {}
        self.ctok = None

    def sem(self, name=None):
        h = self.stack.enter_context(self.nc.semaphore(name or f"s{self._si}"))
        self._si += 1
        return Sem(h)

    def sb(self, name, shape, dt):
        st = self.scopes[-1] if getattr(self, "scopes", None) else self.stack
        name = f"{name}_{self._si}" if getattr(self, "scopes", None) and len(self.scopes) > 0 else name
        self._si += 1
        return st.enter_context(self.nc.sbuf_tensor(name, shape, dt))

    def ps(self, name, shape, dt=F32):
        st = self.scopes[-1] if getattr(self, "scopes", None) else self.stack
        name = f"{name}_{self._si}" if getattr(self, "scopes", None) and len(self.scopes) > 0 else name
        self._si += 1
        return st.enter_context(self.nc.psum_tensor(name, shape, dt))

    def push_scope(self):
        if not hasattr(self, "scopes"):
            self.scopes = []
        self.scopes.append(ExitStack())

    def pop_scope(self):
        self.scopes.pop().close()

    def fence(self):
        g = self.gate(self.ld, self.mm, self.ve, self.ac, self.st, self.gp)
        if hasattr(self, "cc"):
            g += self.gate(self.cc)
        for e in ("sync", "tensor", "vector", "scalar", "gpsimd"):
            self.op(e, lambda E: E.nop(), wait=g)
        self.war = {}

    def io(self, name, shape, dt, ov=None):
        if ov and name in ov:
            return ov[name]
        return self.din(name, shape, dt)

    def oo(self, name, shape, dt, ov=None):
        if ov and name in ov:
            return ov[name]
        return self.dout(name, shape, dt)

    def din(self, name, shape, dt):
        return self.nc.dram_tensor(name, list(shape), dt, kind="ExternalInput").ap()

    def dout(self, name, shape, dt):
        return self.nc.dram_tensor(name, list(shape), dt, kind="ExternalOutput").ap()

    def dtmp(self, name, shape, dt, shared=False):
        kw = {"addr_space": "Shared"} if shared else {}
        return self.nc.dram_tensor(name, list(shape), dt, **kw).ap()

    def gate(self, *sems):
        return [(s, s.n) for s in sems]

    def allgather(self, src_ap, dst_ap, wait=()):
        return self.op("gpsimd", lambda E: E.collective_compute(
            "AllGather", ALU.bypass, replica_groups=RG8,
            ins=[src_ap], outs=[dst_ap]), wait=wait, inc=self.cc)

    def dynload(self, ap_fn, wait=(), eng="gpsimd"):
        et = {"gpsimd": EngineType.Pool, "sync": EngineType.SP,
              "scalar": EngineType.Activation}[eng]
        holder = {}
        def f(E):
            holder["v"] = self.nc.values_load(ap_fn(), engines=[et])
            return None
        tok = self.op(eng, f, wait=wait)
        return holder, tok

    def ccol(self, v):
        v = float(v)
        if v not in self.cmap:
            idx = len(self.cmap)
            assert idx < 32
            self.cmap[v] = idx
            self.ctok = self.op("vector", lambda E, idx=idx, v=v: E.memset(
                self.cvals[:, idx:idx + 1], v), inc=self.ve)
        return self.cvals[:, self.cmap[v]:self.cmap[v] + 1]

    def op(self, eng, fn, wait=(), inc=None, dma=False):
        waits = [(s.h, n) for (s, n) in wait if n > 0] if wait else []
        amt = 16 if dma else 1
        tok = None
        incs = []
        if inc is not None:
            inc.n += amt
            tok = (inc, inc.n)
            incs = [(inc.h, amt)]

        def g(E, fn=fn, waits=waits, incs=incs):
            for (h, n) in waits:
                E.wait_ge(h, n)
            r = fn(E)
            for (h, a) in incs:
                r.then_inc(h, a)

        self.items[eng].append(g)
        return tok

    def finalize(self):
        with self.nc.Block() as blk:
            for e in ENGS:
                if self.items[e]:
                    def runner(E, items=self.items[e]):
                        for f in items:
                            f(E)
                    getattr(blk, e)(runner)
        return self.nc


def later(*toks):
    out = {}
    for t in toks:
        if t is None:
            continue
        s, n = t
        out[s] = max(out.get(s, 0), n)
    return [(s, n) for s, n in out.items()]


def cdiv(a, b):
    return (a + b - 1) // b



def emit_load_w(c, wdram, w_sb, OD, rows=D, eng="gpsimd", wait0=()):
    toks = []
    nk = rows // 128
    sem = c.gp if eng == "gpsimd" else c.ld
    for k in range(nk):
        toks.append(c.op(eng, lambda E, k=k: E.dma_start(
            out=w_sb[:, k * OD:(k + 1) * OD], in_=wdram[k * 128:(k + 1) * 128, :]),
            wait=(wait0 if k == 0 else []), inc=sem, dma=True))
    return later(*toks)


def emit_mm_group(c, ps_ap, parts, wait0, war_key, last_stop=True):
    n = len(parts)
    tok = None
    warw = later(c.war.get(war_key)) if war_key in c.war else []
    for i, (lf, rf) in enumerate(parts):
        wv = (list(wait0) + warw) if i == 0 else []
        tok = c.op("tensor", lambda E, lf=lf, rf=rf, i=i: E.matmul(
            ps_ap, lf(), rf(), start=(i == 0), stop=(i == n - 1 and last_stop)),
            wait=wv, inc=(c.mm if i == n - 1 else None))
    return tok



def build_A(c=None, ov=None):
    if c is None:
        c = Ctx()
    x = c.io("x", [TOK, D], F32, ov)
    wq_de = c.io("wq_de", [D, D], F32, ov)
    wk_de = c.io("wk_de", [D, D], F32, ov)
    wv = c.io("wv", [D, D], F32, ov)
    wqkv_qk = c.io("wqkv_qk", [D, 2 * D], F32, ov)
    wqkv_v = c.io("wqkv_v", [D, D], F32, ov)
    wkvt0 = c.io("wkvt0", [D, 2 * D], F32, ov)
    wkvt1 = c.io("wkvt1", [D, 2 * D], F32, ov)
    wqgt0 = c.io("wqgt0", [D, D], F32, ov)
    wqgt1 = c.io("wqgt1", [D, D], F32, ov)
    gains = c.io("gains", [128, 40], F32, ov)
    lnb_cols = c.io("lnb_cols", [128, 8], F32, ov)
    rts = c.io("rts", [128, 4 * D], F32, ov)
    cosf = c.io("cosf", [128, TOK], F32, ov)
    sinf = c.io("sinf", [128, TOK], F32, ov)
    ident = c.io("ident", [128, 128], BF16, ov)
    o_q0 = c.oo("q0", [H * HD, TOK], BF16, ov)
    o_k0 = c.oo("k0", [H * HD, TOK], BF16, ov)
    o_v0 = c.oo("v0", [TOK, D], BF16, ov)
    o_ql = c.oo("qloc", [H * HD, TOK], BF16, ov)
    o_kl = c.oo("kloc", [H * HD, TOK], BF16, ov)
    o_vl = c.oo("vloc", [TOK, D], BF16, ov)
    o_kvk0 = c.oo("kvk0", [D, TOK], BF16, ov)
    o_kvk1 = c.oo("kvk1", [D, TOK], BF16, ov)
    o_kvv0 = c.oo("kvv0", [TOK, D], BF16, ov)
    o_kvv1 = c.oo("kvv1", [TOK, D], BF16, ov)
    o_scolg = c.oo("scolg", [128, 4 * NT + 8], F32, ov)
    o_xq0 = c.oo("xq0", [H * HD, TOK], BF16, ov)
    o_xq1 = c.oo("xq1", [H * HD, TOK], BF16, ov)

    x_t = c.sb("x_t", [128, 2 * D], F32)
    xn_t = c.sb("xn_t", [128, NT * D], F32)
    tm_bf = c.sb("tm_bf", [128, 2 * D], BF16)
    stat2 = c.sb("stat2", [128, NT * 12], F32)
    mv = c.sb("mv", [128, NT * 2], F32)
    cB = c.sb("cB", [128, 8 * NT], F32)
    scol = c.sb("scol", [128, 4 * NT + 8], F32)
    rts_sb = c.sb("rts_sb", [128, 4 * D], F32)
    trash = c.sb("trash", [128, D], F32)
    trash2 = c.sb("trash2", [128, D], F32)
    iden = c.sb("iden", [128, 128], BF16)
    xfm = c.sb("xfm", [128, 8 * TOK], BF16)
    w_sb = c.sb("w_sb", [128, 8 * 2 * D], BF16)
    gain = c.sb("gain", [128, 40], F32)
    lnbc = c.sb("lnbc", [128, 8], BF16)
    brow = c.sb("brow", [1, 2 * D], BF16)
    onesr = c.sb("onesr", [1, 512], BF16)
    ra = c.sb("ra", [128, 512], F32)
    rb = c.sb("rb", [128, 512], F32)
    rc = c.sb("rc", [128, 512], F32)
    rd = c.sb("rd", [128, 512], F32)
    pout = c.sb("pout", [128, 2 * TOK], BF16)
    cos_sb = c.sb("cos_sb", [128, TOK], F32)
    sin_sb = c.sb("sin_sb", [128, TOK], F32)

    psA = c.ps("psA", [128, 512], F32)
    psB = c.ps("psB", [128, 512], F32)
    psb_ = c.ps("psb_", [1, 512], F32)
    psT = c.ps("psT", [128, 128], BF16)

    tks = [
        c.op("sync", lambda E: E.dma_start(out=rts_sb[:, :], in_=rts[:, :]), inc=c.ld, dma=True),
        c.op("sync", lambda E: E.dma_start(out=cos_sb[:, :], in_=cosf[:, :]), inc=c.ld, dma=True),
        c.op("sync", lambda E: E.dma_start(out=sin_sb[:, :], in_=sinf[:, :]), inc=c.ld, dma=True),
        c.op("sync", lambda E: E.dma_start(out=gain[:, :], in_=gains[:, :]), inc=c.ld, dma=True),
        c.op("gpsimd", lambda E: E.dma_start(out=lnbc[:, :], in_=lnb_cols[:, :]), inc=c.gp, dma=True),
        c.op("sync", lambda E: E.dma_start(out=iden[:, :], in_=ident[:, :]), inc=c.ld, dma=True),
        c.op("vector", lambda E: E.memset(onesr[:, :], 1.0), inc=c.ve),
    ]
    t_const = later(*tks)

    xn_done = [None] * NT
    xnl_ready = [None] * NT
    xhat_ready = [None] * NT
    prev_tile = [None, None]
    for i in range(NT):
        buf = i % 2
        src = x_t[:, buf * D:(buf + 1) * D]
        tok_x = c.op("sync", lambda E, i=i, buf=buf: E.dma_start(
            out=x_t[:, buf * D:(buf + 1) * D], in_=x[i * 128:(i + 1) * 128, :]),
            wait=later(prev_tile[buf]), inc=c.ld, dma=True)
        c.op("vector", lambda E, i=i, src=src: E.bn_stats(stat2[:, i * 12:i * 12 + 6], src[:, 0:512]),
             wait=[tok_x], inc=c.ve)
        t_bs = c.op("vector", lambda E, i=i, src=src: E.bn_stats(stat2[:, i * 12 + 6:i * 12 + 12], src[:, 512:1024]), inc=c.ve)
        t_ag = c.op("vector", lambda E, i=i: E.bn_aggr(mv[:, i * 2:i * 2 + 2], stat2[:, i * 12:i * 12 + 12]), wait=[t_bs], inc=c.ve)
        eps5 = c.ccol(1e-5)
        t_sq = c.op("scalar", lambda E, i=i, eps5=eps5: E.activation(out=cB[:, 4 * NT + i:4 * NT + i + 1],
                                                          in_=mv[:, i * 2 + 1:i * 2 + 2],
                                                          func=AF.Sqrt, bias=eps5, scale=1.0),
                    wait=[t_ag] + later(c.ctok), inc=c.ac)
        t_rc = c.op("vector", lambda E, i=i: E.reciprocal(cB[:, i:i + 1], cB[:, 4 * NT + i:4 * NT + i + 1]),
                    wait=[t_sq], inc=c.ve)
        t_xn = c.op("vector", lambda E, i=i, src=src: E.tensor_scalar(
            out=xn_t[:, i * D:(i + 1) * D], in0=src, scalar1=mv[:, i * 2:i * 2 + 1],
            scalar2=cB[:, i:i + 1], op0=ALU.subtract, op1=ALU.mult), wait=[t_rc], inc=c.ve)
        xn_done[i] = t_xn
        prev_tile[buf] = t_xn

    for i in range(NT):
        tok = None
        for j in range(4):
            tok = c.op("vector", lambda E, i=i, j=j: E.scalar_tensor_tensor(
                out=trash[:, :], in0=xn_t[:, i * D:(i + 1) * D], scalar=1.0,
                in1=rts_sb[:, j * D:(j + 1) * D], op0=ALU.mult, op1=ALU.mult,
                accum_out=scol[:, j * NT + i:j * NT + i + 1]),
                wait=(later(xn_done[i]) + t_const) if j == 0 else [], inc=c.ve)

    t_n1 = c.op("vector", lambda E: E.memset(scol[:, 4 * NT:4 * NT + 4], -1e30), inc=c.ve)
    t_n2 = c.op("vector", lambda E: E.memset(scol[:, 4 * NT + 4:4 * NT + 8], 0.0), inc=c.ve)
    c.op("sync", lambda E: E.dma_start(out=o_scolg[:, :], in_=scol[:, :]),
         wait=c.gate(c.ve), inc=c.st, dma=True)

    for i in range(NT):
        c.op("vector", lambda E, i=i: E.bn_stats(stat2[:, i * 12:i * 12 + 6], xn_t[:, i * D:i * D + 512]),
             wait=later(xn_done[i]), inc=c.ve)
        t_bs = c.op("vector", lambda E, i=i: E.bn_stats(stat2[:, i * 12 + 6:i * 12 + 12], xn_t[:, i * D + 512:(i + 1) * D]), inc=c.ve)
        t_ag = c.op("vector", lambda E, i=i: E.bn_aggr(mv[:, i * 2:i * 2 + 2], stat2[:, i * 12:i * 12 + 12]), wait=[t_bs], inc=c.ve)
        eps5 = c.ccol(1e-5)
        t_sq = c.op("scalar", lambda E, i=i, eps5=eps5: E.activation(out=cB[:, 4 * NT + i:4 * NT + i + 1],
                                                          in_=mv[:, i * 2 + 1:i * 2 + 2],
                                                          func=AF.Sqrt, bias=eps5, scale=1.0),
                    wait=[t_ag] + later(c.ctok), inc=c.ac)
        t_rc = c.op("vector", lambda E, i=i: E.reciprocal(cB[:, 2 * NT + i:2 * NT + i + 1],
                                                          cB[:, 4 * NT + i:4 * NT + i + 1]),
                    wait=[t_sq], inc=c.ve)
        t_mu = c.op("vector", lambda E, i=i: E.tensor_copy(cB[:, NT + i:NT + i + 1], mv[:, i * 2:i * 2 + 1]),
                    wait=[t_ag], inc=c.ve)
        xnl_ready[i] = later(t_rc, t_mu)
        t_ss = c.op("scalar", lambda E, i=i: E.activation(out=trash2[:, :], in_=xn_t[:, i * D:(i + 1) * D],
                                                          func=AF.Square, accum_out=cB[:, 5 * NT + i:5 * NT + i + 1]),
                    wait=later(xn_done[i]), inc=c.ac)
        t_nr = c.op("scalar", lambda E, i=i: E.activation(out=cB[:, 6 * NT + i:6 * NT + i + 1],
                                                          in_=cB[:, 5 * NT + i:5 * NT + i + 1],
                                                          func=AF.Sqrt),
                    wait=[t_ss], inc=c.ac)
        t_mx = c.op("vector", lambda E, i=i: E.tensor_scalar(out=cB[:, 7 * NT + i:7 * NT + i + 1],
                                                             in0=cB[:, 6 * NT + i:6 * NT + i + 1],
                                                             scalar1=1e-12, scalar2=None,
                                                             op0=ALU.max, op1=ALU.bypass),
                    wait=[t_nr], inc=c.ve)
        t_rn = c.op("vector", lambda E, i=i: E.reciprocal(cB[:, 3 * NT + i:3 * NT + i + 1],
                                                          cB[:, 7 * NT + i:7 * NT + i + 1]),
                    wait=[t_mx], inc=c.ve)
        xhat_ready[i] = later(t_rn)

    def src_to_fm(mk_tile):
        prev = [None, None]
        last = None
        for i in range(NT):
            buf = i % 2
            t_c = mk_tile(i, buf, later(prev[buf]))
            for k in range(8):
                t_mm = c.op("tensor", lambda E, k=k, buf=buf: E.transpose(
                    psT[:, :], tm_bf[:, buf * D + k * 128:buf * D + (k + 1) * 128], iden[:, :]),
                    wait=later(t_c, last), inc=c.mm)
                last = c.op("scalar", lambda E, i=i, k=k: E.copy(
                    xfm[:, k * TOK + i * 128:k * TOK + (i + 1) * 128], psT[:, :]),
                    wait=[t_mm], inc=c.ac)
            prev[buf] = last
        return later(last)

    xch = [xfm[:, k * TOK:(k + 1) * TOK] for k in range(8)]

    def emit_bias_row(OD, wtok):
        toks = []
        for j in range(OD // 512):
            tok = emit_mm_group(
                c, psb_[:, :],
                [(lambda k=k: lnbc[:, k:k + 1],
                  lambda k=k, j=j: w_sb[:, k * OD + j * 512:k * OD + (j + 1) * 512]) for k in range(8)],
                wtok + t_const, "psb_")
            tv = c.op("vector", lambda E, j=j: E.tensor_copy(brow[:, j * 512:(j + 1) * 512], psb_[:, :]),
                      wait=[tok], inc=c.ve)
            c.war["psb_"] = tv
            toks.append(tv)
        return later(*toks)

    def fold_gain(OD, gslice, scale, wtok):
        toks = []
        for k in range(8):
            toks.append(c.op("vector", lambda E, k=k: E.tensor_scalar(
                out=w_sb[:, k * OD:(k + 1) * OD], in0=w_sb[:, k * OD:(k + 1) * OD],
                scalar1=gain[:, gslice + k:gslice + k + 1], scalar2=float(scale),
                op0=ALU.mult, op1=ALU.mult),
                wait=(wtok + t_const) if k == 0 else [], inc=c.ve))
        return later(*toks)

    def proj_rope_store(OD, chA, chB, out_dram, gate, bias):
        done = None
        for j in range(4):
            rope_toks = []
            for ti in range(4):
                t0 = ti * 512
                partsA = [(lambda k=k, oi=chA + j: w_sb[:, k * OD + oi * 128:k * OD + oi * 128 + 128],
                           lambda k=k, t0=t0: xch[k][:, t0:t0 + 512]) for k in range(8)]
                partsB = [(lambda k=k, oi=chB + j: w_sb[:, k * OD + oi * 128:k * OD + oi * 128 + 128],
                           lambda k=k, t0=t0: xch[k][:, t0:t0 + 512]) for k in range(8)]
                tA = emit_mm_group(c, psA[:, :], partsA, gate, "psA", last_stop=not bias)
                if bias:
                    tA = c.op("tensor", lambda E, j=j: E.matmul(
                        psA[:, :], brow[:, (chA + j) * 128:(chA + j + 1) * 128], onesr[:, :],
                        start=False, stop=True), inc=c.mm)
                tB = emit_mm_group(c, psB[:, :], partsB, gate, "psB", last_stop=not bias)
                if bias:
                    tB = c.op("tensor", lambda E, j=j: E.matmul(
                        psB[:, :], brow[:, (chB + j) * 128:(chB + j + 1) * 128], onesr[:, :],
                        start=False, stop=True), inc=c.mm)
                evA = c.op("vector", lambda E: E.tensor_copy(ra[:, :], psA[:, :]), wait=[tA], inc=c.ve)
                evB = c.op("vector", lambda E: E.tensor_copy(rb[:, :], psB[:, :]), wait=[tB], inc=c.ve)
                c.war["psA"] = evA
                c.war["psB"] = evB
                wpout = later(c.war.get("pout"))
                t1 = c.op("vector", lambda E, t0=t0: E.tensor_tensor(out=rc[:, :], in0=ra[:, :], in1=cos_sb[:, t0:t0 + 512], op=ALU.mult), wait=later(evA), inc=c.ve)
                t2 = c.op("vector", lambda E, t0=t0: E.tensor_tensor(out=rd[:, :], in0=rb[:, :], in1=sin_sb[:, t0:t0 + 512], op=ALU.mult), wait=later(evB), inc=c.ve)
                t3 = c.op("vector", lambda E, t0=t0: E.tensor_tensor(out=pout[:, t0:t0 + 512], in0=rc[:, :], in1=rd[:, :], op=ALU.subtract), wait=wpout, inc=c.ve)
                t4 = c.op("vector", lambda E, t0=t0: E.tensor_tensor(out=rc[:, :], in0=ra[:, :], in1=sin_sb[:, t0:t0 + 512], op=ALU.mult), inc=c.ve)
                t5 = c.op("vector", lambda E, t0=t0: E.tensor_tensor(out=rd[:, :], in0=rb[:, :], in1=cos_sb[:, t0:t0 + 512], op=ALU.mult), inc=c.ve)
                t6 = c.op("vector", lambda E, t0=t0: E.tensor_tensor(out=pout[:, TOK + t0:TOK + t0 + 512], in0=rc[:, :], in1=rd[:, :], op=ALU.add), inc=c.ve)
                rope_toks.append(t6)
            stoks = []
            for m in range(4):
                hh = 4 * j + m
                stoks.append(c.op("sync", lambda E, hh=hh, m=m: E.dma_start(
                    out=out_dram[hh * HD:hh * HD + 32, :],
                    in_=pout[32 * m:32 * m + 32, 0:TOK]),
                    wait=later(*rope_toks), inc=c.st, dma=True))
                stoks.append(c.op("sync", lambda E, hh=hh, m=m: E.dma_start(
                    out=out_dram[hh * HD + 32:hh * HD + 64, :],
                    in_=pout[32 * m:32 * m + 32, TOK:2 * TOK]),
                    wait=later(*rope_toks), inc=c.st, dma=True))
            done = later(*stoks)
            c.war["pout"] = done[0] if done else None
        return done

    def proj_fm_store(OD, out_dram, gate):
        for oi in range(OD // 128):
            evs = []
            for ti in range(4):
                t0 = ti * 512
                parts = [(lambda k=k, oi=oi: w_sb[:, k * OD + oi * 128:k * OD + oi * 128 + 128],
                          lambda k=k, t0=t0: xch[k][:, t0:t0 + 512]) for k in range(8)]
                tok = emit_mm_group(c, psA[:, :], parts, gate, "psA")
                ev = c.op("vector", lambda E, t0=t0: E.tensor_copy(pout[:, t0:t0 + 512], psA[:, :]),
                          wait=[tok] + later(c.war.get("pout")), inc=c.ve)
                c.war["psA"] = ev
                evs.append(ev)
            tok = c.op("sync", lambda E, oi=oi: E.dma_start(
                out=out_dram[oi * 128:(oi + 1) * 128, :], in_=pout[:, 0:TOK]),
                wait=later(*evs), inc=c.st, dma=True)
            c.war["pout"] = tok
        return

    def proj_tm_store(OD, out_dram, gate, bias):
        for tb in range(NT):
            evs = []
            for oj in range(OD // 512):
                ps = psA if oj % 2 == 0 else psB
                key = "psA" if oj % 2 == 0 else "psB"
                parts = [(lambda k=k, tb=tb: xch[k][:, tb * 128:(tb + 1) * 128],
                          lambda k=k, oj=oj: w_sb[:, k * OD + oj * 512:k * OD + (oj + 1) * 512]) for k in range(8)]
                tok = emit_mm_group(c, ps[:, :], parts, gate, key, last_stop=not bias)
                if bias:
                    tok = c.op("tensor", lambda E, oj=oj, ps=ps: E.matmul(
                        ps[:, :], onesr[0:1, 0:128], brow[:, oj * 512:(oj + 1) * 512],
                        start=False, stop=True), inc=c.mm)
                ev = c.op("vector", lambda E, oj=oj, ps=ps: E.tensor_copy(
                    pout[:, oj * 512:(oj + 1) * 512], ps[:, :]),
                    wait=[tok] + later(c.war.get("pout")),
                    inc=c.ve)
                c.war[key] = ev
                evs.append(ev)
            tok = c.op("sync", lambda E, tb=tb: E.dma_start(
                out=out_dram[tb * 128:(tb + 1) * 128, :], in_=pout[:, 0:OD]),
                wait=later(*evs), inc=c.st, dma=True)
            c.war["pout"] = tok
        return

    def mk_xn(i, buf, we):
        return c.op("vector", lambda E, i=i, buf=buf: E.tensor_copy(
            tm_bf[:, buf * D:(buf + 1) * D], xn_t[:, i * D:(i + 1) * D]),
            wait=later(xn_done[i]) + we, inc=c.ve)

    gfm = src_to_fm(mk_xn) + c.gate(c.ac)

    wtok = emit_load_w(c, wq_de, w_sb, D)
    proj_rope_store(D, 0, 4, o_q0, gfm + wtok, bias=False)
    g_w = c.gate(c.mm)
    wtok = emit_load_w(c, wk_de, w_sb, D, wait0=g_w)
    proj_rope_store(D, 0, 4, o_k0, gfm + wtok, bias=False)
    g_w = c.gate(c.mm)
    wtok = emit_load_w(c, wv, w_sb, D, wait0=g_w)
    proj_tm_store(D, o_v0, gfm + wtok, bias=False)

    g_x = c.gate(c.mm)

    def mk_xnl(i, buf, we):
        return c.op("vector", lambda E, i=i, buf=buf: E.tensor_scalar(
            out=tm_bf[:, buf * D:(buf + 1) * D], in0=xn_t[:, i * D:(i + 1) * D],
            scalar1=cB[:, NT + i:NT + i + 1], scalar2=cB[:, 2 * NT + i:2 * NT + i + 1],
            op0=ALU.subtract, op1=ALU.mult),
            wait=xnl_ready[i] + we + (g_x if i == 0 else []), inc=c.ve)

    gfm = src_to_fm(mk_xnl) + c.gate(c.ac)

    g_w = c.gate(c.mm)
    wtok = emit_load_w(c, wqkv_qk, w_sb, 2 * D, wait0=g_w)
    btok = emit_bias_row(2 * D, wtok)
    ftok = fold_gain(2 * D, 0, 1.0, wtok + btok)
    proj_rope_store(2 * D, 0, 4, o_ql, gfm + ftok, bias=True)
    proj_rope_store(2 * D, 8, 12, o_kl, gfm + ftok, bias=True)

    g_w = c.gate(c.mm)
    wtok = emit_load_w(c, wqkv_v, w_sb, D, wait0=g_w)
    btok = emit_bias_row(D, wtok)
    ftok = fold_gain(D, 0, 1.0, wtok + btok)
    proj_tm_store(D, o_vl, gfm + ftok, bias=True)

    g_x = c.gate(c.mm)

    def mk_xhat(i, buf, we):
        return c.op("vector", lambda E, i=i, buf=buf: E.tensor_scalar(
            out=tm_bf[:, buf * D:(buf + 1) * D], in0=xn_t[:, i * D:(i + 1) * D],
            scalar1=cB[:, 3 * NT + i:3 * NT + i + 1], scalar2=None,
            op0=ALU.mult, op1=ALU.bypass),
            wait=xhat_ready[i] + we + (g_x if i == 0 else []), inc=c.ve)

    gfm = src_to_fm(mk_xhat) + c.gate(c.ac)

    for g, (wk_, otk, otv) in enumerate([(wkvt0, o_kvk0, o_kvv0), (wkvt1, o_kvk1, o_kvv1)]):
        g_w = c.gate(c.mm)
        wtok = emit_load_w(c, wk_, w_sb, 2 * D, wait0=g_w)
        ftok = fold_gain(2 * D, 8 + 8 * g, 32.0, wtok)
        for oi in range(8):
            evs = []
            for ti in range(4):
                t0 = ti * 512
                parts = [(lambda k=k, oi=oi: w_sb[:, k * 2 * D + oi * 128:k * 2 * D + oi * 128 + 128],
                          lambda k=k, t0=t0: xch[k][:, t0:t0 + 512]) for k in range(8)]
                tok = emit_mm_group(c, psA[:, :], parts, gfm + ftok, "psA")
                ev = c.op("vector", lambda E, t0=t0: E.tensor_copy(pout[:, t0:t0 + 512], psA[:, :]),
                          wait=[tok] + later(c.war.get("pout")), inc=c.ve)
                c.war["psA"] = ev
                evs.append(ev)
            dk = c.op("sync", lambda E, oi=oi, otk=otk: E.dma_start(
                out=otk[oi * 128:(oi + 1) * 128, :], in_=pout[:, 0:TOK]),
                wait=later(*evs), inc=c.st, dma=True)
            c.war["pout"] = dk
        for tb in range(NT):
            evs = []
            for oj in range(2):
                ps = psA if oj % 2 == 0 else psB
                key = "psA" if oj % 2 == 0 else "psB"
                parts = [(lambda k=k, tb=tb: xch[k][:, tb * 128:(tb + 1) * 128],
                          lambda k=k, oj=oj: w_sb[:, k * 2 * D + D + oj * 512:k * 2 * D + D + (oj + 1) * 512]) for k in range(8)]
                tok = emit_mm_group(c, ps[:, :], parts, gfm + ftok, key)
                ev = c.op("vector", lambda E, oj=oj, ps=ps: E.tensor_copy(
                    pout[:, oj * 512:(oj + 1) * 512], ps[:, :]),
                    wait=[tok] + later(c.war.get("pout")), inc=c.ve)
                c.war[key] = ev
                evs.append(ev)
            dv = c.op("sync", lambda E, tb=tb, otv=otv: E.dma_start(
                out=otv[tb * 128:(tb + 1) * 128, :], in_=pout[:, 0:D]),
                wait=later(*evs), inc=c.st, dma=True)
            c.war["pout"] = dv
    for g, (wk_, ot) in enumerate([(wqgt0, o_xq0), (wqgt1, o_xq1)]):
        g_w = c.gate(c.mm)
        wtok = emit_load_w(c, wk_, w_sb, D, wait0=g_w)
        ftok = fold_gain(D, 24 + 8 * g, 32.0, wtok)
        proj_fm_store(D, ot, gfm + ftok)

    c.op("sync", lambda E: E.nop(), wait=c.gate(c.st))
    c.op("gpsimd", lambda E: E.nop(), wait=c.gate(c.st, c.gp))
    return c


def stage_A_inputs(inputs):
    wqkvT = np.ascontiguousarray(np.asarray(inputs["L0_ffn_w_qkv"]).T.astype(np.float32))
    qA_B = halfsplit_cols(wqkvT[:, 0:D])
    kA_B = halfsplit_cols(wqkvT[:, D:2 * D])
    com = dict(
        wq_de=deinterleave_cols(np.asarray(inputs["L0_attn_wq"]).astype(np.float32)),
        wk_de=deinterleave_cols(np.asarray(inputs["L0_attn_wk"]).astype(np.float32)),
        wv=np.ascontiguousarray(np.asarray(inputs["L0_attn_wv"]).astype(np.float32)),
        wqkv_qk=np.ascontiguousarray(np.concatenate([qA_B, kA_B], axis=1)),
        wqkv_v=np.ascontiguousarray(wqkvT[:, 2 * D:3 * D]),
        ident=IDEN128,
    )
    wkvt = np.asarray(inputs["L0_ffn_w_kv"]).transpose(0, 2, 1).astype(np.float32)
    com["wkvt0"] = np.ascontiguousarray(wkvt[0])
    com["wkvt1"] = np.ascontiguousarray(wkvt[1])
    wqgt = np.asarray(inputs["L0_ffn_w_q"]).transpose(0, 2, 1).astype(np.float32)
    com["wqgt0"] = np.ascontiguousarray(wqgt[0])
    com["wqgt1"] = np.ascontiguousarray(wqgt[1])
    gains = np.zeros((128, 40), np.float32)
    lng = np.asarray(inputs["L0_ffn_ln_g"]).astype(np.float32)
    cn = np.asarray(inputs["L0_ffn_cnorm_g"]).astype(np.float32)
    qn = np.asarray(inputs["L0_ffn_qnorm_g"]).astype(np.float32)
    gains[:, 0:8] = lng.reshape(8, 128).T
    gains[:, 8:24] = cn.reshape(2 * 8, 128).T
    gains[:, 24:40] = qn.reshape(2 * 8, 128).T
    com["gains"] = gains
    com["lnb_cols"] = np.ascontiguousarray(np.asarray(inputs["L0_ffn_ln_b"]).astype(np.float32).reshape(8, 128).T)
    rtq = np.asarray(inputs["L0_ffn_rt_q"]).astype(np.float32)
    rtkv = np.asarray(inputs["L0_ffn_rt_kv"]).astype(np.float32)
    rts4 = np.stack([rtq[0], rtq[1], rtkv[0], rtkv[1]])
    com["rts"] = np.ascontiguousarray(
        np.broadcast_to(rts4[:, None, :], (4, 128, D)).transpose(1, 0, 2).reshape(128, 4 * D).copy())
    x = np.asarray(inputs["L0_attn_x"]).astype(np.float32)

    in_maps = []
    for c in range(NCORE):
        xs = np.ascontiguousarray(np.concatenate(
            [x[b, k * CHUNK:(k + 1) * CHUNK, :] for (b, k) in core_chunks(c)], 0))
        cosf, sinf = stage_rope_fm(c)
        m = dict(com)
        m.update(x=xs, cosf=cosf, sinf=sinf)
        in_maps.append(m)
    return in_maps


_NC_CACHE = {}


def get_nc(name, builder):
    if name not in _NC_CACHE:
        _NC_CACHE[name] = builder().finalize()
    return _NC_CACHE[name]


def run_A(inputs):
    nc = get_nc("A", build_A)
    in_maps = stage_A_inputs(inputs)
    res = run_bass_kernel_spmd(nc, in_maps, core_ids=list(range(NCORE)))
    return res.results


def kernel(**inputs):
    inputs = {k: np.asarray(v) for k, v in inputs.items()}
    resA = run_A(inputs)
    resM = run_M2(inputs, resA)
    out = np.empty((B, S, D), np.float32)
    for cc in range(NCORE):
        for ci, (b, k) in enumerate(core_chunks(cc)):
            out[b, k * CHUNK:(k + 1) * CHUNK] = resM[cc]["out"][ci * CHUNK:(ci + 1) * CHUNK]
    return out



def emit_flash_job(c, ksb, qap, vtiles, n_kv, out_writer, st_bufs, KP=64,
                   masks=None, biases=None, gate=(), grp=3, null_kv=None, scale=0.125):
    scores_ps, p_sb, outT_ps = st_bufs
    masks = masks or {}
    biases = biases or {}
    bias_def = c.ccol(-SHIFT)
    groups = []
    t = 0
    while t < n_kv:
        if t in biases:
            groups.append([t]); t += 1
        else:
            gl = [t]; t += 1
            while t < n_kv and len(gl) < grp and t not in biases:
                gl.append(t); t += 1
            groups.append(gl)
    n_tot = n_kv + (1 if null_kv else 0)
    prev_exp = {}
    prev_av = {}
    av_tok = None
    n_av = 0
    first_av = True
    for gi, gl in enumerate(groups):
        par = gi % 2
        sp = scores_ps[par]
        pb = p_sb[par]
        w0 = list(gate) + later(prev_exp.get(par))
        sim_tok = None
        for j, t_ in enumerate(gl):
            sim_tok = c.op("tensor", lambda E, j=j, t_=t_, sp=sp: E.matmul(
                sp[:, j * 512:(j + 1) * 512], ksb(t_), qap,
                start=True, stop=True), wait=w0 if j == 0 else [], inc=c.mm)
        nb = len(gl)
        bias = biases.get(gl[0], bias_def) if nb == 1 else bias_def
        e_tok = c.op("scalar", lambda E, sp=sp, pb=pb, nb=nb, bias=bias: E.activation(
            out=pb[:, 0:nb * 512], in_=sp[:, 0:nb * 512], func=AF.Exp,
            bias=bias, scale=scale),
            wait=later(sim_tok, prev_av.get(par), c.ctok), inc=c.ac)
        m_tok = None
        for j, t_ in enumerate(gl):
            if t_ in masks:
                m_tok = c.op("vector", lambda E, j=j, t_=t_, pb=pb: E.tensor_tensor(
                    out=pb[:, j * 512:(j + 1) * 512], in0=pb[:, j * 512:(j + 1) * 512],
                    in1=masks[t_], op=ALU.mult), wait=later(e_tok), inc=c.ve)
        prev_exp[par] = later(e_tok, m_tok)[0]
        for j, t_ in enumerate(gl):
            wv = later(e_tok, m_tok)
            if first_av:
                wv = wv + later(c.war.get("outT"))
            n_av += 1
            av_tok = c.op("tensor", lambda E, j=j, t_=t_, pb=pb, la=(n_av == n_tot), fa=first_av: E.matmul(
                outT_ps[:, :], vtiles(t_), pb[:, j * 512:(j + 1) * 512],
                start=fa, stop=la), wait=wv, inc=c.mm)
            first_av = False
        prev_av[par] = av_tok
    if null_kv:
        kn, vn = null_kv
        par = len(groups) % 2
        sp = scores_ps[par]
        pb = p_sb[par]
        w0 = list(gate) + later(prev_exp.get(par))
        s_tok = c.op("tensor", lambda E: E.matmul(sp[0:1, 0:512], kn, qap,
                                                  start=True, stop=True), wait=w0, inc=c.mm)
        e_tok = c.op("scalar", lambda E, sp=sp, pb=pb: E.activation(
            out=pb[0:1, 0:512], in_=sp[0:1, 0:512], func=AF.Exp,
            bias=bias_def[0:1, :], scale=scale),
            wait=later(s_tok, prev_av.get(par), c.ctok), inc=c.ac)
        wv = later(e_tok)
        if first_av:
            wv = wv + later(c.war.get("outT"))
        av_tok = c.op("tensor", lambda E, pb=pb, fa=first_av: E.matmul(
            outT_ps[:, :], vn, pb[0:1, 0:512],
            start=fa, stop=True), wait=wv, inc=c.mm)
    out_writer(av_tok)


def emit_div_fm(c, outT_ps, psR, ones64, rrow, osb_dst, ev_f32, av_tok, qmask_row=None):
    t_l = c.op("vector", lambda E: E.tensor_scalar(
        out=rrow[:, :], in0=outT_ps[64:65, :], scalar1=1e-30, scalar2=None,
        op0=ALU.add, op1=ALU.bypass), wait=later(av_tok, c.war.get("rrow")), inc=c.ve)
    t_r = c.op("vector", lambda E: E.reciprocal(rrow[:, :], rrow[:, :]), wait=[t_l], inc=c.ve)
    if qmask_row is not None:
        t_r = c.op("vector", lambda E: E.tensor_tensor(out=rrow[:, :], in0=rrow[:, :],
                                                       in1=qmask_row, op=ALU.mult),
                   wait=[t_r], inc=c.ve)
    t_b = c.op("tensor", lambda E: E.matmul(psR[:, :], ones64, rrow[:, :],
                                            start=True, stop=True),
               wait=later(t_r, c.war.get("psR")), inc=c.mm)
    c.war["rrow"] = t_b
    t_e = c.op("vector", lambda E: E.tensor_copy(ev_f32[:, :], outT_ps[0:64, :]),
               wait=later(av_tok, c.war.get("ev_f32")), inc=c.ve)
    c.war["outT"] = t_e
    t_o = c.op("vector", lambda E: E.tensor_tensor(out=osb_dst, in0=ev_f32[:, :],
                                                   in1=psR[:, :], op=ALU.mult),
               wait=later(t_b, t_e), inc=c.ve)
    c.war["psR"] = t_o
    c.war["ev_f32"] = t_o
    return t_o



def build_B(c=None, ov=None):
    if c is None:
        c = Ctx()
    qj = c.io("qj", [4, HD, S], BF16, ov)
    kj = c.io("kj", [4, HD, S], BF16, ov)
    vj = c.io("vj", [4, S, HD], BF16, ov)
    maskc = c.io("maskc", [128, 4 * 512], BF16, ov)
    o_a = c.oo("aout", [4, HD, S], BF16, ov)

    ksb = c.sb("ksb", [64, S], BF16)
    qsb = c.sb("qsb", [64, S], BF16)
    vsb = c.sb("vsb", [128, 64, 65], BF16)
    msb = c.sb("msb", [128, 4 * 512], BF16)
    p_sb0 = c.sb("p_sb0", [128, 3 * 512], BF16)
    p_sb1 = c.sb("p_sb1", [128, 3 * 512], BF16)
    osb = c.sb("osb", [64, S], BF16)
    ev_f32 = c.sb("ev_f32", [64, 512], F32)
    rrow = c.sb("rrow", [1, 512], F32)
    ones64 = c.sb("ones64", [1, 64], F32)

    sc_ps0 = c.ps("sc_ps0", [128, 3 * 512], F32)
    sc_ps1 = c.ps("sc_ps1", [128, 3 * 512], F32)
    outT_ps = c.ps("outT_ps", [65, 512], F32)
    psR = c.ps("psR", [64, 512], F32)

    t_m = c.op("sync", lambda E: E.dma_start(out=msb[:, :], in_=maskc[:, :]), inc=c.ld, dma=True)
    t_o = c.op("vector", lambda E: E.memset(ones64[:, :], 1.0), inc=c.ve)

    for job in range(4):
        g_j = c.gate(c.mm, c.ve, c.ac)
        t_k = c.op("sync", lambda E, job=job: E.dma_start(out=ksb[:, :], in_=kj[job]), wait=g_j, inc=c.ld, dma=True)
        t_q = c.op("sync", lambda E, job=job: E.dma_start(out=qsb[:, :], in_=qj[job]), inc=c.ld, dma=True)
        t_v = c.op("sync", lambda E, job=job: E.dma_start(
            out=vsb[:, :, 0:64], in_=vj[job].rearrange("(t p) d -> p t d", p=128)), inc=c.ld, dma=True)
        t_1 = c.op("vector", lambda E: E.memset(vsb[:, :, 64:65], 1.0), wait=g_j + later(t_v), inc=c.ve)
        gate0 = later(t_k, t_q, t_v, t_1, t_m, t_o)
        for qt in range(16):
            Q0 = qt * 512
            n_kv = 4 * qt + 4
            masks = {4 * qt + i: msb[:, (i) * 512:(i + 1) * 512] for i in range(4)}
            emit_flash_job(
                c,
                ksb=lambda t_: ksb[:, t_ * 128:(t_ + 1) * 128],
                qap=qsb[:, Q0:Q0 + 512],
                vtiles=lambda t_: vsb[:, t_, :],
                n_kv=n_kv,
                out_writer=lambda av_tok, qt=qt, job=job: emit_div_fm(
                    c, outT_ps, psR, ones64[:, :], rrow, osb[:, qt * 512:(qt + 1) * 512],
                    ev_f32, av_tok),
                st_bufs=([sc_ps0, sc_ps1], [p_sb0, p_sb1], outT_ps),
                masks=masks, gate=gate0)
        g_all = c.gate(c.ve)
        c.op("sync", lambda E, job=job: E.dma_start(out=o_a[job], in_=osb[:, :]),
             wait=g_all, inc=c.st, dma=True)
    c.op("sync", lambda E: E.nop(), wait=c.gate(c.st))
    c.op("gpsimd", lambda E: E.nop(), wait=c.gate(c.st))
    return c


def assemble_full_fm(res, key):
    dims = res[0][key].shape[0]
    out = np.empty((B, dims, S), res[0][key].dtype)
    for cc in range(NCORE):
        for ci, (b, k) in enumerate(core_chunks(cc)):
            out[b, :, k * CHUNK:(k + 1) * CHUNK] = res[cc][key][:, ci * CHUNK:(ci + 1) * CHUNK]
    return out


def assemble_full_tm(res, key):
    dims = res[0][key].shape[1]
    out = np.empty((B, S, dims), res[0][key].dtype)
    for cc in range(NCORE):
        for ci, (b, k) in enumerate(core_chunks(cc)):
            out[b, k * CHUNK:(k + 1) * CHUNK, :] = res[cc][key][ci * CHUNK:(ci + 1) * CHUNK, :]
    return out


def stage_B_inputs(resA):
    qf = assemble_full_fm(resA, "q0")
    kf = assemble_full_fm(resA, "k0")
    vf = assemble_full_tm(resA, "v0")
    in_maps = []
    for cc in range(NCORE):
        qjv = np.stack([qf[b, h * HD:(h + 1) * HD, :] for (b, h) in core_bh_jobs(cc)])
        kjv = np.stack([kf[b, h * HD:(h + 1) * HD, :] for (b, h) in core_bh_jobs(cc)])
        vjv = np.stack([vf[b, :, h * HD:(h + 1) * HD] for (b, h) in core_bh_jobs(cc)])
        in_maps.append(dict(qj=np.ascontiguousarray(qjv), kj=np.ascontiguousarray(kjv),
                            vj=np.ascontiguousarray(vjv),
                            maskc=np.ascontiguousarray(MASK_C.transpose(1, 0, 2).reshape(128, 4 * 512))))
    return in_maps


def run_B(resA):
    nc = get_nc("B", build_B)
    res = run_bass_kernel_spmd(nc, stage_B_inputs(resA), core_ids=list(range(NCORE)))
    return res.results



def build_C(c=None, ov=None):
    if c is None:
        c = Ctx()
    qj = c.io("qj", [4, HD, S], BF16, ov)
    kj = c.io("kj", [4, HD, S], BF16, ov)
    vj = c.io("vj", [4, S, HD], BF16, ov)
    maskc = c.io("maskc", [128, 4 * 512], BF16, ov)
    maskp = c.io("maskp", [128, 4 * 512], BF16, ov)
    o_a = c.oo("aout", [4, HD, S], BF16, ov)

    ksb = c.sb("ksb", [64, S], BF16)
    qsb = c.sb("qsb", [64, S], BF16)
    vsb = c.sb("vsb", [128, 64, 65], BF16)
    msb = c.sb("msb", [128, 8 * 512], BF16)
    p_sb0 = c.sb("p_sb0", [128, 3 * 512], BF16)
    p_sb1 = c.sb("p_sb1", [128, 3 * 512], BF16)
    osb = c.sb("osb", [64, S], BF16)
    ev_f32 = c.sb("ev_f32", [64, 512], F32)
    rrow = c.sb("rrow", [1, 512], F32)
    ones64 = c.sb("ones64", [1, 64], F32)
    sc_ps0 = c.ps("sc_ps0", [128, 3 * 512], F32)
    sc_ps1 = c.ps("sc_ps1", [128, 3 * 512], F32)
    outT_ps = c.ps("outT_ps", [65, 512], F32)
    psR = c.ps("psR", [64, 512], F32)

    t_m = c.op("sync", lambda E: E.dma_start(out=msb[:, 0:4 * 512], in_=maskp[:, :]), inc=c.ld, dma=True)
    t_m2 = c.op("sync", lambda E: E.dma_start(out=msb[:, 4 * 512:8 * 512], in_=maskc[:, :]), inc=c.ld, dma=True)
    t_o = c.op("vector", lambda E: E.memset(ones64[:, :], 1.0), inc=c.ve)

    for job in range(4):
        g_j = c.gate(c.mm, c.ve, c.ac)
        t_k = c.op("sync", lambda E, job=job: E.dma_start(out=ksb[:, :], in_=kj[job]), wait=g_j, inc=c.ld, dma=True)
        t_q = c.op("sync", lambda E, job=job: E.dma_start(out=qsb[:, :], in_=qj[job]), inc=c.ld, dma=True)
        t_v = c.op("sync", lambda E, job=job: E.dma_start(
            out=vsb[:, :, 0:64], in_=vj[job].rearrange("(t p) d -> p t d", p=128)), inc=c.ld, dma=True)
        t_1 = c.op("vector", lambda E: E.memset(vsb[:, :, 64:65], 1.0), wait=g_j + later(t_v), inc=c.ve)
        gate0 = later(t_k, t_q, t_v, t_1, t_m, t_m2, t_o)
        for w in range(NWIN):
            if w == 0:
                tiles = [4 * w + i for i in range(4)]
                masks = {j: msb[:, (4 + i) * 512:(4 + i + 1) * 512] for j, i in enumerate(range(4))}
            else:
                tiles = [4 * (w - 1) + i for i in range(4)] + [4 * w + i for i in range(4)]
                masks = {j: msb[:, j * 512:(j + 1) * 512] for j in range(8)}
            emit_flash_job(
                c,
                ksb=lambda t_, tiles=tiles: ksb[:, tiles[t_] * 128:(tiles[t_] + 1) * 128],
                qap=qsb[:, w * 512:(w + 1) * 512],
                vtiles=lambda t_, tiles=tiles: vsb[:, tiles[t_], :],
                n_kv=len(tiles),
                out_writer=lambda av_tok, w=w: emit_div_fm(
                    c, outT_ps, psR, ones64[:, :], rrow, osb[:, w * 512:(w + 1) * 512],
                    ev_f32, av_tok),
                st_bufs=([sc_ps0, sc_ps1], [p_sb0, p_sb1], outT_ps),
                masks=masks, gate=gate0)
        g_all = c.gate(c.ve)
        c.op("sync", lambda E, job=job: E.dma_start(out=o_a[job], in_=osb[:, :]),
             wait=g_all, inc=c.st, dma=True)
    c.op("sync", lambda E: E.nop(), wait=c.gate(c.st))
    c.op("gpsimd", lambda E: E.nop(), wait=c.gate(c.st))
    return c


def stage_C_inputs(resA):
    qf = assemble_full_fm(resA, "qloc")
    kf = assemble_full_fm(resA, "kloc")
    vf = assemble_full_tm(resA, "vloc")
    mkc = np.ascontiguousarray(MASK_C.transpose(1, 0, 2).reshape(128, 4 * 512))
    mkp = np.ascontiguousarray(MASK_P.transpose(1, 0, 2).reshape(128, 4 * 512))
    in_maps = []
    for cc in range(NCORE):
        qjv = np.stack([qf[b, h * HD:(h + 1) * HD, :] for (b, h) in core_bh_jobs(cc)])
        kjv = np.stack([kf[b, h * HD:(h + 1) * HD, :] for (b, h) in core_bh_jobs(cc)])
        vjv = np.stack([vf[b, :, h * HD:(h + 1) * HD] for (b, h) in core_bh_jobs(cc)])
        in_maps.append(dict(qj=np.ascontiguousarray(qjv), kj=np.ascontiguousarray(kjv),
                            vj=np.ascontiguousarray(vjv), maskc=mkc, maskp=mkp))
    return in_maps


def run_C(resA):
    nc = get_nc("C", build_C)
    res = run_bass_kernel_spmd(nc, stage_C_inputs(resA), core_ids=list(range(NCORE)))
    return res.results



SC_KV = sum(SLOT_C)
KV_OFF = [sum(SLOT_C[:r]) for r in range(8)]
SC_Q = 8 * 4


def emit_coor(c, s_all, mask_all, SC, logk, n_iters, bufs, gate):
    bneg, tb1, tb2, ebuf = bufs["bneg"], bufs["tb1"], bufs["tb2"], bufs["ebuf"]
    rmax8, rsum8, g8, g8r, arow = bufs["rmax8"], bufs["rsum8"], bufs["g8"], bufs["g8r"], bufs["arow"]
    psT2, psB, psTot, psRow, psAE = bufs["psT2"], bufs["psB"], bufs["psTot"], bufs["psRow"], bufs["psAE"]
    ind8, ones128r, ones128c, idenf = bufs["ind8"], bufs["ones128r"], bufs["ones128c"], bufs["idenf"]
    slot_cols = bufs["slot_cols"]

    t_b = c.op("vector", lambda E: E.tensor_scalar(
        out=bneg[:, 0:SC], in0=s_all, scalar1=-1.0, scalar2=None,
        op0=ALU.mult, op1=ALU.bypass), wait=list(gate), inc=c.ve)
    a_tok = None
    for it in range(n_iters):
        ce = CD_SCHED[it]
        t1 = c.op("vector", lambda E: E.tensor_tensor(out=tb1[:, 0:SC], in0=s_all,
                                                      in1=bneg[:, 0:SC], op=ALU.add),
                  wait=later(t_b), inc=c.ve)
        if mask_all is not None:
            t2 = c.op("vector", lambda E: E.tensor_tensor(out=tb2[:, 0:SC], in0=tb1[:, 0:SC],
                                                          in1=mask_all, op=ALU.add),
                      wait=[t1], inc=c.ve)
        else:
            t2 = c.op("vector", lambda E: E.tensor_copy(tb2[:, 0:SC], tb1[:, 0:SC]),
                      wait=[t1], inc=c.ve)
        tmx = None
        for r, (off, C) in enumerate(slot_cols):
            tmx = c.op("vector", lambda E, off=off, C=C, r=r: E.tensor_scalar(
                out=tb1[:, off:off + C], in0=tb2[:, off:off + C], scalar1=0.0,
                scalar2=-3.0e38, op0=ALU.add, op1=ALU.max,
                accum_out=rmax8[:, r:r + 1]), wait=[t2] if r == 0 else [], inc=c.ve)
        ttr = c.op("tensor", lambda E: E.transpose(psT2[:, :], rmax8[:, 0:8], idenf[:, :]),
                   wait=later(tmx, c.war.get("psT2")), inc=c.mm)
        tgm = c.op("vector", lambda E: E.reduce_max(g8[0:8, 0:1], psT2[0:8, :], axis=mybir.AxisListType.X),
                   wait=[ttr], inc=c.ve)
        c.war["psT2"] = tgm
        tns = c.op("vector", lambda E, ce=ce: E.tensor_scalar(
            out=g8[:, 1:2], in0=g8[:, 0:1], scalar1=float(-1.0 / ce), scalar2=None,
            op0=ALU.mult, op1=ALU.bypass), wait=[tgm], inc=c.ve)
        ttr2 = c.op("tensor", lambda E: E.transpose(psT2[0:1, 0:8], g8[0:8, 1:2], idenf[0:8, 0:8]),
                    wait=later(tns, c.war.get("psT2")), inc=c.mm)
        tcp = c.op("vector", lambda E: E.tensor_copy(g8r[:, 0:8], psT2[0:1, 0:8]),
                   wait=[ttr2], inc=c.ve)
        c.war["psT2"] = tcp
        tb_ = c.op("tensor", lambda E: E.matmul(psB[:, 0:8], ones128r[:, :], g8r[:, 0:8],
                                                start=True, stop=True),
                   wait=later(tcp, c.war.get("psB")), inc=c.mm)
        tbg = c.op("vector", lambda E: E.tensor_copy(bufs["biasg"][:, 0:8], psB[:, 0:8]),
                   wait=later(tb_, c.war.get("biasg")), inc=c.ve)
        c.war["psB"] = tbg
        texp = None
        for r, (off, C) in enumerate(slot_cols):
            texp = c.op("scalar", lambda E, r=r, off=off, C=C, ce=ce: E.activation(
                out=ebuf[:, off:off + C], in_=tb2[:, off:off + C], func=AF.Exp,
                bias=bufs["biasg"][:, r:r + 1], scale=float(1.0 / ce),
                accum_out=rsum8[:, r:r + 1]), wait=[tbg] if r == 0 else [], inc=c.ac)
        c.war["biasg"] = texp
        ttot = c.op("tensor", lambda E: E.matmul(psTot[:, 0:1], rsum8[:, 0:8], ones128c[:, :],
                                                 start=True, stop=True),
                    wait=later(texp, c.war.get("psTot")), inc=c.mm)
        tln = c.op("scalar", lambda E: E.activation(out=g8[0:8, 2:3], in_=psTot[0:8, 0:1],
                                                    func=AF.Ln), wait=[ttot], inc=c.ac)
        c.war["psTot"] = tln
        ta1 = c.op("vector", lambda E, ce=ce: E.tensor_scalar(
            out=g8[:, 3:4], in0=g8[:, 2:3], scalar1=float(-ce),
            scalar2=float(ce * logk), op0=ALU.mult, op1=ALU.add),
            wait=[tln], inc=c.ve)
        ta2 = c.op("vector", lambda E: E.tensor_tensor(out=g8[:, 4:5], in0=g8[:, 3:4],
                                                       in1=g8[:, 0:1], op=ALU.subtract),
                   wait=[ta1], inc=c.ve)
        trow = c.op("tensor", lambda E, SC=SC: E.matmul(psRow[0:1, 0:SC], g8[0:8, 4:5], ind8[0:8, 0:SC],
                                                        start=True, stop=True),
                    wait=later(ta2, c.war.get("psRow")), inc=c.mm)
        tcw = c.op("vector", lambda E, SC=SC: E.tensor_copy(arow[:, 0:SC], psRow[0:1, 0:SC]),
                   wait=[trow], inc=c.ve)
        c.war["psRow"] = tcw
        tae = c.op("tensor", lambda E, SC=SC: E.matmul(psAE[:, 0:SC], ones128r[:, :], arow[:, 0:SC],
                                                       start=True, stop=True),
                   wait=later(tcw, c.war.get("psAE")), inc=c.mm)
        a_tok = tae
        if it < n_iters - 1:
            tr1 = c.op("vector", lambda E, SC=SC: E.scalar_tensor_tensor(
                out=tb1[:, 0:SC], in0=s_all, scalar=1.0, in1=psAE[:, 0:SC],
                op0=ALU.mult, op1=ALU.add), wait=[tae], inc=c.ve)
            t_b = c.op("vector", lambda E, SC=SC: E.tensor_scalar(
                out=bneg[:, 0:SC], in0=tb1[:, 0:SC], scalar1=0.0, scalar2=-1.0,
                op0=ALU.max, op1=ALU.mult), wait=[tr1], inc=c.ve)
            c.war["psAE"] = t_b
    return a_tok


def emit_select(c, s_all, SC, cap, bufs, a_tok, out_part, mkbias=None):
    tb1, ebuf = bufs["tb1"], bufs["ebuf"]
    psAE, psRow = bufs["psAE"], bufs["psRow"]
    psP1, psCS = bufs["psP1"], bufs["psCS"]
    satb, prefs = bufs["satb"], bufs["prefs"]
    ustrict, ones128b, resetrow, arow = bufs["ustrict"], bufs["ones128b"], bufs["resetrow"], bufs["arow"]
    crow, crow2 = bufs["crow"], bufs["crow2"]

    tf = c.op("vector", lambda E: E.scalar_tensor_tensor(
        out=tb1[:, 0:SC], in0=s_all, scalar=1.0, in1=psAE[:, 0:SC],
        op0=ALU.mult, op1=ALU.add), wait=later(a_tok), inc=c.ve)
    c.war["psAE"] = tf
    ts = c.op("vector", lambda E: E.tensor_scalar(
        out=satb[:, 0:SC], in0=tb1[:, 0:SC], scalar1=0.0, scalar2=None,
        op0=ALU.is_ge, op1=ALU.bypass), wait=[tf], inc=c.ve)
    tp1 = c.op("tensor", lambda E: E.matmul(psP1[:, 0:SC], ustrict[:, :], satb[:, 0:SC],
                                            start=True, stop=True),
               wait=later(ts, c.war.get("psP1")), inc=c.mm)
    tcs = c.op("tensor", lambda E: E.matmul(psCS[0:1, 0:SC], ones128b[:, :], satb[:, 0:SC],
                                            start=True, stop=True),
               wait=later(ts, c.war.get("psCS")), inc=c.mm)
    tcc = c.op("vector", lambda E: E.tensor_copy(crow[:, 0:SC], psCS[0:1, 0:SC]),
               wait=[tcs], inc=c.ve)
    c.war["psCS"] = tcc
    tsc = c.op("vector", lambda E: E.tensor_tensor_scan(
        crow2[:, 0:SC], resetrow[:, 0:SC], crow[:, 0:SC], 0.0, ALU.mult, ALU.add),
        wait=[tcc], inc=c.ve)
    tex = c.op("vector", lambda E: E.tensor_tensor(out=crow2[:, 0:SC], in0=crow2[:, 0:SC],
                                                   in1=crow[:, 0:SC], op=ALU.subtract),
               wait=[tsc], inc=c.ve)
    tbc = c.op("tensor", lambda E: E.matmul(psRow[0:1, 0:SC], bufs["ones1b"], crow2[:, 0:SC],
                                            start=True, stop=True),
               wait=later(tex, c.war.get("psRow")), inc=c.mm) if False else None
    tbc = c.op("tensor", lambda E: E.matmul(bufs["psAE"][:, 0:SC], bufs["ones128r"][:, :], crow2[:, 0:SC],
                                            start=True, stop=True),
               wait=later(tex, c.war.get("psAE")), inc=c.mm)
    te = c.op("vector", lambda E: E.tensor_copy(prefs[:, 0:SC], psP1[:, 0:SC]),
              wait=[tp1], inc=c.ve)
    c.war["psP1"] = te
    tpr = c.op("vector", lambda E: E.tensor_tensor(out=prefs[:, 0:SC], in0=prefs[:, 0:SC],
                                                   in1=bufs["psAE"][:, 0:SC], op=ALU.add),
               wait=later(te, tbc), inc=c.ve)
    c.war["psAE"] = tpr
    tpl = c.op("vector", lambda E, cap=cap: E.tensor_scalar(
        out=tb1[:, 0:SC], in0=prefs[:, 0:SC], scalar1=float(cap), scalar2=None,
        op0=ALU.is_lt, op1=ALU.bypass), wait=[tpr], inc=c.ve)
    tpart = c.op("vector", lambda E: E.tensor_tensor(out=out_part[:, 0:SC], in0=tb1[:, 0:SC],
                                                     in1=satb[:, 0:SC], op=ALU.mult),
                 wait=[tpl], inc=c.ve)
    tokb = None
    if mkbias is not None:
        tokb = c.op("vector", lambda E: E.tensor_scalar(
            out=mkbias[:, 0:SC], in0=out_part[:, 0:SC], scalar1=59985.0, scalar2=-60000.0,
            op0=ALU.mult, op1=ALU.add), wait=[tpart], inc=c.ve)
    return later(tpart, tokb)


def build_E(c=None, ov=None):
    if c is None:
        c = Ctx()
    skv = c.io("skv", [128, SC_KV], F32, ov)
    mkv = c.io("mkv", [128, SC_KV], F32, ov)
    sq = c.io("sq", [128, SC_Q], F32, ov)
    qslot = c.io("qslot", [8, 128, 8 * 512], BF16, ov)
    nullk = c.io("nullk", [8, 64, 16], BF16, ov)
    nullv = c.io("nullv", [8, 1, 16 * 65], BF16, ov)
    ustrict = c.io("ustrict", [128, 128], BF16, ov)
    idenf = c.io("idenf", [128, 128], F32, ov)
    ind8 = c.io("ind8", [8, SC_KV], F32, ov)
    ind8q = c.io("ind8q", [8, SC_Q], F32, ov)
    resetkv = c.io("resetkv", [1, SC_KV], F32, ov)
    resetq = c.io("resetq", [1, SC_Q], F32, ov)
    bigmjkv = c.io("bigmjkv", [128, SC_KV], F32, ov)
    bigmjq = c.io("bigmjq", [128, SC_Q], F32, ov)
    kregs = [c.io(f"kreg{r}", [1024, SLOT_C[r] * 128], BF16) for r in range(8)]
    vregs = [c.io(f"vreg{r}", [SLOT_C[r] * 128, 1024], BF16) for r in range(8)]
    o_a = c.oo("aout", [8, 128, 8 * 512], BF16, ov)
    o_pq = c.oo("pq", [8, 128, 4], F32, ov)
    o_part = c.oo("dbg_part", [128, SC_KV], F32, ov)

    skv_sb = c.sb("skv_sb", [128, SC_KV], F32)
    mkv_sb = c.sb("mkv_sb", [128, SC_KV], F32)
    sq_sb = c.sb("sq_sb", [128, SC_Q], F32)
    bneg = c.sb("bneg", [128, SC_KV], F32)
    tb1 = c.sb("tb1", [128, SC_KV], F32)
    tb2 = c.sb("tb2", [128, SC_KV], F32)
    ebuf = c.sb("ebuf", [128, SC_KV], F32)
    rmax8 = c.sb("rmax8", [128, 8], F32)
    rsum8 = c.sb("rsum8", [128, 8], F32)
    g8 = c.sb("g8", [128, 16], F32)
    g8r = c.sb("g8r", [1, 8], F32)
    biasg = c.sb("biasg", [128, 8], F32)
    arow = c.sb("arow", [1, SC_KV], F32)
    crow = c.sb("crow", [1, SC_KV], F32)
    crow2 = c.sb("crow2", [1, SC_KV], F32)
    satb = c.sb("satb", [128, SC_KV], BF16)
    prefs = c.sb("prefs", [128, SC_KV], F32)
    part_kv = c.sb("part_kv", [128, SC_KV], F32)
    biasb = c.sb("biasb", [128, SC_KV], F32)
    part_q = c.sb("part_q", [128, SC_Q], F32)
    ones128r = c.sb("ones128r", [1, 128], F32)
    ones128c = c.sb("ones128c", [128, 1], F32)
    ones128b = c.sb("ones128b", [128, 1], BF16)
    ust_sb = c.sb("ust_sb", [128, 128], BF16)
    idf_sb = c.sb("idf_sb", [128, 128], F32)
    ind8_sb = c.sb("ind8_sb", [8, SC_KV], F32)
    ind8q_sb = c.sb("ind8q_sb", [8, SC_Q], F32)
    rkv_sb = c.sb("rkv_sb", [1, SC_KV], F32)
    rq_sb = c.sb("rq_sb", [1, SC_Q], F32)
    bmjkv_sb = c.sb("bmjkv_sb", [128, SC_KV], F32)
    bmj_sb = c.sb("bmj_sb", [128, SC_Q], F32)
    keyb = c.sb("keyb", [128, SC_KV], F32)
    msk32 = c.sb("msk32", [128, SC_KV], I32)
    g8i = c.sb("g8i", [128, 2], I32)
    ksbs = [c.sb("ksbA", [65, 7680], BF16), c.sb("ksbB", [65, 7680], BF16)]
    q2sbs = [c.sb("q2sbA", [65, 512], BF16), c.sb("q2sbB", [65, 512], BF16)]
    vsbs = [c.sb("vsbA", [128, 60, 65], BF16), c.sb("vsbB", [128, 60, 65], BF16)]
    nk_sb = c.sb("nk_sb", [65, 16], BF16)
    nv_sb = c.sb("nv_sb", [1, 16 * 65], BF16)
    krow_sb = c.sb("krow_sb", [1, SC_KV * 128], BF16)
    btmp = c.sb("btmp", [128, 128], BF16)
    p_sb0 = c.sb("p_sb0", [128, 3 * 512], BF16)
    p_sb1 = c.sb("p_sb1", [128, 3 * 512], BF16)
    osb = c.sb("osb", [128, 8 * 512], BF16)
    ev_f32 = c.sb("ev_f32", [64, 512], F32)
    rrow = c.sb("rrow", [1, 512], F32)
    ones64 = c.sb("ones64", [1, 64], F32)

    tks = [
        c.op("sync", lambda E: E.dma_start(out=skv_sb[:, :], in_=skv[:, :]), inc=c.ld, dma=True),
        c.op("sync", lambda E: E.dma_start(out=mkv_sb[:, :], in_=mkv[:, :]), inc=c.ld, dma=True),
        c.op("sync", lambda E: E.dma_start(out=sq_sb[:, :], in_=sq[:, :]), inc=c.ld, dma=True),
        c.op("sync", lambda E: E.dma_start(out=ust_sb[:, :], in_=ustrict[:, :]), inc=c.ld, dma=True),
        c.op("sync", lambda E: E.dma_start(out=idf_sb[:, :], in_=idenf[:, :]), inc=c.ld, dma=True),
        c.op("sync", lambda E: E.dma_start(out=ind8_sb[:, :], in_=ind8[:, :]), inc=c.ld, dma=True),
        c.op("sync", lambda E: E.dma_start(out=ind8q_sb[:, :], in_=ind8q[:, :]), inc=c.ld, dma=True),
        c.op("sync", lambda E: E.dma_start(out=rkv_sb[:, :], in_=resetkv[:, :]), inc=c.ld, dma=True),
        c.op("sync", lambda E: E.dma_start(out=rq_sb[:, :], in_=resetq[:, :]), inc=c.ld, dma=True),
        c.op("sync", lambda E: E.dma_start(out=bmjkv_sb[:, :], in_=bigmjkv[:, :]), inc=c.ld, dma=True),
        c.op("sync", lambda E: E.dma_start(out=bmj_sb[:, :], in_=bigmjq[:, :]), inc=c.ld, dma=True),
        c.op("vector", lambda E: E.memset(ones128r[:, :], 1.0), inc=c.ve),
        c.op("vector", lambda E: E.memset(ones128c[:, :], 1.0), inc=c.ve),
        c.op("vector", lambda E: E.memset(ones128b[:, :], 1.0), inc=c.ve),
        c.op("vector", lambda E: E.memset(ones64[:, :], 1.0), inc=c.ve),
    ]
    gate_in = later(*tks)

    with (c.nc.psum_tensor("psT2", [8, 128], F32) as psT2,
          c.nc.psum_tensor("psB", [128, 8], F32) as psB,
          c.nc.psum_tensor("psTot", [8, 1], F32) as psTot,
          c.nc.psum_tensor("psRow", [1, SC_KV], F32) as psRow,
          c.nc.psum_tensor("psAE", [128, SC_KV], F32) as psAE,
          c.nc.psum_tensor("psP1", [128, SC_KV], F32) as psP1,
          c.nc.psum_tensor("psCS", [1, SC_KV], F32) as psCS):
        bufs = dict(bneg=bneg, tb1=tb1, tb2=tb2, ebuf=ebuf, rmax8=rmax8, rsum8=rsum8,
                    biasg=biasg,
                    g8=g8, g8r=g8r, arow=arow, crow=crow, crow2=crow2, satb=satb,
                    prefs=prefs, psT2=psT2, psB=psB, psTot=psTot, psRow=psRow, psAE=psAE,
                    psP1=psP1, psCS=psCS, ind8=ind8_sb, ones128r=ones128r,
                    ones128c=ones128c, ones128b=ones128b, idenf=idf_sb,
                    ustrict=ust_sb, resetrow=rkv_sb,
                    slot_cols=[(KV_OFF[r], SLOT_C[r]) for r in range(8)])
        bufs["keyb"] = keyb
        bufs["msk32"] = msk32
        bufs["g8i"] = g8i
        a_kv = emit_coor(c, skv_sb[:, :], mkv_sb[:, :], SC_KV, LOGK_KV, CD_ITERS, bufs, gate_in)
        sel_kv = emit_topk_bisect(c, skv_sb[:, :], SC_KV, NKV, bufs, a_kv, part_kv, bmjkv_sb[:, :])
        tbi = c.op("vector", lambda E: E.tensor_scalar(
            out=biasb[:, :], in0=part_kv[:, :], scalar1=479880.0, scalar2=-479880.0,
            op0=ALU.mult, op1=ALU.add), wait=sel_kv, inc=c.ve)
        sel_kv = later(tbi)
        c.op("sync", lambda E: E.dma_start(out=o_part[:, :], in_=part_kv[:, :]),
             wait=sel_kv, inc=c.st, dma=True)
        bufs["ind8"] = ind8q_sb
        bufs["resetrow"] = rq_sb
        bufs["slot_cols"] = [(4 * r, 4) for r in range(8)]
        a_q = emit_coor(c, sq_sb[:, :], None, SC_Q, LOGK_Q, CD_ITERS, bufs, sel_kv)
        sel_q = emit_topk_bisect(c, sq_sb[:, :], SC_Q, NQ, bufs, a_q, part_q, bmj_sb[:, :])
        for r in range(8):
            c.op("sync", lambda E, r=r: E.dma_start(out=o_pq[r], in_=part_q[:, 4 * r:4 * (r + 1)]),
                 wait=sel_q, inc=c.st, dma=True)
    g_coor = c.gate(c.ve, c.ac, c.mm)

    krow_toks = []
    with c.nc.psum_tensor("psB3", [128, 128], F32) as psB3:
        for r in range(8):
            C = SLOT_C[r]
            off = KV_OFF[r]
            ttr = c.op("tensor", lambda E, off=off, C=C: E.transpose(
                psB3[0:C, :], biasb[:, off:off + C], idf_sb[:, :]),
                wait=g_coor + later(c.war.get("psB3")), inc=c.mm)
            tcp = c.op("vector", lambda E, C=C: E.tensor_copy(btmp[0:C, :], psB3[0:C, :]),
                       wait=[ttr], inc=c.ve)
            c.war["psB3"] = tcp
            krow_toks.append(c.op("sync", lambda E, off=off, C=C: E.dma_start(
                out=krow_sb[0:1, off * 128:(off + C) * 128], in_=btmp[0:C, :]),
                wait=[tcp], inc=c.ld, dma=True))
    t_krow = later(*krow_toks)

    sc_ps0 = c.ps("sc_ps0", [128, 3 * 512], F32)
    sc_ps1 = c.ps("sc_ps1", [128, 3 * 512], F32)
    outT_ps = c.ps("outT_ps", [65, 512], F32)
    psR = c.ps("psR", [64, 512], F32)

    t_q1a = c.op("vector", lambda E: E.memset(q2sbs[0][64:65, :], 1.0), inc=c.ve)
    t_q1b = c.op("vector", lambda E: E.memset(q2sbs[1][64:65, :], 1.0), inc=c.ve)
    t_nk0 = c.op("vector", lambda E: E.memset(nk_sb[64:65, :], 0.0), inc=c.ve)
    t_init = later(t_q1a, t_q1b, t_nk0)

    idx = 0
    prev_cmp = []
    prev_buf = [[], []]
    for r in range(8):
        C = SLOT_C[r]
        off = KV_OFF[r]
        g_slot = c.gate(c.mm)
        t_nk = c.op("sync", lambda E, r=r: E.dma_start(out=nk_sb[0:64, :], in_=nullk[r]),
                    wait=g_slot, inc=c.ld, dma=True)
        t_nv = c.op("sync", lambda E, r=r: E.dma_start(out=nv_sb[:, :], in_=nullv[r]), inc=c.ld, dma=True)
        for h in range(H):
            par = idx % 2
            idx += 1
            wld = list(prev_buf[par])
            ksb_p, q2_p, vsb_p = ksbs[par], q2sbs[par], vsbs[par]
            t_k = c.op("sync", lambda E, r=r, h=h, C=C, ksb_p=ksb_p: E.dma_start(
                out=ksb_p[0:64, 0:C * 128], in_=kregs[r][h * 64:(h + 1) * 64, :]),
                wait=wld, inc=c.ld, dma=True)
            t_kb = c.op("sync", lambda E, off=off, C=C, ksb_p=ksb_p: E.dma_start(
                out=ksb_p[64:65, 0:C * 128], in_=krow_sb[0:1, off * 128:(off + C) * 128]),
                wait=t_krow, inc=c.ld, dma=True)
            t_v = c.op("sync", lambda E, r=r, h=h, C=C, vsb_p=vsb_p: E.dma_start(
                out=vsb_p[:, 0:C, 0:64],
                in_=vregs[r][:, h * 64:(h + 1) * 64].rearrange("(t p) d -> p t d", p=128)),
                inc=c.ld, dma=True)
            t_1 = c.op("vector", lambda E, C=C, vsb_p=vsb_p: E.memset(vsb_p[:, 0:C, 64:65], 1.0),
                       wait=wld + later(t_v), inc=c.ve)
            ch, po = h // 2, (h % 2) * 64
            t_q = c.op("sync", lambda E, r=r, po=po, ch=ch, q2_p=q2_p: E.dma_start(
                out=q2_p[0:64, :], in_=qslot[r][po:po + 64, ch * 512:(ch + 1) * 512]),
                inc=c.ld, dma=True)

            gate0 = later(t_q, t_nk, t_nv, t_k, t_kb, t_v, t_1) + g_coor + gate_in + t_init + prev_cmp
            emit_flash_job(
                c,
                ksb=lambda t_, ksb_p=ksb_p: ksb_p[0:65, t_ * 128:(t_ + 1) * 128],
                qap=q2_p[0:65, :],
                vtiles=lambda t_, vsb_p=vsb_p: vsb_p[:, t_, :],
                n_kv=C,
                out_writer=lambda av_tok, ch=ch, po=po: emit_div_fm(
                    c, outT_ps, psR, ones64[:, :], rrow,
                    osb[po:po + 64, ch * 512:(ch + 1) * 512], ev_f32, av_tok),
                st_bufs=([sc_ps0, sc_ps1], [p_sb0, p_sb1], outT_ps),
                gate=gate0,
                null_kv=(nk_sb[0:65, h:h + 1], nv_sb[0:1, h * 65:(h + 1) * 65]))
            prev_cmp = c.gate(c.mm, c.ac, c.ve)
            prev_buf[par] = c.gate(c.mm)
        g_all = c.gate(c.ve)
        c.op("sync", lambda E, r=r: E.dma_start(out=o_a[r], in_=osb[:, :]),
             wait=g_all, inc=c.st, dma=True)
    c.op("sync", lambda E: E.nop(), wait=c.gate(c.st))
    c.op("gpsimd", lambda E: E.nop(), wait=c.gate(c.st))
    return c


def stage_E_consts():
    ust = np.triu(np.ones((128, 128), np.float32), 1).astype(BF)
    idenf = np.eye(128, dtype=np.float32)
    ind8 = np.zeros((8, SC_KV), np.float32)
    resetkv = np.ones((1, SC_KV), np.float32)
    for r in range(8):
        ind8[r, KV_OFF[r]:KV_OFF[r] + SLOT_C[r]] = 1.0
        resetkv[0, KV_OFF[r]] = 0.0
    ind8q = np.zeros((8, SC_Q), np.float32)
    resetq = np.ones((1, SC_Q), np.float32)
    for r in range(8):
        ind8q[r, 4 * r:4 * r + 4] = 1.0
        resetq[0, 4 * r] = 0.0
    bigmj = np.zeros((128, SC_Q), np.float32)
    for r in range(8):
        for col in range(4):
            bigmj[:, 4 * r + col] = 1.0e6 - (col * 128 + np.arange(128))
    bigmjkv = np.zeros((128, SC_KV), np.float32)
    for r in range(8):
        for col in range(SLOT_C[r]):
            bigmjkv[:, KV_OFF[r] + col] = 1.0e6 - (col * 128 + np.arange(128))
    return dict(ustrict=np.ascontiguousarray(ust), idenf=idenf, ind8=ind8,
                ind8q=ind8q, resetkv=resetkv, resetq=resetq, bigmjq=bigmj,
                bigmjkv=bigmjkv)


def scol_to_sfull(resA):
    s_full = np.zeros((B, S, 4), np.float32)
    for cc in range(NCORE):
        sc = resA[cc]["scolg"].astype(np.float32)
        for ci, (b, k) in enumerate(core_chunks(cc)):
            for j in range(4):
                blk = sc[:, j * NT + ci * 8: j * NT + (ci + 1) * 8]
                s_full[b, k * CHUNK:(k + 1) * CHUNK, j] = blk.T.reshape(CHUNK)
    return s_full


def stage_E_inputs(resA, inputs):
    s_full = scol_to_sfull(resA)
    xq_full = [assemble_full_fm(resA, f"xq{g}") for g in range(G)]
    kvk_full = [assemble_full_fm(resA, f"kvk{g}") for g in range(G)]
    kvv_full = [assemble_full_tm(resA, f"kvv{g}") for g in range(G)]
    nkv = np.asarray(inputs["L0_ffn_null_kv"], np.float32)
    consts = stage_E_consts()
    in_maps = []
    for cc in range(NCORE):
        skv = np.zeros((128, SC_KV), np.float32)
        mkv = np.zeros((128, SC_KV), np.float32)
        sq = np.zeros((128, SC_Q), np.float32)
        qslot = np.zeros((8, 128, 8 * 512), BF)
        nullk = np.zeros((8, 64, 16), BF)
        nullv = np.zeros((8, 1, 16 * 65), BF)
        nullv[:, 0, 64::65] = 1.0
        m = dict(consts)
        for r in range(8):
            task = SLOTS[r][cc]
            Cr = SLOT_C[r]
            if task is None:
                m[f"kreg{r}"] = np.zeros((1024, Cr * 128), BF)
                m[f"vreg{r}"] = np.zeros((Cr * 128, 1024), BF)
                continue
            b, w, g = task
            R = w * W
            Ct = R // 128
            sv = s_full[b, :R, 2 + g]
            skv[:, KV_OFF[r]:KV_OFF[r] + Ct] = sv.reshape(Ct, 128).T
            skv[:, KV_OFF[r] + Ct:KV_OFF[r] + Cr] = -1e30
            mkv[:, KV_OFF[r] + Ct:KV_OFF[r] + Cr] = -1e30
            sq[:, 4 * r:4 * r + 4] = s_full[b, w * W:(w + 1) * W, g].reshape(4, 128).T
            qs = xq_full[g][b][:, w * W:(w + 1) * W]
            qslot[r] = qs.reshape(8, 128, 512).transpose(1, 0, 2).reshape(128, 8 * 512)
            kr = np.zeros((1024, Cr * 128), BF)
            kr[:, 0:R] = kvk_full[g][b][:, :R]
            m[f"kreg{r}"] = np.ascontiguousarray(kr)
            vr = np.zeros((Cr * 128, 1024), BF)
            vr[0:R, :] = kvv_full[g][b][:R, :]
            m[f"vreg{r}"] = np.ascontiguousarray(vr)
            nullk[r] = nkv[0, g, :, 0, :].T.astype(BF)
            nv = np.zeros((16, 65), BF)
            nv[:, 0:64] = nkv[1, g, :, 0, :].astype(BF)
            nv[:, 64] = 1.0
            nullv[r] = nv.reshape(1, 16 * 65)
        m.update(skv=skv, mkv=mkv, sq=sq, qslot=qslot, nullk=nullk, nullv=nullv)
        in_maps.append(m)
    return in_maps


def run_E(resA, inputs):
    nc = get_nc("E", build_E)
    res = run_bass_kernel_spmd(nc, stage_E_inputs(resA, inputs), core_ids=list(range(NCORE)))
    return res.results


def emit_topk_bisect(c, s_all, SC, cap, bufs, a_tok, out_part, bigmj, n_iter=48):
    tb1, tb2, ebuf = bufs["tb1"], bufs["tb2"], bufs["ebuf"]
    psAE, psT2, psB, psTot = bufs["psAE"], bufs["psT2"], bufs["psB"], bufs["psTot"]
    g8, g8r = bufs["g8"], bufs["g8r"]
    rc8 = bufs["rsum8"]
    idenf, ones128r, ones128c = bufs["idenf"], bufs["ones128r"], bufs["ones128c"]
    slot_cols = bufs["slot_cols"]
    keyb = bufs["keyb"]
    midb = bufs["biasg"]
    trashq = bufs["rmax8"]

    tf = c.op("vector", lambda E: E.scalar_tensor_tensor(
        out=tb1[:, 0:SC], in0=s_all, scalar=1.0, in1=psAE[:, 0:SC],
        op0=ALU.mult, op1=ALU.add), wait=later(a_tok), inc=c.ve)
    c.war["psAE"] = tf
    tv = c.op("vector", lambda E: E.tensor_scalar(
        out=tb2[:, 0:SC], in0=tb1[:, 0:SC], scalar1=0.0, scalar2=None,
        op0=ALU.min, op1=ALU.bypass), wait=[tf], inc=c.ve)
    msk32 = bufs["msk32"]
    g8i = bufs["g8i"]
    tsat = c.op("vector", lambda E: E.tensor_scalar(
        out=msk32[:, 0:SC], in0=tb1[:, 0:SC], scalar1=0.0, scalar2=None,
        op0=ALU.is_ge, op1=ALU.bypass), wait=[tf], inc=c.ve)
    tkey = c.op("vector", lambda E: E.select(out=keyb[:, 0:SC], mask=msk32[:, 0:SC],
                                             on_true=bigmj, on_false=tb2[:, 0:SC]),
                wait=later(tv, tsat), inc=c.ve)
    tlo = c.op("vector", lambda E: E.memset(g8[:, 8:9], -65536.0), inc=c.ve)
    thi = c.op("vector", lambda E: E.memset(g8[:, 9:10], 2.0e6), inc=c.ve)
    tprev = later(tkey, tlo, thi)
    for it in range(n_iter):
        tm1 = c.op("vector", lambda E: E.tensor_tensor(out=g8[:, 10:11], in0=g8[:, 8:9],
                                                       in1=g8[:, 9:10], op=ALU.add),
                   wait=tprev, inc=c.ve)
        tm2 = c.op("vector", lambda E: E.tensor_scalar(out=g8[:, 10:11], in0=g8[:, 10:11],
                                                       scalar1=0.5, scalar2=None,
                                                       op0=ALU.mult, op1=ALU.bypass),
                   wait=[tm1], inc=c.ve)
        ttr = c.op("tensor", lambda E: E.transpose(psT2[0:1, 0:8], g8[0:8, 10:11], idenf[0:8, 0:8]),
                   wait=later(tm2, c.war.get("psT2")), inc=c.mm)
        tcp = c.op("vector", lambda E: E.tensor_copy(g8r[:, 0:8], psT2[0:1, 0:8]),
                   wait=[ttr], inc=c.ve)
        c.war["psT2"] = tcp
        tb_ = c.op("tensor", lambda E: E.matmul(psB[:, 0:8], ones128r[:, :], g8r[:, 0:8],
                                                start=True, stop=True),
                   wait=later(tcp, c.war.get("psB")), inc=c.mm)
        tmb = c.op("vector", lambda E: E.tensor_copy(midb[:, 0:8], psB[:, 0:8]),
                   wait=later(tb_, c.war.get("biasg")), inc=c.ve)
        c.war["psB"] = tmb
        tcn = None
        for r, (off, C) in enumerate(slot_cols):
            tcn = c.op("vector", lambda E, r=r, off=off, C=C: E.tensor_scalar(
                out=ebuf[:, off:off + C], in0=keyb[:, off:off + C],
                scalar1=midb[:, r:r + 1], scalar2=0.0, op0=ALU.is_ge, op1=ALU.add,
                accum_out=rc8[:, r:r + 1]), wait=[tmb] if r == 0 else [], inc=c.ve)
        ttot = c.op("tensor", lambda E: E.matmul(psTot[:, 0:1], rc8[:, 0:8], ones128c[:, :],
                                                 start=True, stop=True),
                    wait=later(tcn, c.war.get("psTot")), inc=c.mm)
        tge = c.op("vector", lambda E, cap=cap: E.tensor_scalar(
            out=g8i[0:8, 0:1], in0=psTot[0:8, 0:1], scalar1=float(cap), scalar2=None,
            op0=ALU.is_ge, op1=ALU.bypass), wait=[ttot], inc=c.ve)
        c.war["psTot"] = tge
        ts1 = c.op("vector", lambda E: E.select(out=g8[0:8, 8:9], mask=g8i[0:8, 0:1],
                                                on_true=g8[0:8, 10:11], on_false=g8[0:8, 8:9]),
                   wait=[tge], inc=c.ve)
        ts2 = c.op("vector", lambda E: E.select(out=g8[0:8, 9:10], mask=g8i[0:8, 0:1],
                                                on_true=g8[0:8, 9:10], on_false=g8[0:8, 10:11]),
                   wait=[ts1], inc=c.ve)
        tprev = [ts2[0] if False else ts2][0] and later(ts2)
    ttr = c.op("tensor", lambda E: E.transpose(psT2[0:1, 0:8], g8[0:8, 8:9], idenf[0:8, 0:8]),
               wait=later(later(ts2)[0], c.war.get("psT2")), inc=c.mm)
    tcp = c.op("vector", lambda E: E.tensor_copy(g8r[:, 0:8], psT2[0:1, 0:8]),
               wait=[ttr], inc=c.ve)
    c.war["psT2"] = tcp
    tb_ = c.op("tensor", lambda E: E.matmul(psB[:, 0:8], ones128r[:, :], g8r[:, 0:8],
                                            start=True, stop=True),
               wait=later(tcp, c.war.get("psB")), inc=c.mm)
    tmb = c.op("vector", lambda E: E.tensor_copy(midb[:, 0:8], psB[:, 0:8]),
               wait=later(tb_, c.war.get("biasg")), inc=c.ve)
    c.war["psB"] = tmb
    tp = None
    for r, (off, C) in enumerate(slot_cols):
        tp = c.op("vector", lambda E, r=r, off=off, C=C: E.tensor_scalar(
            out=out_part[:, off:off + C], in0=keyb[:, off:off + C],
            scalar1=midb[:, r:r + 1], scalar2=None, op0=ALU.is_ge, op1=ALU.bypass),
            wait=[tmb] if r == 0 else [], inc=c.ve)
    return later(tp)



def build_F(c=None, ov=None):
    if c is None:
        c = Ctx()
    x = c.io("x", [TOK, D], F32, ov)
    attn0 = c.io("attn0", [H * HD, TOK], BF16, ov)
    attnl = c.io("attnl", [H * HD, TOK], BF16, ov)
    attng0 = c.io("attng0", [H * HD, TOK], BF16, ov)
    attng1 = c.io("attng1", [H * HD, TOK], BF16, ov)
    pq = c.io("pq", [128, NT * 2], F32, ov)
    wo = c.io("wo", [D, D], F32, ov)
    wob = c.io("wob", [1, D], F32, ov)
    wlo = c.io("wlo", [D, D], F32, ov)
    wog0 = c.io("wog0", [D, D], F32, ov)
    wog1 = c.io("wog1", [D, D], F32, ov)
    wq1 = c.io("wq1", [D, D], F32, ov)
    wk1 = c.io("wk1", [D, D], F32, ov)
    wv1 = c.io("wv1", [D, D], F32, ov)
    bqkv = c.io("bqkv", [1, 3 * D], F32, ov)
    rwb = c.io("rwb", [128, 8 * D], F32, ov)
    ebias = c.io("ebias", [128, 8], F32, ov)
    sW1 = c.io("sW1", [D, 512], F32, ov)
    sW2 = c.io("sW2", [512, D], F32, ov)
    rW1 = [c.io(f"rW1_{e}", [D, 512], F32, ov) for e in range(8)]
    rW2 = [c.io(f"rW2_{e}", [512, D], F32, ov) for e in range(8)]
    hbias = c.io("hbias", [1, 9 * 512], F32, ov)
    rb2 = c.io("rb2", [8, D], F32, ov)
    sb2 = c.io("sb2", [1, D], F32, ov)
    ident = c.io("ident", [128, 128], BF16, ov)
    sel8d = c.io("sel8d", [8, 8 * 128], BF16, ov)

    o_x1 = c.oo("x1", [TOK, D], F32, ov)
    o_xn1 = c.oo("xn1", [TOK, D], F32, ov)
    o_g1 = c.oo("g1", [TOK, D], F32, ov)
    o_q1 = c.oo("q1", [H * HD, TOK], BF16, ov)
    o_k1 = c.oo("k1", [H * HD, TOK], BF16, ov)
    o_v1 = c.oo("v1", [TOK, D], BF16, ov)
    o_aug = c.oo("k1aug", [1, TOK], BF16, ov)
    o_qm = c.oo("qmask", [1, TOK], BF16, ov)
    o_dbg = c.oo("dbg", [128, 64], F32, ov)
    s0scr = c.dtmp("s0scr", [TOK, D], F32)
    hscr = c.dtmp("hscr", [9, 512, TOK], BF16)

    BUF_A = c.sb("BUF_A", [128, NT * D], F32)
    asrc = c.sb("asrc", [128, 8 * TOK], BF16)
    w_sb = c.sb("w_sb", [128, 8 * D], BF16)
    x_t = c.sb("x_t", [128, 2 * D], F32)
    sc1 = c.sb("sc1", [128, 2 * D], F32)
    pout = c.sb("pout", [128, TOK], BF16)
    stat2 = c.sb("stat2", [128, 12], F32)
    mv = c.sb("mv", [128, 2 * NT], F32)
    cols = c.sb("cols", [128, 32], F32)
    pq_sb = c.sb("pq_sb", [128, NT * 2], F32)
    brow = c.sb("brow", [1, 3 * D], BF16)
    onesr = c.sb("onesr", [1, 512], BF16)
    iden = c.sb("iden", [128, 128], BF16)
    tm_bf = c.sb("tm_bf", [128, 2 * D], BF16)
    lgc = c.sb("lgc", [128, NT * 8], F32)
    actc = c.sb("actc", [128, NT * 8], F32)
    combc = c.sb("combc", [128, NT * 8], F32)
    m8 = c.sb("m8", [128, 8], F32)
    ct8 = c.sb("ct8", [8, TOK], BF16)
    sel8 = c.sb("sel8", [8, 8 * 128], BF16)
    hrow = c.sb("hrow", [1, 9 * 512], BF16)
    rb2sb = c.sb("rb2sb", [8, D], BF16)
    sb2row = c.sb("sb2row", [1, D], BF16)
    hsb = c.sb("hsb", [128, 4 * TOK], BF16)
    hld = c.sb("hld", [128, 4 * 128], BF16)
    rsdbg = c.sb("rsdbg", [128, NT], F32)
    augc = c.sb("augc", [128, NT], BF16)
    qmc = c.sb("qmc", [128, NT], F32)
    augr = c.sb("augr", [1, TOK], BF16)
    qmr = c.sb("qmr", [1, TOK], BF16)
    rwf = c.sb("rwf", [128, D], F32)
    ebias_sb = c.sb("ebias_sb", [128, 8], F32)
    trash = c.sb("trash", [128, D], F32)
    wobrow = c.sb("wobrow", [1, D], BF16)

    psA = c.ps("psA", [128, 512], F32)
    psB = c.ps("psB", [128, 512], F32)
    psT = c.ps("psT", [128, 128], BF16)
    psC = c.ps("psC", [128, 512], F32)

    tks = [
        c.op("sync", lambda E: E.dma_start(out=iden[:, :], in_=ident[:, :]), inc=c.ld, dma=True),
        c.op("sync", lambda E: E.dma_start(out=pq_sb[:, :], in_=pq[:, :]), inc=c.ld, dma=True),
        c.op("sync", lambda E: E.dma_start(out=ebias_sb[:, :], in_=ebias[:, :]), inc=c.ld, dma=True),
        c.op("gpsimd", lambda E: E.dma_start(out=brow[:, :], in_=bqkv[:, :]), inc=c.gp, dma=True),
        c.op("gpsimd", lambda E: E.dma_start(out=hrow[:, :], in_=hbias[:, :]), inc=c.gp, dma=True),
        c.op("gpsimd", lambda E: E.dma_start(out=rb2sb[:, :], in_=rb2[:, :]), inc=c.gp, dma=True),
        c.op("gpsimd", lambda E: E.dma_start(out=sb2row[:, :], in_=sb2[:, :]), inc=c.gp, dma=True),
        c.op("gpsimd", lambda E: E.dma_start(out=wobrow[:, :], in_=wob[:, :]), inc=c.gp, dma=True),
        c.op("sync", lambda E: E.dma_start(out=sel8[:, :], in_=sel8d[:, :]), inc=c.ld, dma=True),
        c.op("vector", lambda E: E.memset(onesr[:, :], 1.0), inc=c.ve),
    ]
    t_const = later(*tks)

    def load_fm(src_ap, gate):
        toks = []
        for kk in range(8):
            toks.append(c.op("sync", lambda E, kk=kk, src_ap=src_ap: E.dma_start(
                out=asrc[:, kk * TOK:(kk + 1) * TOK], in_=src_ap[kk * 128:(kk + 1) * 128, :]),
                wait=(gate if kk == 0 else []), inc=c.ld, dma=True))
        return later(*toks)

    def tm_proj_pass(wdram, evac_fn, gate, bias_row=None):
        wtok = emit_load_w(c, wdram, w_sb, D, wait0=c.gate(c.mm))
        for tb in range(NT):
            for oj in range(2):
                ps = psA if oj % 2 == 0 else psB
                key = "psA" if oj % 2 == 0 else "psB"
                parts = [(lambda k=k, tb=tb: asrc[:, k * TOK + tb * 128:k * TOK + (tb + 1) * 128],
                          lambda k=k, oj=oj: w_sb[:, k * D + oj * 512:k * D + (oj + 1) * 512]) for k in range(8)]
                tok = emit_mm_group(c, ps[:, :], parts, gate + wtok, key,
                                    last_stop=(bias_row is None))
                if bias_row is not None:
                    tok = c.op("tensor", lambda E, oj=oj, ps=ps, bias_row=bias_row: E.matmul(
                        ps[:, :], onesr[0:1, 0:128], bias_row[0:1, oj * 512:(oj + 1) * 512],
                        start=False, stop=True), inc=c.mm)
                evac_fn(tb, oj, ps, "psA" if oj % 2 == 0 else "psB", tok)

    g0 = load_fm(attnl, [])
    def ev_local(tb, oj, ps, key, tok):
        ev = c.op("vector", lambda E, tb=tb, oj=oj, ps=ps: E.tensor_copy(
            BUF_A[:, tb * D + oj * 512:tb * D + (oj + 1) * 512], ps[:, :]), wait=[tok], inc=c.ve)
        c.war[key] = ev
    tm_proj_pass(wlo, ev_local, g0)

    def ev_glob(gg):
        def f(tb, oj, ps, key, tok):
            sl = slice(tb * D + oj * 512, tb * D + (oj + 1) * 512)
            ev = c.op("vector", lambda E, tb=tb, sl=sl, ps=ps, gg=gg: E.scalar_tensor_tensor(
                out=BUF_A[:, sl], in0=ps[:, :], scalar=pq_sb[:, gg * NT + tb:gg * NT + tb + 1],
                in1=BUF_A[:, sl], op0=ALU.mult, op1=ALU.add), wait=[tok] + t_const, inc=c.ve)
            c.war[key] = ev
        return f
    g0 = load_fm(attng0, c.gate(c.mm))
    tm_proj_pass(wog0, ev_glob(0), g0)
    g0 = load_fm(attng1, c.gate(c.mm))
    tm_proj_pass(wog1, ev_glob(1), g0)
    g0 = load_fm(attn0, c.gate(c.mm))
    def ev_s0(tb, oj, ps, key, tok):
        wv = [tok] + (later(c.war.get("sc1dma")) if oj == 0 else [])
        ev = c.op("vector", lambda E, oj=oj, ps=ps: E.tensor_copy(
            sc1[:, oj * 512:(oj + 1) * 512], ps[:, :]), wait=wv, inc=c.ve)
        c.war[key] = ev
        if oj == 1:
            d = c.op("sync", lambda E, tb=tb: E.dma_start(out=s0scr[tb * 128:(tb + 1) * 128, :],
                                                          in_=sc1[:, 0:D]), wait=[ev], inc=c.st, dma=True)
            c.war["sc1dma"] = d
    tm_proj_pass(wo, ev_s0, g0 + c.gate(c.gp), bias_row=wobrow)

    g_acc = c.gate(c.ve, c.mm, c.st)
    xn1_done = [None] * NT
    prev = [None, None]
    for i in range(NT):
        buf = i % 2
        xs = x_t[:, buf * D:(buf + 1) * D]
        s0s = sc1[:, buf * D:(buf + 1) * D]
        t_x = c.op("sync", lambda E, i=i, buf=buf: E.dma_start(
            out=x_t[:, buf * D:(buf + 1) * D], in_=x[i * 128:(i + 1) * 128, :]),
            wait=(g_acc if i == 0 else []) + later(prev[buf]), inc=c.ld, dma=True)
        t_s = c.op("sync", lambda E, i=i, buf=buf: E.dma_start(
            out=sc1[:, buf * D:(buf + 1) * D], in_=s0scr[i * 128:(i + 1) * 128, :]),
            wait=(g_acc if i == 0 else []), inc=c.ld, dma=True)
        t_c = c.op("vector", lambda E, i=i: E.scalar_tensor_tensor(
            out=cols[:, 0:1], in0=pq_sb[:, i:i + 1], scalar=1.0,
            in1=pq_sb[:, NT + i:NT + i + 1], op0=ALU.mult, op1=ALU.add),
            wait=(g_acc if i == 0 else []), inc=c.ve)
        t_c2 = c.op("vector", lambda E: E.tensor_scalar(
            out=cols[:, 1:2], in0=cols[:, 0:1], scalar1=1.0, scalar2=1e-5,
            op0=ALU.add, op1=ALU.max), wait=[t_c], inc=c.ve)
        t_rc = c.op("vector", lambda E: E.reciprocal(cols[:, 2:3], cols[:, 1:2]), wait=[t_c2], inc=c.ve)
        t_g = c.op("vector", lambda E, i=i: E.tensor_scalar(
            out=trash[:, 0:D], in0=BUF_A[:, i * D:(i + 1) * D], scalar1=cols[:, 2:3],
            scalar2=None, op0=ALU.mult, op1=ALU.bypass), wait=[t_rc], inc=c.ve)
        t_si = c.op("scalar", lambda E: E.activation(out=trash[:, 0:D], in_=trash[:, 0:D],
                                                     func=AF.Silu), wait=[t_g], inc=c.ac)
        t_m = c.op("vector", lambda E, s0s=s0s: E.tensor_tensor(
            out=trash[:, 0:D], in0=trash[:, 0:D], in1=s0s, op=ALU.mult),
            wait=[t_si] + later(t_s), inc=c.ve)
        t_x1 = c.op("vector", lambda E, xs=xs: E.tensor_tensor(
            out=xs, in0=xs, in1=trash[:, 0:D], op=ALU.add), wait=[t_m] + later(t_x), inc=c.ve)
        t_st = c.op("sync", lambda E, i=i, buf=buf: E.dma_start(
            out=o_x1[i * 128:(i + 1) * 128, :], in_=x_t[:, buf * D:(buf + 1) * D]),
            wait=[t_x1], inc=c.st, dma=True)
        t_b1 = c.op("vector", lambda E, xs=xs: E.bn_stats(stat2[:, 0:6], xs[:, 0:512]), wait=[t_x1], inc=c.ve)
        t_b2 = c.op("vector", lambda E, xs=xs: E.bn_stats(stat2[:, 6:12], xs[:, 512:1024]), inc=c.ve)
        t_ag = c.op("vector", lambda E, i=i: E.bn_aggr(mv[:, 2 * i:2 * i + 2], stat2[:, 0:12]), wait=[t_b1, t_b2], inc=c.ve)
        t_ep = c.op("vector", lambda E, i=i: E.tensor_scalar(
            out=cols[:, 24 + (i % 2):25 + (i % 2)], in0=mv[:, 2 * i + 1:2 * i + 2],
            scalar1=1e-5, scalar2=None, op0=ALU.add, op1=ALU.bypass), wait=[t_ag], inc=c.ve)
        t_sq = c.op("scalar", lambda E, i=i: E.activation(out=cols[:, 8 + i:9 + i],
                                                          in_=cols[:, 24 + (i % 2):25 + (i % 2)],
                                                          func=AF.Sqrt),
                    wait=[t_ep], inc=c.ac)
        t_rc2 = c.op("vector", lambda E, i=i: E.reciprocal(cols[:, 4:5], cols[:, 8 + i:9 + i]), wait=[t_sq], inc=c.ve)
        t_xn = c.op("vector", lambda E, i=i, xs=xs: E.tensor_scalar(
            out=BUF_A[:, i * D:(i + 1) * D], in0=xs, scalar1=mv[:, 2 * i:2 * i + 1],
            scalar2=cols[:, 4:5], op0=ALU.subtract, op1=ALU.mult), wait=[t_rc2], inc=c.ve)
        c.op("sync", lambda E, i=i: E.dma_start(
            out=o_xn1[i * 128:(i + 1) * 128, :], in_=BUF_A[:, i * D:(i + 1) * D]),
            wait=[t_xn], inc=c.st, dma=True)
        xn1_done[i] = t_xn
        prev[buf] = later(t_x1, t_st)[0]
        t_rs = c.op("vector", lambda E, i=i: E.tensor_scalar(
            out=trash[:, 0:D], in0=BUF_A[:, i * D:(i + 1) * D], scalar1=0.0, scalar2=0.0,
            op0=ALU.add, op1=ALU.add, accum_out=cols[:, 5:6]), wait=[t_xn], inc=c.ve)
        t_qm = c.op("vector", lambda E, i=i: E.tensor_scalar(
            out=qmc[:, i:i + 1], in0=cols[:, 5:6], scalar1=0.0, scalar2=None,
            op0=ALU.not_equal, op1=ALU.bypass), wait=[t_rs], inc=c.ve)
        c.op("vector", lambda E, i=i: E.tensor_copy(rsdbg[:, i:i + 1], cols[:, 5:6]), inc=c.ve)
        c.op("vector", lambda E, i=i: E.tensor_scalar(
            out=augc[:, i:i + 1], in0=qmc[:, i:i + 1], scalar1=float(-AUGNEG),
            scalar2=float(AUGNEG), op0=ALU.mult, op1=ALU.add), wait=[t_qm], inc=c.ve)

    g_xn = later(*[t for t in xn1_done]) + t_const
    for e in range(8):
        t_rw = c.op("sync", lambda E, e=e: E.dma_start(out=rwf[:, :], in_=rwb[:, e * D:(e + 1) * D]),
                    wait=c.gate(c.ve), inc=c.ld, dma=True)
        for i in range(NT):
            c.op("vector", lambda E, i=i, e=e: E.scalar_tensor_tensor(
                out=trash[:, 0:D], in0=BUF_A[:, i * D:(i + 1) * D], scalar=1.0,
                in1=rwf[:, :], op0=ALU.mult, op1=ALU.mult,
                accum_out=lgc[:, i * 8 + e:i * 8 + e + 1]),
                wait=(g_xn + later(t_rw)) if i == 0 else [], inc=c.ve)
    for i in range(NT):
        t_lb = c.op("vector", lambda E, i=i: E.tensor_tensor(
            out=lgc[:, i * 8:(i + 1) * 8], in0=lgc[:, i * 8:(i + 1) * 8],
            in1=ebias_sb[:, 0:8], op=ALU.add), wait=c.gate(c.ve) if i == 0 else [], inc=c.ve)
        t_r = c.op("vector", lambda E, i=i: E.tensor_scalar(
            out=actc[:, i * 8:(i + 1) * 8], in0=lgc[:, i * 8:(i + 1) * 8],
            scalar1=0.0, scalar2=None, op0=ALU.max, op1=ALU.bypass), wait=[t_lb], inc=c.ve)
        t_a0 = c.op("vector", lambda E, i=i: E.tensor_scalar(
            out=combc[:, i * 8:(i + 1) * 8], in0=lgc[:, i * 8:(i + 1) * 8],
            scalar1=-1.0, scalar2=None, op0=ALU.mult, op1=ALU.bypass), wait=[t_lb], inc=c.ve)
        t_a = c.op("vector", lambda E, i=i: E.tensor_tensor(
            out=combc[:, i * 8:(i + 1) * 8], in0=combc[:, i * 8:(i + 1) * 8],
            in1=lgc[:, i * 8:(i + 1) * 8], op=ALU.max), wait=[t_a0], inc=c.ve)
        t_e = c.op("scalar", lambda E, i=i: E.activation(
            out=combc[:, i * 8:(i + 1) * 8], in_=combc[:, i * 8:(i + 1) * 8],
            func=AF.Exp, scale=-1.0), wait=[t_a], inc=c.ac)
        t_l0 = c.op("vector", lambda E, i=i: E.tensor_scalar(
            out=combc[:, i * 8:(i + 1) * 8], in0=combc[:, i * 8:(i + 1) * 8],
            scalar1=1.0, scalar2=None, op0=ALU.add, op1=ALU.bypass), wait=[t_e], inc=c.ve)
        t_l = c.op("scalar", lambda E, i=i: E.activation(
            out=combc[:, i * 8:(i + 1) * 8], in_=combc[:, i * 8:(i + 1) * 8],
            func=AF.Ln), wait=[t_l0], inc=c.ac)
        t_sp = c.op("vector", lambda E, i=i: E.tensor_tensor(
            out=actc[:, i * 8:(i + 1) * 8], in0=actc[:, i * 8:(i + 1) * 8],
            in1=combc[:, i * 8:(i + 1) * 8], op=ALU.add), wait=[t_l], inc=c.ve)
        t_sa = c.op("scalar", lambda E, i=i: E.activation(
            out=actc[:, i * 8:(i + 1) * 8], in_=actc[:, i * 8:(i + 1) * 8],
            func=AF.Sqrt), wait=[t_sp], inc=c.ac)
        t_m8 = c.op("vector", lambda E, i=i: E.max(out=m8[:, 0:8], in_=actc[:, i * 8:(i + 1) * 8]),
                    wait=[t_sa], inc=c.ve)
        t_ge = c.op("vector", lambda E, i=i: E.tensor_scalar(
            out=combc[:, i * 8:(i + 1) * 8], in0=actc[:, i * 8:(i + 1) * 8],
            scalar1=m8[:, 1:2], scalar2=None, op0=ALU.is_ge, op1=ALU.bypass),
            wait=[t_m8], inc=c.ve)
        c.op("vector", lambda E, i=i: E.tensor_tensor(
            out=combc[:, i * 8:(i + 1) * 8], in0=combc[:, i * 8:(i + 1) * 8],
            in1=actc[:, i * 8:(i + 1) * 8], op=ALU.mult), wait=[t_ge], inc=c.ve)

    g_cols = c.gate(c.ve)
    last = None
    for i in range(NT):
        t1 = c.op("tensor", lambda E, i=i: E.transpose(psT[0:1, 0:128], augc[:, i:i + 1], iden[:, :]),
                  wait=g_cols + later(last) + t_const, inc=c.mm)
        last = c.op("vector", lambda E, i=i: E.tensor_copy(augr[:, i * 128:(i + 1) * 128], psT[0:1, 0:128]),
                    wait=[t1], inc=c.ve)
    combcb = c.sb("combcb", [128, 8], BF16)
    for i in range(NT):
        t0b = c.op("vector", lambda E, i=i: E.tensor_copy(combcb[:, :], combc[:, i * 8:(i + 1) * 8]),
                   wait=later(last), inc=c.ve)
        t1 = c.op("tensor", lambda E, i=i: E.transpose(psT[0:8, 0:128], combcb[:, :], iden[:, :]),
                  wait=later(t0b), inc=c.mm)
        last = c.op("vector", lambda E, i=i: E.tensor_copy(ct8[:, i * 128:(i + 1) * 128], psT[0:8, 0:128]),
                    wait=[t1], inc=c.ve)
    qmcb = c.sb("qmcb", [128, NT], BF16)
    t_qc = c.op("vector", lambda E: E.tensor_copy(qmcb[:, :], qmc[:, :]), wait=later(last), inc=c.ve)
    for i in range(NT):
        t1 = c.op("tensor", lambda E, i=i: E.transpose(psT[0:1, 0:128], qmcb[:, i:i + 1], iden[:, :]),
                  wait=later(last, t_qc), inc=c.mm)
        last = c.op("vector", lambda E, i=i: E.tensor_copy(qmr[:, i * 128:(i + 1) * 128], psT[0:1, 0:128]),
                    wait=[t1], inc=c.ve)
    c.op("sync", lambda E: E.dma_start(out=o_aug[:, :], in_=augr[:, :]), wait=later(last), inc=c.st, dma=True)
    c.op("sync", lambda E: E.dma_start(out=o_qm[:, :], in_=qmr[:, :]), wait=later(last), inc=c.st, dma=True)
    t_cr = c.gate(c.ve)[0]

    g_asrc = c.gate(c.mm)
    prevt = [None, None]
    lastT = None
    for i in range(NT):
        buf = i % 2
        t_cst = c.op("vector", lambda E, i=i, buf=buf: E.tensor_copy(
            tm_bf[:, buf * D:(buf + 1) * D], BUF_A[:, i * D:(i + 1) * D]),
            wait=later(xn1_done[i], prevt[buf]) + (g_asrc if i == 0 else []), inc=c.ve)
        for kk in range(8):
            t_mm = c.op("tensor", lambda E, kk=kk, buf=buf: E.transpose(
                psT[:, :], tm_bf[:, buf * D + kk * 128:buf * D + (kk + 1) * 128], iden[:, :]),
                wait=later(t_cst, lastT), inc=c.mm)
            lastT = c.op("scalar", lambda E, i=i, kk=kk: E.copy(
                asrc[:, kk * TOK + i * 128:kk * TOK + (i + 1) * 128], psT[:, :]),
                wait=[t_mm], inc=c.ac)
        prevt[buf] = lastT
    g_xfm = c.gate(c.ac)

    def fm_proj_store(wdram, out_dram, brow_off):
        wtok = emit_load_w(c, wdram, w_sb, D, wait0=c.gate(c.mm))
        for oi in range(8):
            evs = []
            for ti in range(4):
                t0 = ti * 512
                parts = [(lambda k=k, oi=oi: w_sb[:, k * D + oi * 128:k * D + oi * 128 + 128],
                          lambda k=k, t0=t0: asrc[:, k * TOK + t0:k * TOK + t0 + 512]) for k in range(8)]
                tok = emit_mm_group(c, psA[:, :], parts, g_xfm + wtok + t_const, "psA", last_stop=False)
                tok = c.op("tensor", lambda E, oi=oi, brow_off=brow_off: E.matmul(
                    psA[:, :], brow[0:1, brow_off + oi * 128:brow_off + (oi + 1) * 128], onesr[:, :],
                    start=False, stop=True), inc=c.mm)
                ev = c.op("scalar", lambda E, t0=t0: E.activation(
                    out=pout[:, t0:t0 + 512], in_=psA[:, :], func=AF.Relu),
                    wait=[tok] + later(c.war.get("pout")), inc=c.ac)
                c.war["psA"] = ev
                evs.append(ev)
            d = c.op("sync", lambda E, oi=oi, out_dram=out_dram: E.dma_start(
                out=out_dram[oi * 128:(oi + 1) * 128, :], in_=pout[:, :]),
                wait=later(*evs), inc=c.st, dma=True)
            c.war["pout"] = d
    fm_proj_store(wq1, o_q1, 0)
    fm_proj_store(wk1, o_k1, D)
    wtok = emit_load_w(c, wv1, w_sb, D, wait0=c.gate(c.mm))
    for tb in range(NT):
        evs = []
        for oj in range(2):
            ps = psA if oj % 2 == 0 else psB
            key = "psA" if oj % 2 == 0 else "psB"
            parts = [(lambda k=k, tb=tb: asrc[:, k * TOK + tb * 128:k * TOK + (tb + 1) * 128],
                      lambda k=k, oj=oj: w_sb[:, k * D + oj * 512:k * D + (oj + 1) * 512]) for k in range(8)]
            tok = emit_mm_group(c, ps[:, :], parts, g_xfm + wtok, key, last_stop=False)
            tok = c.op("tensor", lambda E, oj=oj, ps=ps: E.matmul(
                ps[:, :], onesr[0:1, 0:128], brow[0:1, 2 * D + oj * 512:2 * D + (oj + 1) * 512],
                start=False, stop=True), inc=c.mm)
            ev = c.op("scalar", lambda E, oj=oj, ps=ps: E.activation(
                out=pout[:, oj * 512:(oj + 1) * 512], in_=ps[:, :], func=AF.Relu),
                wait=[tok] + later(c.war.get("pout")), inc=c.ac)
            c.war[key] = ev
            evs.append(ev)
        d = c.op("sync", lambda E, tb=tb: E.dma_start(
            out=o_v1[tb * 128:(tb + 1) * 128, :], in_=pout[:, 0:D]),
            wait=later(*evs), inc=c.st, dma=True)
        c.war["pout"] = d

    for br in range(9):
        wd = sW1 if br == 0 else rW1[br - 1]
        wtok = emit_load_w(c, wd, w_sb, 512, wait0=c.gate(c.mm))
        dtoks = []
        for oi in range(4):
            evs = []
            for ti in range(4):
                t0 = ti * 512
                parts = [(lambda k=k, oi=oi: w_sb[:, k * 512 + oi * 128:k * 512 + oi * 128 + 128],
                          lambda k=k, t0=t0: asrc[:, k * TOK + t0:k * TOK + t0 + 512]) for k in range(8)]
                tok = emit_mm_group(c, psA[:, :], parts, g_xfm + wtok + t_const + (later(c.war.get("hsbdma")) if (oi == 0 and ti == 0) else []), "psA", last_stop=False)
                tok = c.op("tensor", lambda E, oi=oi, br=br: E.matmul(
                    psA[:, :], hrow[0:1, br * 512 + oi * 128:br * 512 + (oi + 1) * 128], onesr[:, :],
                    start=False, stop=True), inc=c.mm)
                ev = c.op("scalar", lambda E, oi=oi, t0=t0: E.activation(
                    out=hsb[:, oi * TOK + t0:oi * TOK + t0 + 512], in_=psA[:, :], func=AF.Silu),
                    wait=[tok], inc=c.ac)
                c.war["psA"] = ev
                if br > 0:
                    tb_ = c.op("tensor", lambda E, br=br, t0=t0: E.matmul(
                        psC[:, :], sel8[:, (br - 1) * 128:br * 128], ct8[:, t0:t0 + 512],
                        start=True, stop=True), wait=later(ev, c.war.get("psC"), t_cr), inc=c.mm)
                    ev = c.op("vector", lambda E, oi=oi, t0=t0: E.tensor_tensor(
                        out=hsb[:, oi * TOK + t0:oi * TOK + t0 + 512],
                        in0=hsb[:, oi * TOK + t0:oi * TOK + t0 + 512], in1=psC[:, :], op=ALU.mult),
                        wait=[tb_], inc=c.ve)
                    c.war["psC"] = ev
                evs.append(ev)
            dtoks.append(c.op("sync", lambda E, br=br, oi=oi: E.dma_start(
                out=hscr[br, oi * 128:(oi + 1) * 128, :], in_=hsb[:, oi * TOK:(oi + 1) * TOK]),
                wait=later(*evs), inc=c.st, dma=True))
        c.war["hsbdma"] = later(*dtoks)[0]

    g_h = c.gate(c.st, c.mm)
    for br in range(10):
        if br < 9:
            wd = sW2 if br == 0 else rW2[br - 1]
            wtok = emit_load_w(c, wd, w_sb, D, rows=512, wait0=c.gate(c.mm))
        for tb in range(NT):
            if br < 9:
                t_hl = c.op("sync", lambda E, br=br, tb=tb: E.dma_start(
                    out=hld[:, :].rearrange("p (k f) -> p k f", k=4),
                    in_=hscr[br, :, tb * 128:(tb + 1) * 128].rearrange("(k p) t -> p k t", p=128)),
                    wait=(g_h if (br == 0 and tb == 0) else []) + later(c.war.get("hld")), inc=c.ld, dma=True)
            for oj in range(2):
                ps = psA if oj % 2 == 0 else psB
                key = "psA" if oj % 2 == 0 else "psB"
                if br < 9:
                    parts = [(lambda k=k: hld[:, k * 128:(k + 1) * 128],
                              lambda k=k, oj=oj: w_sb[:, k * D + oj * 512:k * D + (oj + 1) * 512]) for k in range(4)]
                    tok = emit_mm_group(c, ps[:, :], parts, later(t_hl) + wtok, key)
                else:
                    tok = emit_mm_group(c, ps[:, :],
                                        [(lambda tb=tb: ct8[:, tb * 128:(tb + 1) * 128],
                                          lambda oj=oj: rb2sb[:, oj * 512:(oj + 1) * 512])],
                                        c.gate(c.ve) if tb == 0 and oj == 0 else [], key, last_stop=False)
                    tok = c.op("tensor", lambda E, oj=oj, ps=ps: E.matmul(
                        ps[:, :], onesr[0:1, 0:128], sb2row[0:1, oj * 512:(oj + 1) * 512],
                        start=False, stop=True), inc=c.mm)
                sl = slice(tb * D + oj * 512, tb * D + (oj + 1) * 512)
                if br == 0:
                    ev = c.op("vector", lambda E, sl=sl, ps=ps: E.tensor_copy(BUF_A[:, sl], ps[:, :]),
                              wait=[tok], inc=c.ve)
                else:
                    ev = c.op("vector", lambda E, sl=sl, ps=ps: E.scalar_tensor_tensor(
                        out=BUF_A[:, sl], in0=ps[:, :], scalar=1.0, in1=BUF_A[:, sl],
                        op0=ALU.mult, op1=ALU.add), wait=[tok], inc=c.ve)
                c.war[key] = ev
            if br < 9:
                c.war["hld"] = ev
    c.op("sync", lambda E: E.dma_start(out=o_dbg[:, 0:32], in_=mv[:, :]), wait=c.gate(c.ve, c.ac), inc=c.st, dma=True)
    c.op("sync", lambda E: E.dma_start(out=o_dbg[:, 32:48], in_=rsdbg[:, :]), inc=c.st, dma=True)
    c.op("sync", lambda E: E.dma_start(out=o_dbg[:, 48:64], in_=cols[:, 0:16]), inc=c.st, dma=True)
    g_y = c.gate(c.ve)
    for i in range(NT):
        t_si = c.op("scalar", lambda E, i=i: E.activation(
            out=BUF_A[:, i * D:(i + 1) * D], in_=BUF_A[:, i * D:(i + 1) * D], func=AF.Silu),
            wait=(g_y if i == 0 else []), inc=c.ac)
        c.op("sync", lambda E, i=i: E.dma_start(
            out=o_g1[i * 128:(i + 1) * 128, :], in_=BUF_A[:, i * D:(i + 1) * D]),
            wait=[t_si], inc=c.st, dma=True)

    c.op("sync", lambda E: E.nop(), wait=c.gate(c.st))
    c.op("gpsimd", lambda E: E.nop(), wait=c.gate(c.st))
    return c


def stage_F_inputs(inputs, resA, resB, resC, resE):
    a0 = assemble_full_fm(resB, "aout2") if False else None
    def asm_bh(res, key):
        out = np.empty((B, H * HD, S), BF)
        for cc in range(NCORE):
            for j, (b, h) in enumerate(core_bh_jobs(cc)):
                out[b, h * HD:(h + 1) * HD, :] = res[cc][key][j]
        return out
    attn0f = asm_bh(resB, "aout")
    attnlf = asm_bh(resC, "aout")
    attngf = np.zeros((G, B, H * HD, S), BF)
    pqf = np.zeros((B, S, G), np.float32)
    for cc in range(NCORE):
        for r in range(8):
            task = SLOTS[r][cc]
            if task is None:
                continue
            b, w, g = task
            blk = resE[cc]["aout"][r].reshape(128, 8, 512).transpose(1, 0, 2).reshape(H * HD, 512)
            attngf[g, b, :, w * W:(w + 1) * W] = blk
            pqf[b, w * W:(w + 1) * W, g] = resE[cc]["pq"][r].T.reshape(W)
    com = stage_F_com(inputs)
    x = np.asarray(inputs["L0_attn_x"], np.float32)
    in_maps = []
    for cc in range(NCORE):
        m = dict(com)
        xs, a0s, als, ag0, ag1 = [], [], [], [], []
        pqs = []
        for (b, k) in core_chunks(cc):
            sl = slice(k * CHUNK, (k + 1) * CHUNK)
            xs.append(x[b, sl, :])
            a0s.append(attn0f[b, :, sl])
            als.append(attnlf[b, :, sl])
            ag0.append(attngf[0, b, :, sl])
            ag1.append(attngf[1, b, :, sl])
            pqs.append(pqf[b, sl, :])
        m["x"] = np.ascontiguousarray(np.concatenate(xs, 0))
        m["attn0"] = np.ascontiguousarray(np.concatenate(a0s, 1))
        m["attnl"] = np.ascontiguousarray(np.concatenate(als, 1))
        m["attng0"] = np.ascontiguousarray(np.concatenate(ag0, 1))
        m["attng1"] = np.ascontiguousarray(np.concatenate(ag1, 1))
        pqo = np.concatenate(pqs, 0)
        m["pq"] = np.ascontiguousarray(pqo.reshape(NT, 128, 2).transpose(1, 2, 0).reshape(128, 2 * NT))
        in_maps.append(m)
    return in_maps


def stage_F_com(inputs):
    wo = np.ascontiguousarray(np.asarray(inputs["L0_attn_wo_w"], np.float32))
    com = dict(
        wo=wo, wob=np.asarray(inputs["L0_attn_wo_b"], np.float32).reshape(1, D),
        wlo=np.ascontiguousarray(np.asarray(inputs["L0_ffn_w_lo"], np.float32).T),
        wog0=np.ascontiguousarray(np.asarray(inputs["L0_ffn_w_out"], np.float32)[0].T),
        wog1=np.ascontiguousarray(np.asarray(inputs["L0_ffn_w_out"], np.float32)[1].T),
        wq1=np.ascontiguousarray(np.asarray(inputs["L1_attn_Wq"], np.float32)),
        wk1=np.ascontiguousarray(np.asarray(inputs["L1_attn_Wk"], np.float32)),
        wv1=np.ascontiguousarray(np.asarray(inputs["L1_attn_Wv"], np.float32)),
        bqkv=np.concatenate([np.asarray(inputs["L1_attn_bq"], np.float32),
                             np.asarray(inputs["L1_attn_bk"], np.float32),
                             np.asarray(inputs["L1_attn_bv"], np.float32)]).reshape(1, 3 * D),
        rwb=np.ascontiguousarray(np.broadcast_to(
            np.asarray(inputs["L1_ffn_routing_W"], np.float32).reshape(1, 8 * D), (128, 8 * D))),
        ebias=np.ascontiguousarray(np.broadcast_to(
            np.asarray(inputs["L1_ffn_expert_bias"], np.float32).reshape(1, 8), (128, 8))),
        sW1=np.ascontiguousarray(np.asarray(inputs["L1_ffn_sW1"], np.float32)),
        sW2=np.ascontiguousarray(np.asarray(inputs["L1_ffn_sW2"], np.float32)),
        hbias=np.concatenate([np.asarray(inputs["L1_ffn_sb1"], np.float32),
                              np.asarray(inputs["L1_ffn_rb1"], np.float32).reshape(8 * 512)]).reshape(1, 9 * 512),
        rb2=np.ascontiguousarray(np.asarray(inputs["L1_ffn_rb2"], np.float32)),
        sb2=np.asarray(inputs["L1_ffn_sb2"], np.float32).reshape(1, D),
        ident=IDEN128,
    )
    for e in range(8):
        com[f"rW1_{e}"] = np.ascontiguousarray(np.asarray(inputs["L1_ffn_rW1"], np.float32)[e])
        com[f"rW2_{e}"] = np.ascontiguousarray(np.asarray(inputs["L1_ffn_rW2"], np.float32)[e])
    sel8 = np.zeros((8, 8 * 128), BF)
    for e in range(8):
        sel8[e, e * 128:(e + 1) * 128] = 1.0
    com["sel8d"] = sel8
    return com


def run_F(inputs, resA, resB, resC, resE):
    nc = get_nc("F", build_F)
    res = run_bass_kernel_spmd(nc, stage_F_inputs(inputs, resA, resB, resC, resE),
                               core_ids=list(range(NCORE)))
    return res.results



def build_G1(c=None, ov=None, vload=None):
    if c is None:
        c = Ctx()
    qj = c.io("qj", [4, HD, S], BF16, ov)
    kj = c.io("kj", [4, HD, S], BF16, ov)
    vj = c.io("vj", [4, S, HD], BF16, ov)
    augj = c.io("augj", [4, 1, S], BF16, ov)
    qmj = c.io("qmj", [4, 1, S], BF16, ov)
    maskc = c.io("maskc", [128, 4 * 512], BF16, ov)
    o_a = c.oo("aout", [4, HD, S], BF16, ov)

    ksb = c.sb("ksb", [65, S], BF16)
    qsb = c.sb("qsb", [65, S], BF16)
    qm_sb = c.sb("qm_sb", [1, S], BF16)
    vsb = c.sb("vsb", [128, 64, 65], BF16)
    msb = c.sb("msb", [128, 4 * 512], BF16)
    p_sb0 = c.sb("p_sb0", [128, 3 * 512], BF16)
    p_sb1 = c.sb("p_sb1", [128, 3 * 512], BF16)
    osb = c.sb("osb", [64, S], BF16)
    ev_f32 = c.sb("ev_f32", [64, 512], F32)
    rrow = c.sb("rrow", [1, 512], F32)
    ones64 = c.sb("ones64", [1, 64], F32)
    sc_ps0 = c.ps("sc_ps0", [128, 3 * 512], F32)
    sc_ps1 = c.ps("sc_ps1", [128, 3 * 512], F32)
    outT_ps = c.ps("outT_ps", [65, 512], F32)
    psR = c.ps("psR", [64, 512], F32)

    t_m = c.op("sync", lambda E: E.dma_start(out=msb[:, :], in_=maskc[:, :]), inc=c.ld, dma=True)
    t_o = c.op("vector", lambda E: E.memset(ones64[:, :], 1.0), inc=c.ve)
    t_q1 = c.op("vector", lambda E: E.memset(qsb[64:65, :], 1.0), inc=c.ve)

    for job in range(4):
        g_j = c.gate(c.mm, c.ve, c.ac)
        t_k = c.op("sync", lambda E, job=job: E.dma_start(out=ksb[0:64, :], in_=kj[job]), wait=g_j, inc=c.ld, dma=True)
        t_ka = c.op("sync", lambda E, job=job: E.dma_start(out=ksb[64:65, :], in_=augj[job]), inc=c.ld, dma=True)
        t_q = c.op("sync", lambda E, job=job: E.dma_start(out=qsb[0:64, :], in_=qj[job]), inc=c.ld, dma=True)
        t_qm = c.op("sync", lambda E, job=job: E.dma_start(out=qm_sb[:, :], in_=qmj[job]), inc=c.ld, dma=True)
        if vload is None:
            t_v = c.op("sync", lambda E, job=job: E.dma_start(
                out=vsb[:, :, 0:64], in_=vj[job].rearrange("(t p) d -> p t d", p=128)), inc=c.ld, dma=True)
        else:
            t_v = c.op("gpsimd", lambda E, job=job: E.dma_start(
                out=vsb[:, :, 0:64], in_=vload(job)), wait=g_j, inc=c.gp, dma=True)
        t_1 = c.op("vector", lambda E: E.memset(vsb[:, :, 64:65], 1.0), wait=g_j + later(t_v), inc=c.ve)
        gate0 = later(t_k, t_ka, t_q, t_qm, t_v, t_1, t_m, t_o, t_q1)
        for qt in range(16):
            Q0 = qt * 512
            n_kv = 4 * qt + 4
            masks = {4 * qt + i: msb[:, i * 512:(i + 1) * 512] for i in range(4)}
            emit_flash_job(
                c,
                ksb=lambda t_: ksb[:, t_ * 128:(t_ + 1) * 128],
                qap=qsb[:, Q0:Q0 + 512],
                vtiles=lambda t_: vsb[:, t_, :],
                n_kv=n_kv,
                out_writer=lambda av_tok, qt=qt: emit_div_fm(
                    c, outT_ps, psR, ones64[:, :], rrow, osb[:, qt * 512:(qt + 1) * 512],
                    ev_f32, av_tok, qmask_row=qm_sb[:, qt * 512:(qt + 1) * 512]),
                st_bufs=([sc_ps0, sc_ps1], [p_sb0, p_sb1], outT_ps),
                masks=masks, gate=gate0)
        g_all = c.gate(c.ve)
        c.op("sync", lambda E, job=job: E.dma_start(out=o_a[job], in_=osb[:, :]),
             wait=g_all, inc=c.st, dma=True)
    c.op("sync", lambda E: E.nop(), wait=c.gate(c.st))
    c.op("gpsimd", lambda E: E.nop(), wait=c.gate(c.st))
    return c


def build_G2(c=None, ov=None):
    if c is None:
        c = Ctx()
    x1 = c.io("x1", [TOK, D], F32, ov)
    xn1 = c.io("xn1", [TOK, D], F32, ov)
    g1 = c.io("g1", [TOK, D], F32, ov)
    attnf = c.io("attnf", [D, TOK], BF16, ov)
    gmb = c.io("gmb", [128, D], F32, ov)
    btb = c.io("btb", [128, D], F32, ov)
    identg = c.io("identg", [128, 128], BF16, ov)
    o_out = c.oo("out", [TOK, D], F32, ov)

    afm = c.sb("afm", [128, 8 * 128], BF16)
    ideng = c.sb("ideng", [128, 128], BF16)
    psTg = c.ps("psTg", [128, 128], BF16)
    a_t = c.sb("a_t", [128, 2 * D], F32)
    b_t = c.sb("b_t", [128, 2 * D], F32)
    c_t = c.sb("c_t", [128, 2 * D], F32)
    gm = c.sb("gm", [128, D], F32)
    bt = c.sb("bt", [128, D], F32)
    stat2 = c.sb("stat2", [128, 12], F32)
    mv = c.sb("mv", [128, 2 * NT], F32)
    cols = c.sb("cols", [128, 32], F32)

    t_g = c.op("sync", lambda E: E.dma_start(out=gm[:, :], in_=gmb[:, :]), inc=c.ld, dma=True)
    t_b = c.op("sync", lambda E: E.dma_start(out=bt[:, :], in_=btb[:, :]), inc=c.ld, dma=True)
    t_id = c.op("sync", lambda E: E.dma_start(out=ideng[:, :], in_=identg[:, :]), inc=c.ld, dma=True)
    prev = [None, None]
    for i in range(NT):
        buf = i % 2
        asl = a_t[:, buf * D:(buf + 1) * D]
        bsl = b_t[:, buf * D:(buf + 1) * D]
        csl = c_t[:, buf * D:(buf + 1) * D]
        tl = []
        for kk in range(8):
            tl.append(c.op("sync", lambda E, i=i, kk=kk: E.dma_start(
                out=afm[:, kk * 128:(kk + 1) * 128],
                in_=attnf[kk * 128:(kk + 1) * 128, i * 128:(i + 1) * 128]),
                wait=later(prev[buf], t_g, t_b, t_id) + later(c.war.get("afm")) if kk == 0 else [],
                inc=c.ld, dma=True))
        t_a = None
        for kk in range(8):
            tmm = c.op("tensor", lambda E, kk=kk: E.transpose(
                psTg[:, :], afm[:, kk * 128:(kk + 1) * 128], ideng[:, :]),
                wait=later(*(tl + [t_a, c.war.get("psTg")])), inc=c.mm)
            t_a = c.op("scalar", lambda E, kk=kk, buf=buf: E.copy(
                a_t[:, buf * D + kk * 128:buf * D + (kk + 1) * 128], psTg[:, :]),
                wait=[tmm], inc=c.ac)
            c.war["psTg"] = t_a
        c.war["afm"] = t_a
        t_x = c.op("sync", lambda E, i=i, buf=buf: E.dma_start(
            out=b_t[:, buf * D:(buf + 1) * D], in_=xn1[i * 128:(i + 1) * 128, :]),
            inc=c.ld, dma=True)
        t_o = c.op("vector", lambda E, asl=asl, bsl=bsl: E.tensor_tensor(
            out=asl, in0=asl, in1=bsl, op=ALU.add), wait=later(t_a, t_x), inc=c.ve)
        c.war["afm"] = t_o
        t_b1 = c.op("vector", lambda E, asl=asl: E.bn_stats(stat2[:, 0:6], asl[:, 0:512]), wait=[t_o], inc=c.ve)
        t_b2 = c.op("vector", lambda E, asl=asl: E.bn_stats(stat2[:, 6:12], asl[:, 512:1024]), inc=c.ve)
        t_ag = c.op("vector", lambda E, i=i: E.bn_aggr(mv[:, 2 * i:2 * i + 2], stat2[:, 0:12]),
                    wait=[t_b1, t_b2], inc=c.ve)
        t_ep = c.op("vector", lambda E, i=i: E.tensor_scalar(
            out=cols[:, (i % 2):(i % 2) + 1], in0=mv[:, 2 * i + 1:2 * i + 2],
            scalar1=1e-8, scalar2=None, op0=ALU.add, op1=ALU.bypass), wait=[t_ag], inc=c.ve)
        t_sq = c.op("scalar", lambda E, i=i: E.activation(
            out=cols[:, 8 + i:9 + i], in_=cols[:, (i % 2):(i % 2) + 1], func=AF.Sqrt),
            wait=[t_ep], inc=c.ac)
        t_rc = c.op("vector", lambda E, i=i: E.reciprocal(cols[:, 2 + (i % 2):3 + (i % 2)], cols[:, 8 + i:9 + i]),
                    wait=[t_sq], inc=c.ve)
        t_n = c.op("vector", lambda E, i=i, asl=asl: E.tensor_scalar(
            out=asl, in0=asl, scalar1=mv[:, 2 * i:2 * i + 1],
            scalar2=cols[:, 2 + (i % 2):3 + (i % 2)], op0=ALU.subtract, op1=ALU.mult),
            wait=[t_rc], inc=c.ve)
        t_s1 = c.op("vector", lambda E, asl=asl: E.tensor_tensor(
            out=asl, in0=asl, in1=gm[:, :], op=ALU.mult), wait=[t_n], inc=c.ve)
        t_s2 = c.op("vector", lambda E, asl=asl: E.tensor_tensor(
            out=asl, in0=asl, in1=bt[:, :], op=ALU.add), wait=[t_s1], inc=c.ve)
        t_gl = c.op("sync", lambda E, i=i, buf=buf: E.dma_start(
            out=c_t[:, buf * D:(buf + 1) * D], in_=g1[i * 128:(i + 1) * 128, :]),
            wait=later(prev[buf]), inc=c.ld, dma=True)
        t_m = c.op("vector", lambda E, asl=asl, csl=csl: E.tensor_tensor(
            out=asl, in0=asl, in1=csl, op=ALU.mult), wait=[t_s2] + later(t_gl), inc=c.ve)
        t_xl = c.op("sync", lambda E, i=i, buf=buf: E.dma_start(
            out=b_t[:, buf * D:(buf + 1) * D], in_=x1[i * 128:(i + 1) * 128, :]),
            inc=c.ld, dma=True)
        t_f = c.op("vector", lambda E, asl=asl, bsl=bsl: E.tensor_tensor(
            out=asl, in0=asl, in1=bsl, op=ALU.add), wait=[t_m] + later(t_xl), inc=c.ve)
        t_st = c.op("sync", lambda E, i=i, buf=buf: E.dma_start(
            out=o_out[i * 128:(i + 1) * 128, :], in_=a_t[:, buf * D:(buf + 1) * D]),
            wait=[t_f], inc=c.st, dma=True)
        prev[buf] = t_st
    c.op("sync", lambda E: E.nop(), wait=c.gate(c.st))
    c.op("gpsimd", lambda E: E.nop(), wait=c.gate(c.st))
    return c


def stage_G1_inputs(resF):
    def asm_fm(key):
        out = np.empty((B, resF[0][key].shape[0], S), resF[0][key].dtype)
        for cc in range(NCORE):
            for ci, (b, k) in enumerate(core_chunks(cc)):
                out[b, :, k * CHUNK:(k + 1) * CHUNK] = resF[cc][key][:, ci * CHUNK:(ci + 1) * CHUNK]
        return out
    def asm_tm(key):
        out = np.empty((B, S, resF[0][key].shape[-1]), resF[0][key].dtype)
        for cc in range(NCORE):
            for ci, (b, k) in enumerate(core_chunks(cc)):
                out[b, k * CHUNK:(k + 1) * CHUNK] = resF[cc][key][ci * CHUNK:(ci + 1) * CHUNK]
        return out
    qf = asm_fm("q1"); kf = asm_fm("k1"); vf = asm_tm("v1")
    augf = asm_fm("k1aug")
    qmf = asm_fm("qmask")
    mkc = np.ascontiguousarray(MASK_C.transpose(1, 0, 2).reshape(128, 4 * 512))
    in_maps = []
    for cc in range(NCORE):
        jobs = core_bh_jobs(cc)
        in_maps.append(dict(
            qj=np.ascontiguousarray(np.stack([qf[b, h * HD:(h + 1) * HD, :] for (b, h) in jobs])),
            kj=np.ascontiguousarray(np.stack([kf[b, h * HD:(h + 1) * HD, :] for (b, h) in jobs])),
            vj=np.ascontiguousarray(np.stack([vf[b, :, h * HD:(h + 1) * HD] for (b, h) in jobs])),
            augj=np.ascontiguousarray(np.stack([augf[b] for (b, h) in jobs])),
            qmj=np.ascontiguousarray(np.stack([qmf[b] for (b, h) in jobs])),
            maskc=mkc))
    return in_maps


def run_G1(resF):
    nc = get_nc("G1", build_G1)
    return run_bass_kernel_spmd(nc, stage_G1_inputs(resF), core_ids=list(range(NCORE))).results



def build_BCE(c=None):
    c = Ctx()
    ovB = {n: c.din("B" + n, sh, dt) for n, sh, dt in [
        ("qj", [4, HD, S], BF16), ("kj", [4, HD, S], BF16),
        ("vj", [4, S, HD], BF16), ("maskc", [128, 4 * 512], BF16)]}
    ovB["aout"] = c.dout("Baout", [4, HD, S], BF16)
    ovC = {n: c.din("C" + n, sh, dt) for n, sh, dt in [
        ("qj", [4, HD, S], BF16), ("kj", [4, HD, S], BF16),
        ("vj", [4, S, HD], BF16), ("maskc", [128, 4 * 512], BF16),
        ("maskp", [128, 4 * 512], BF16)]}
    ovC["aout"] = c.dout("Caout", [4, HD, S], BF16)
    c.push_scope()
    build_B(c, ovB)
    c.fence()
    c.pop_scope()
    c.push_scope()
    build_C(c, ovC)
    c.fence()
    c.pop_scope()
    c.push_scope()
    build_E(c, None)
    c.pop_scope()
    return c


def run_BCE(resA, inputs):
    nc = get_nc("BCE", build_BCE)
    mB = stage_B_inputs(resA)
    mC = stage_C_inputs(resA)
    mE = stage_E_inputs(resA, inputs)
    in_maps = []
    for cc in range(NCORE):
        m = {}
        m.update({"B" + k: v for k, v in mB[cc].items()})
        m.update({"C" + k: v for k, v in mC[cc].items()})
        m.update(mE[cc])
        in_maps.append(m)
    return run_bass_kernel_spmd(nc, in_maps, core_ids=list(range(NCORE))).results



def build_FG(c=None, ovF_extra=None):
    if c is None:
        c = Ctx()
    q1s = c.dtmp("q1s", [H * HD, TOK], BF16)
    k1s = c.dtmp("k1s", [H * HD, TOK], BF16)
    v1s = c.dtmp("v1s", [TOK, D], BF16)
    aqs = c.dtmp("aqs", [2, TOK], BF16)
    x1s = c.dtmp("x1s", [TOK, D], F32)
    xn1s = c.dtmp("xn1s", [TOK, D], F32)
    g1s = c.dtmp("g1s", [TOK, D], F32)
    q1ag = c.dtmp("q1ag", [8, H * HD, TOK], BF16)
    k1ag = c.dtmp("k1ag", [8, H * HD, TOK], BF16)
    v1ag = c.dtmp("v1ag", [8, TOK, D], BF16)
    aqag = c.dtmp("aqag", [8, 2, TOK], BF16)
    qj_i = c.dtmp("qj_i", [4, HD, S], BF16)
    kj_i = c.dtmp("kj_i", [4, HD, S], BF16)
    vj_i = c.dtmp("vj_i", [4, S, HD], BF16)
    aj_i = c.dtmp("aj_i", [4, 1, S], BF16)
    qmj_i = c.dtmp("qmj_i", [4, 1, S], BF16)
    packg = c.dtmp("packg", [4, HD, S], BF16)
    pgag = c.dtmp("pgag", [8, 4, HD, S], BF16)
    attnf_i = c.dtmp("attnf_i", [D, TOK], BF16)
    offs = c.din("offs", [1, 4], I32)
    offs_sb = c.sb("offs_sb", [1, 4], I32)
    t_off = c.op("sync", lambda E: E.dma_start(out=offs_sb[:, :], in_=offs[:, :]),
                 inc=c.ld, dma=True)
    s0, _ = c.dynload(lambda: offs_sb[0:1, 0:1], wait=[t_off], eng="sync")
    s1, _ = c.dynload(lambda: offs_sb[0:1, 1:2], eng="sync")
    h0, _ = c.dynload(lambda: offs_sb[0:1, 0:1], wait=[t_off])
    h1, _ = c.dynload(lambda: offs_sb[0:1, 1:2])
    h2, _ = c.dynload(lambda: offs_sb[0:1, 2:3])

    ovF = dict(x1=x1s, xn1=xn1s, g1=g1s, q1=q1s, k1=k1s, v1=v1s,
               k1aug=aqs[0:1, :], qmask=aqs[1:2, :])
    if ovF_extra:
        ovF.update(ovF_extra)
    c.push_scope()
    build_F(c, ovF)
    c.fence()
    c.pop_scope()

    c.allgather(q1s[:, :], q1ag[:, :, :])
    c.allgather(k1s[:, :], k1ag[:, :, :])
    c.allgather(v1s[:, :], v1ag[:, :, :])
    tcc = c.allgather(aqs[:, :], aqag[:, :, :])
    for b in range(2):
        c.op("sync", lambda E, b=b: E.dma_start(
            out=qj_i[2 * b:2 * b + 2].rearrange("j p (k f) -> j p k f", k=8),
            in_=q1ag[:, ds(s0["v"], 2 * HD), b * CHUNK:(b + 1) * CHUNK].rearrange("k (j p) f -> j p k f", j=2)),
            wait=[tcc], inc=c.ld, dma=True)
        c.op("sync", lambda E, b=b: E.dma_start(
            out=kj_i[2 * b:2 * b + 2].rearrange("j p (k f) -> j p k f", k=8),
            in_=k1ag[:, ds(s0["v"], 2 * HD), b * CHUNK:(b + 1) * CHUNK].rearrange("k (j p) f -> j p k f", j=2)),
            inc=c.ld, dma=True)
        for j in (2 * b, 2 * b + 1):
            if b == 0:
                sv = s0 if j % 2 == 0 else s1
                for k in range(8):
                    c.op("sync", lambda E, j=j, b=b, k=k, sv=sv: E.dma_start(
                        out=vj_i[j, k * CHUNK:(k + 1) * CHUNK, :],
                        in_=v1ag[k, b * CHUNK:(b + 1) * CHUNK, ds(sv["v"], HD)]),
                        wait=[tcc] if k == 0 else [], inc=c.ld, dma=True)
            else:
                hv = h0 if j % 2 == 0 else h1
                for k in range(8):
                    c.op("gpsimd", lambda E, j=j, b=b, k=k, hv=hv: E.dma_start(
                        out=vj_i[j, k * CHUNK:(k + 1) * CHUNK, :],
                        in_=v1ag[k, b * CHUNK:(b + 1) * CHUNK, ds(hv["v"], HD)]),
                        wait=[tcc], inc=c.gp, dma=True)
            c.op("gpsimd", lambda E, j=j, b=b: E.dma_start(
                out=aj_i[j].rearrange("o (k f) -> o k f", k=8),
                in_=aqag[:, 0:1, b * CHUNK:(b + 1) * CHUNK].rearrange("k o f -> o k f")),
                wait=[tcc], inc=c.gp, dma=True)
            c.op("gpsimd", lambda E, j=j, b=b: E.dma_start(
                out=qmj_i[j].rearrange("o (k f) -> o k f", k=8),
                in_=aqag[:, 1:2, b * CHUNK:(b + 1) * CHUNK].rearrange("k o f -> o k f")),
                wait=[tcc], inc=c.gp, dma=True)
    c.fence()

    ovG1 = dict(qj=qj_i, kj=kj_i, vj=vj_i, augj=aj_i, qmj=qmj_i, aout=packg)
    c.push_scope()
    build_G1(c, ovG1)
    c.fence()
    c.pop_scope()

    tg = c.allgather(packg[:, :, :], pgag[:, :, :, :])
    for b in range(B):
        c.op("gpsimd", lambda E, b=b: E.dma_start(
            out=attnf_i[:, b * CHUNK:(b + 1) * CHUNK].rearrange("(a e p) f -> a e p f", a=8, e=2),
            in_=pgag[:, 2 * b:2 * b + 2, :, ds(h2["v"], CHUNK)]),
            wait=[tg], inc=c.gp, dma=True)
    c.fence()

    ovG2 = dict(x1=x1s, xn1=xn1s, g1=g1s, attnf=attnf_i)
    c.push_scope()
    build_G2(c, ovG2)
    c.pop_scope()
    return c



def task_slot(b, w, g):
    i = (NWIN - 1 - w) * 4 + b * 2 + g
    return i // 8, i % 8


def build_M2(c=None):
    c = Ctx()
    ovB = {n: c.din("B" + n, sh, dt) for n, sh, dt in [
        ("qj", [4, HD, S], BF16), ("kj", [4, HD, S], BF16),
        ("vj", [4, S, HD], BF16), ("maskc", [128, 4 * 512], BF16)]}
    ovB["aout"] = c.dtmp("b_aout", [4, HD, S], BF16)
    ovC = {n: c.din("C" + n, sh, dt) for n, sh, dt in [
        ("qj", [4, HD, S], BF16), ("kj", [4, HD, S], BF16),
        ("vj", [4, S, HD], BF16), ("maskc", [128, 4 * 512], BF16),
        ("maskp", [128, 4 * 512], BF16)]}
    ovC["aout"] = c.dtmp("c_aout", [4, HD, S], BF16)
    ovE = dict(aout=c.dtmp("e_aout", [8, 128, 8 * 512], BF16),
               pq=c.dtmp("e_pq", [8, 128, 4], F32),
               dbg_part=c.dtmp("e_dbg", [128, SC_KV], F32))
    bag = c.dtmp("bag", [8, 4, HD, S], BF16)
    cag = c.dtmp("cag", [8, 4, HD, S], BF16)
    eag = c.dtmp("eag", [8 * 8 * 128 + 128, 8 * 512], BF16)
    pqag = c.dtmp("pqag", [8 * 8 * 128 + 128, 4], F32)
    attn0_i = c.dtmp("attn0_i", [H * HD, TOK], BF16)
    attnl_i = c.dtmp("attnl_i", [H * HD, TOK], BF16)
    attng0_i = c.dtmp("attng0_i", [H * HD, TOK], BF16)
    attng1_i = c.dtmp("attng1_i", [H * HD, TOK], BF16)
    pq_i = c.dtmp("f_pq", [128, NT * 2], F32)
    offs2 = c.din("offs2", [1, 12], I32)
    offs2_sb = c.sb("offs2_sb", [1, 12], I32)
    zb = c.sb("zb", [128, 8 * 512], BF16)
    zf = c.sb("zf", [128, 4], F32)

    t_o2 = c.op("sync", lambda E: E.dma_start(out=offs2_sb[:, :], in_=offs2[:, :]),
                inc=c.ld, dma=True)
    a2, _ = c.dynload(lambda: offs2_sb[0:1, 8:9], wait=[t_o2], eng="scalar")
    tvs = [c.dynload(lambda i=i: offs2_sb[0:1, i:i + 1], eng="scalar")[0] for i in range(8)]
    t_z1 = c.op("vector", lambda E: E.memset(zb[:, :], 0.0), inc=c.ve)
    t_z2 = c.op("vector", lambda E: E.memset(zf[:, :], 0.0), inc=c.ve)
    t_zd1 = c.op("sync", lambda E: E.dma_start(out=eag[8192:8320, :], in_=zb[:, :]),
                 wait=[t_z1], inc=c.ld, dma=True)
    t_zd2 = c.op("sync", lambda E: E.dma_start(out=pqag[8192:8320, :], in_=zf[:, :]),
                 wait=[t_z2], inc=c.ld, dma=True)

    c.push_scope(); build_B(c, ovB); c.fence(); c.pop_scope()
    c.push_scope(); build_C(c, ovC); c.fence(); c.pop_scope()
    c.push_scope(); build_E(c, ovE); c.fence(); c.pop_scope()

    c.allgather(ovB["aout"][:, :, :], bag[:, :, :, :])
    c.allgather(ovC["aout"][:, :, :], cag[:, :, :, :])
    c.allgather(ovE["aout"][:, :, :], eag[0:8192, :])
    tcc2 = c.allgather(ovE["pq"][:, :, :], pqag[0:8192, :])
    wz = [tcc2, t_zd1, t_zd2]
    for b in range(B):
        c.op("scalar", lambda E, b=b: E.dma_start(
            out=attn0_i[:, b * CHUNK:(b + 1) * CHUNK].rearrange("(a e p) f -> a e p f", a=8, e=2),
            in_=bag[:, 2 * b:2 * b + 2, :, ds(a2["v"], CHUNK)]), wait=wz, inc=c.ac, dma=True)
        c.op("scalar", lambda E, b=b: E.dma_start(
            out=attnl_i[:, b * CHUNK:(b + 1) * CHUNK].rearrange("(a e p) f -> a e p f", a=8, e=2),
            in_=cag[:, 2 * b:2 * b + 2, :, ds(a2["v"], CHUNK)]), wait=wz, inc=c.ac, dma=True)
    for g in range(G):
        dstg = attng0_i if g == 0 else attng1_i
        for b in range(B):
            for wl in range(2):
                tv = tvs[g * 4 + b * 2 + wl]
                c.op("scalar", lambda E, dstg=dstg, b=b, wl=wl, tv=tv: E.dma_start(
                    out=dstg[:, b * CHUNK + wl * 512:b * CHUNK + (wl + 1) * 512
                             ].rearrange("(ch p) f -> ch p f", ch=8),
                    in_=eag[ds(tv["v"], 128), :].rearrange("p (ch f) -> ch p f", ch=8)),
                    wait=wz, inc=c.ac, dma=True)
                c.op("scalar", lambda E, g=g, b=b, wl=wl, tv=tv: E.dma_start(
                    out=pq_i[:, g * NT + b * 8 + wl * 4:g * NT + b * 8 + wl * 4 + 4],
                    in_=pqag[ds(tv["v"], 128), :]),
                    wait=wz, inc=c.ac, dma=True)
    c.fence()

    build_FG(c, ovF_extra=dict(attn0=attn0_i, attnl=attnl_i, attng0=attng0_i,
                               attng1=attng1_i, pq=pq_i))
    return c


def stage_M2_inputs(inputs, resA):
    mB = stage_B_inputs(resA)
    mC = stage_C_inputs(resA)
    mE = stage_E_inputs(resA, inputs)
    com = stage_F_com(inputs)
    x = np.asarray(inputs["L0_attn_x"], np.float32)
    mkc = np.ascontiguousarray(MASK_C.transpose(1, 0, 2).reshape(128, 4 * 512))
    gmb = np.ascontiguousarray(np.broadcast_to(np.asarray(inputs["L1_attn_gamma"], np.float32)[None, :], (128, D)))
    btb = np.ascontiguousarray(np.broadcast_to(np.asarray(inputs["L1_attn_beta"], np.float32)[None, :], (128, D)))
    in_maps = []
    for cc in range(NCORE):
        m = {}
        m.update({"B" + k: v for k, v in mB[cc].items()})
        m.update({"C" + k: v for k, v in mC[cc].items()})
        m.update(mE[cc])
        m.update(com)
        xs = [x[b, k * CHUNK:(k + 1) * CHUNK, :] for (b, k) in core_chunks(cc)]
        m["x"] = np.ascontiguousarray(np.concatenate(xs, 0))
        m["maskc"] = mkc
        m["gmb"] = gmb
        m["btb"] = btb
        m["identg"] = IDEN128
        m["offs"] = np.array([[128 * cc, 128 * cc + 64, 1024 * cc, 0]], np.int32)
        o2 = np.zeros((1, 12), np.int32)
        for g in range(G):
            for b in range(B):
                for wl in range(2):
                    w = 2 * cc + wl
                    if w == 0:
                        v = 8192
                    else:
                        r, ce = task_slot(b, w, g)
                        v = ce * 1024 + r * 128
                    o2[0, g * 4 + b * 2 + wl] = v
        o2[0, 8] = 1024 * cc
        m["offs2"] = o2
        in_maps.append(m)
    return in_maps


def run_M2(inputs, resA):
    nc = get_nc("M2", build_M2)
    return run_bass_kernel_spmd(nc, stage_M2_inputs(inputs, resA),
                                core_ids=list(range(NCORE))).results


def run_FG(inputs, resA, resB, resC, resE):
    nc = get_nc("FG", build_FG)
    mF = stage_F_inputs(inputs, resA, resB, resC, resE)
    mkc = np.ascontiguousarray(MASK_C.transpose(1, 0, 2).reshape(128, 4 * 512))
    gmb = np.ascontiguousarray(np.broadcast_to(np.asarray(inputs["L1_attn_gamma"], np.float32)[None, :], (128, D)))
    btb = np.ascontiguousarray(np.broadcast_to(np.asarray(inputs["L1_attn_beta"], np.float32)[None, :], (128, D)))
    in_maps = []
    for cc in range(NCORE):
        m = dict(mF[cc])
        m["maskc"] = mkc
        m["gmb"] = gmb
        m["btb"] = btb
        m["identg"] = IDEN128
        m["offs"] = np.array([[128 * cc, 128 * cc + 64, 1024 * cc, 0]], np.int32)
        in_maps.append(m)
    return run_bass_kernel_spmd(nc, in_maps, core_ids=list(range(NCORE))).results


def run_G2(inputs, resF, resG1):
    attnf = np.empty((B, H * HD, S), BF)
    for cc in range(NCORE):
        for j, (b, h) in enumerate(core_bh_jobs(cc)):
            attnf[b, h * HD:(h + 1) * HD, :] = resG1[cc]["aout"][j]
    gmb = np.ascontiguousarray(np.broadcast_to(np.asarray(inputs["L1_attn_gamma"], np.float32)[None, :], (128, D)))
    btb = np.ascontiguousarray(np.broadcast_to(np.asarray(inputs["L1_attn_beta"], np.float32)[None, :], (128, D)))
    in_maps = []
    for cc in range(NCORE):
        at = []
        for (b, k) in core_chunks(cc):
            at.append(attnf[b, :, k * CHUNK:(k + 1) * CHUNK])
        in_maps.append(dict(
            x1=resF[cc]["x1"], xn1=resF[cc]["xn1"], g1=resF[cc]["g1"],
            attnf=np.ascontiguousarray(np.concatenate(at, 1)), gmb=gmb, btb=btb,
            identg=IDEN128))
    nc = get_nc("G2", build_G2)
    return run_bass_kernel_spmd(nc, in_maps, core_ids=list(range(NCORE))).results
